# version 15
# baseline (speedup 1.0000x reference)
import numpy as np
import ml_dtypes
from contextlib import ExitStack

import concourse.bass as bass
import concourse.mybir as mybir
from concourse.bass_utils import run_bass_kernel_spmd

F32 = mybir.dt.float32
BF16 = mybir.dt.bfloat16
AF = mybir.ActivationFunctionType
ALU = mybir.AluOpType
AX = mybir.AxisListType

B = 8
SEQ = 4096
D = 2048
FC = 16
H = 16
HD = 128
CH = 128
TILE = 512
NT = SEQ // TILE
NCH = TILE // CH
DQ = 3 * D
MQK = 32
MP = 16
MF = 16
NEG = 4
RND_C = float(3 * 2**22)
EPS32 = float(np.finfo(np.float32).eps)

ENG_NAMES = {'t': 'tensor', 'v': 'vector', 's': 'scalar', 'g': 'gpsimd', 'y': 'sync'}
PARANOID_PE = False
STRICT_MAIN = False
DEBUG_O = False


class Buf:
    __slots__ = ('last_w', 'readers', 'name')

    def __init__(self, name=''):
        self.last_w = None
        self.readers = []
        self.name = name


DMA_SEMS = (('xA', 'xB', 'wpA', 'wpB', 'st', 'qs', 'cl', 'b1', 'b2', 'vs0',
             'qa0', 'qa1', 'qb0', 'qb1')
            + tuple(f'qi{i}' for i in range(40)))


class Prog:
    def __init__(self, nc):
        self.nc = nc
        self.recs = {k: [] for k in ENG_NAMES}
        self.dma_cnt = {k: 0 for k in DMA_SEMS}
        self.pe_strict = True

    def emit(self, ek, fn, reads=(), writes=(), also=(), tag='', tr=False):
        deps = []
        for b in reads:
            if b.last_w is not None:
                deps.append(b.last_w)
        for b in writes:
            if b.last_w is not None:
                deps.append(b.last_w)
            deps.extend(b.readers)
        deps.extend(also)
        ndeps = []
        seen = set()
        for d in deps:
            if d is None:
                continue
            kind, e2, i2 = d
            if (kind == 'c' and e2 == ek == 't' and not PARANOID_PE
                    and not self.pe_strict
                    and not tr and not self.recs['t'][i2].get('tr')):
                continue
            if kind == 'd':
                d = (kind, e2, self.dma_cnt[e2])
            if d in seen:
                continue
            seen.add(d)
            if kind == 'c':
                self.recs[e2][i2]['inc'] = True
            ndeps.append(d)
        idx = len(self.recs[ek])
        self.recs[ek].append(dict(fn=fn, waits=ndeps, inc=False, dma=None, tag=tag,
                                  tr=tr))
        tok = ('c', ek, idx)
        for b in reads:
            b.readers.append(tok)
        for b in writes:
            b.last_w = tok
            b.readers = []
        return tok

    def dma(self, qk, sem, out_ap, in_ap, reads=(), writes=(), also=(), tag='', **kw):
        deps = []
        for b in reads:
            if b.last_w is not None:
                deps.append(b.last_w)
        for b in writes:
            if b.last_w is not None:
                deps.append(b.last_w)
            deps.extend(b.readers)
        deps.extend(also)
        ndeps = []
        seen = set()
        for d in deps:
            if d is None:
                continue
            kind, e2, i2 = d
            if kind == 'd':
                d = (kind, e2, self.dma_cnt[e2])
            if d in seen:
                continue
            seen.add(d)
            if kind == 'c':
                self.recs[e2][i2]['inc'] = True
            ndeps.append(d)
        self.dma_cnt[sem] += 1
        n = self.dma_cnt[sem]
        fn = lambda eng, o=out_ap, i=in_ap, kw=kw: eng.dma_start(out=o, in_=i, **kw)
        self.recs[qk].append(dict(fn=fn, waits=ndeps, inc=False, dma=sem, tag=tag))
        tok = ('d', sem, n)
        for b in reads:
            b.readers.append(tok)
        for b in writes:
            b.last_w = tok
            b.readers = []
        return tok

    def finalize(self, block, sems, dsems):
        pref = {}
        for ek, rs in self.recs.items():
            c = 0
            p = []
            for r in rs:
                if r['inc']:
                    c += 1
                p.append(c)
            pref[ek] = p

        def mk_body(ek, rs):
            def body(eng):
                watermark = {}
                for r in rs:
                    for (kind, e2, i2) in r['waits']:
                        if kind == 'c':
                            sem = sems[e2]
                            val = pref[e2][i2]
                            if val == 0:
                                continue
                        else:
                            sem = dsems[e2]
                            val = 16 * i2
                        key = (kind, e2)
                        if watermark.get(key, -1) >= val:
                            continue
                        eng.wait_ge(sem, val)
                        watermark[key] = val
                    inst = r['fn'](eng)
                    if r['dma'] is not None:
                        inst.then_inc(dsems[r['dma']], 16)
                    elif r['inc']:
                        inst.then_inc(sems[ek], 1)
            return body

        for ek in ENG_NAMES:
            getattr(block, ENG_NAMES[ek])(mk_body(ek, self.recs[ek]))


def build_nc():
    nc = bass.Bass("TRN2", target_bir_lowering=False)

    xT = nc.dram_tensor("xT", [D, SEQ], F32, kind="ExternalInput").ap()
    wq_r = nc.dram_tensor("wq_r", [48, 128, 2048], F32, kind="ExternalInput").ap()
    wstat = nc.dram_tensor("wstat", [10, 128, 2048], F32, kind="ExternalInput").ap()
    selT = nc.dram_tensor("selT", [10, 4], F32, kind="ExternalInput").ap()
    selmax = nc.dram_tensor("selmax", [1, 10], F32, kind="ExternalInput").ap()
    wp_r = nc.dram_tensor("wp_r", [16, 128, 2048], F32, kind="ExternalInput").ap()
    wf_r = nc.dram_tensor("wf_r", [16, 128, 2048], F32, kind="ExternalInput").ap()
    vin = {}
    for nm in ("qn", "pn", "nw", "ga", "be", "lw", "lb"):
        vin[nm] = nc.dram_tensor(nm + "_r", [128, FC], F32, kind="ExternalInput").ap()
    s_in = nc.dram_tensor("s_in", [1, H], F32, kind="ExternalInput").ap()
    iota_c = nc.dram_tensor("iota_c", [128, 1], F32, kind="ExternalInput").ap()
    iota_r = nc.dram_tensor("iota_r", [1, 128], F32, kind="ExternalInput").ap()
    diffp = nc.dram_tensor("diffp", [128, 128], F32, kind="ExternalInput").ap()
    causal = nc.dram_tensor("causal", [128, 128], F32, kind="ExternalInput").ap()
    ones128 = nc.dram_tensor("ones128", [128, 128], F32, kind="ExternalInput").ap()
    iden_in = nc.dram_tensor("iden_in", [128, 128], BF16, kind="ExternalInput").ap()
    onescol_bf = nc.dram_tensor("onescol_bf", [128, 1], BF16, kind="ExternalInput").ap()
    onescol_f = nc.dram_tensor("onescol_f", [128, 1], F32, kind="ExternalInput").ap()
    onesrow_f = nc.dram_tensor("onesrow_f", [1, 128], F32, kind="ExternalInput").ap()
    cvals_in = nc.dram_tensor("cvals_in", [128, 4], F32, kind="ExternalInput").ap()
    outT = nc.dram_tensor("outT", [D, SEQ], F32, kind="ExternalOutput").ap()
    odbg = (nc.dram_tensor("odbg", [D, SEQ], BF16, kind="ExternalOutput").ap()
            if DEBUG_O else None)
    qdbg = (nc.dram_tensor("qdbg", [2, 4, 128, FC * TILE], BF16,
                           kind="ExternalOutput").ap() if DEBUG_O else None)

    wq_t = nc.dram_tensor("wq_t", [MQK, 128, FC * 128], BF16, kind="Internal").ap()
    wv_t = nc.dram_tensor("wv_t", [8, 2, 128, 2048], BF16, kind="Internal").ap()
    wp_t = nc.dram_tensor("wp_t", [MP, 128, FC * 128], BF16, kind="Internal").ap()
    wf_t = nc.dram_tensor("wf_t", [MF, 128, FC * 128], BF16, kind="Internal").ap()
    bounce = nc.dram_tensor("bounce", [128], F32, kind="Internal").ap()
    ar_in = nc.dram_tensor("ar_in", [1, 4], F32, kind="Internal").ap()
    ar_out = nc.dram_tensor("ar_out", [1, 4], F32, kind="Internal").ap()
    arm_in = nc.dram_tensor("arm_in", [1, 1], F32, kind="Internal").ap()
    arm_out = nc.dram_tensor("arm_out", [1, 1], F32, kind="Internal").ap()

    A = nc.alloc_sbuf_tensor
    xA = A("xA", [128, FC * TILE], F32).ap()
    xB = A("xB", [128, FC * TILE], F32).ap()
    h = A("h", [128, FC * TILE], BF16).ap()
    qb = A("qb", [128, H * TILE], BF16).ap()
    kb = A("kb", [128, H * TILE], BF16).ap()
    vb = A("vb", [128, NCH * D], BF16).ap()
    pb = A("pb", [128, FC * TILE], BF16).ap()
    Sf = A("Sf", [128, H * HD], F32).ap()
    Sb = A("Sb", [128, H * HD], BF16).ap()
    wpool = A("wpool", [128, 2 * FC * 128], BF16).ap()
    vsl = A("vsl", [128, 8 * 256], BF16).ap()
    maskT = A("maskT", [128, H * 128], BF16).ap()
    ltrep = A("ltrep", [128, H * 128], BF16).ap()
    lrT = A("lrT", [128, H * 128], BF16).ap()
    o128 = A("o128", [128, 128], F32).ap()
    iden = A("iden", [128, 128], BF16).ap()
    dfp = A("dfp", [128, 128], F32).ap()
    cau = A("cau", [128, 128], F32).ap()
    ioc = A("ioc", [128, 1], F32).ap()
    ior = A("ior", [1, 128], F32).ap()
    ocb = A("ocb", [128, 1], BF16).ap()
    ocf = A("ocf", [128, 1], F32).ap()
    orf = A("orf", [1, 128], F32).ap()
    cst = A("cst", [128, 4], F32).ap()
    cols = {nm: A("c_" + nm, [128, FC], F32).ap()
            for nm in ("qn", "pn", "nw", "ga", "be", "lw", "lb",
                       "qnm", "g", "nw2", "gs", "b2")}
    srow = A("srow", [1, H], F32).ap()
    nw2b = A("nw2b", [128, FC], BF16).ap()
    lbbf = A("lbbf", [128, FC], BF16).ap()
    sbc = A("sbc", [128, H], F32).ap()
    snb = A("snb", [128, H], F32).ap()
    sm128 = A("sm128", [128, H], F32).ap()
    lamT_c = A("lamT_c", [128, H], F32).ap()
    lrev_c = A("lrev_c", [128, H], F32).ap()
    mcols = A("mcols", [128, 4], F32).ap()
    sfcol = A("sfcol", [128, 1], F32).ap()
    stat = A("stat", [128, 3 * FC], F32).ap()
    selT_s = A("selT_s", [10, 4], F32).ap()
    etot = A("etot", [10, 1], F32).ap()
    statm = A("statm", [128, FC], F32).ap()
    rows = A("rows", [1, 3 * TILE], F32).ap()
    tot = A("tot", [1, 8], F32).ap()
    bfp = A("bfp", [128, 4 * TILE], BF16).ap()
    t1s = A("t1s", [128, 2 * TILE], BF16).ap()
    qts = A("qts", [128, 2 * TILE], BF16).ap()
    kts = A("kts", [128, 2 * TILE], BF16).ap()
    scm = A("scm", [128, 2 * TILE], BF16).ap()
    brow = A("brow", [128, 3 * TILE], BF16).ap()
    rbf = A("rbf", [1, TILE], BF16).ap()

    P = nc.alloc_psum_tensor
    pA = P("pA", [128, 512], F32).ap()
    pB = P("pB", [128, 512], F32).ap()
    pC = P("pC", [128, 512], F32).ap()
    pD = P("pD", [128, 1024], BF16).ap()
    pE = P("pE", [128, 512], F32).ap()
    pF = P("pF", [128, 512], F32).ap()
    pS1 = P("pS1", [1, 512], F32).ap()
    pS2 = P("pS2", [1, 512], F32).ap()

    es = ExitStack()
    sems = {k: es.enter_context(nc.semaphore("sem_" + k)) for k in ENG_NAMES}
    dsems = {k: es.enter_context(nc.semaphore("dma_" + k)) for k in DMA_SEMS}

    pr = Prog(nc)

    bxA = [Buf(f"xA{i}") for i in range(FC)]
    bxB = [Buf(f"xB{i}") for i in range(FC)]
    bh = [Buf(f"h{i}") for i in range(FC)]
    bqb = [Buf(f"qb{i}") for i in range(H)]
    bkb = [Buf(f"kb{i}") for i in range(H)]
    bvb = [Buf(f"vb{i}") for i in range(NCH * 4)]
    bpb = [Buf(f"pb{i}") for i in range(FC)]
    bSf = [Buf(f"Sf{i}") for i in range(H)]
    bSb = [Buf(f"Sb{i}") for i in range(H)]
    bwp = [Buf("wp0"), Buf("wp1")]
    bvsl = [Buf(f"vsl{i}") for i in range(8)]
    bconsts = Buf("consts")
    bcols = Buf("cols")
    brows = [Buf(f"row{i}") for i in range(4)]
    btot = Buf("tot")
    bbfp = [Buf(f"bfp{i}") for i in range(4)]
    bt1 = [Buf("t1a"), Buf("t1b")]
    bqt = [Buf("qta"), Buf("qtb")]
    bkt = [Buf("kta"), Buf("ktb")]
    bscm = [Buf("scma"), Buf("scmb")]
    bbrow = [Buf("brw0"), Buf("brw1"), Buf("brw2")]
    brbf = Buf("rbf")
    bstat = Buf("stat")
    bpA = Buf("pA"); bpB = Buf("pB"); bpC = Buf("pC"); bpD = [Buf("pD0"), Buf("pD1")]
    bpE = Buf("pE"); bpF = Buf("pF"); bpS1 = Buf("pS1"); bpS2 = Buf("pS2")
    bwq_t = [Buf(f"wqt{i}") for i in range(MQK)]
    bwv_t = [Buf(f"wvt{i}") for i in range(16)]
    bwp_t = [Buf(f"wpt{i}") for i in range(MP)]
    bwf_t = [Buf(f"wft{i}") for i in range(MF)]
    bbounce = Buf("bounce")
    bar1 = Buf("ar1"); bar2 = Buf("ar2"); bar3 = Buf("ar3"); bar4 = Buf("ar4")
    bout = Buf("outT")

    tload = []
    for (dst, src) in ((dfp, diffp), (cau, causal), (o128, ones128), (iden, iden_in),
                       (ioc, iota_c), (ior, iota_r), (ocb, onescol_bf), (ocf, onescol_f),
                       (orf, onesrow_f), (cst, cvals_in), (srow[:1, :], s_in)):
        tload.append(pr.dma('g', 'cl', dst, src, writes=[bconsts], tag='cload'))
    for nm in ("qn", "pn", "nw", "ga", "be", "lw", "lb"):
        tload.append(pr.dma('g', 'cl', cols[nm], vin[nm], writes=[bcols], tag='vload'))

    tok = pr.emit('t', lambda e: e.matmul(pA[:, 0:H], orf, srow[:1, :], start=True, stop=True),
                  reads=[bconsts], writes=[bpA], tag='sbcast')
    pr.emit('v', lambda e: e.tensor_copy(out=sbc, in_=pA[:, 0:H]), reads=[bpA], writes=[bconsts])
    pr.emit('v', lambda e: e.tensor_scalar_mul(out=snb, in0=sbc, scalar1=-1.0),
            reads=[bconsts], writes=[bconsts])
    pr.emit('v', lambda e: e.tensor_scalar_mul(out=sm128, in0=sbc, scalar1=-128.0),
            reads=[bconsts], writes=[bconsts])
    pr.emit('s', lambda e: e.activation(out=lamT_c, in_=sm128, func=AF.Exp),
            reads=[bconsts], writes=[bconsts])
    for hh in range(H):
        pr.emit('s', lambda e, hh=hh: e.activation(
            out=lrev_c[:, hh:hh + 1], in_=ioc, func=AF.Exp,
            scale=sbc[:, hh:hh + 1], bias=sm128[:, hh:hh + 1]),
            reads=[bconsts], writes=[bconsts])
    for hh in range(H):
        pr.emit('v', lambda e, hh=hh: e.tensor_scalar_mul(
            out=lrT[:, hh * 128:(hh + 1) * 128], in0=o128, scalar1=lrev_c[:, hh:hh + 1]),
            reads=[bconsts], writes=[bconsts])
    for hh in range(H):
        pr.emit('s', lambda e, hh=hh: e.activation(
            out=rows[:1, (hh % 3) * 512:(hh % 3) * 512 + 128],
            in_=ior, func=AF.Exp, scale=snb[0:1, hh:hh + 1]),
            reads=[bconsts], writes=[brows[hh % 3]], tag='ltrow')
        pr.emit('t', lambda e, hh=hh: e.matmul(
            pB[:, 0:128], orf, rows[:1, (hh % 3) * 512:(hh % 3) * 512 + 128], start=True, stop=True),
            reads=[brows[hh % 3]], writes=[bpB])
        pr.emit('v', lambda e, hh=hh: e.tensor_copy(
            out=ltrep[:, hh * 128:(hh + 1) * 128], in_=pB[:, 0:128]),
            reads=[bpB], writes=[bconsts])
    for hh in range(H):
        pr.emit('s', lambda e, hh=hh: e.activation(
            out=pE[:, 0:128],
            in_=dfp, func=AF.Exp, scale=snb[:, hh:hh + 1]),
            reads=[bconsts], writes=[bpE], tag='maskexp')
        pr.emit('v', lambda e, hh=hh: e.tensor_tensor(
            out=maskT[:, hh * 128:(hh + 1) * 128], in0=pE[:, 0:128], in1=cau, op=ALU.mult),
            reads=[bpE, bconsts], writes=[bconsts])

    for i in range(5):
        hl = i % 2
        sl = xB[:, hl * 4096:(hl + 1) * 4096]
        buf = bxB[hl * 8:(hl + 1) * 8]
        dt = pr.dma('y' if i % 2 == 0 else 's', 'qb0' if hl == 0 else 'qb1',
                    sl.rearrange("p (a c) -> p a c", a=2),
                    wstat[i * 2:(i + 1) * 2].rearrange("a p c -> p a c"),
                    writes=buf, tag='statload')
        for j2 in range(2):
            j = 2 * i + j2
            ssl = sl[:, j2 * 2048:(j2 + 1) * 2048]
            pr.emit('v', lambda e, ssl=ssl, j=j: e.tensor_reduce(
                out=stat[:, j:j + 1], in_=ssl,
                axis=AX.X, op=ALU.add, apply_absolute_value=True),
                reads=buf, writes=[bstat], also=[dt], tag='statred')
            pr.emit('v', lambda e, ssl=ssl, j=j: e.tensor_reduce(
                out=statm[:, j:j + 1], in_=ssl,
                axis=AX.X, op=ALU.max, apply_absolute_value=True),
                reads=buf, writes=[bstat], also=[dt], tag='statmax')
    pr.dma('g', 'cl', selT_s, selT, writes=[bconsts], tag='selTload')
    pr.dma('g', 'cl', rows[:1, 0:10], selmax, writes=[brows[0]], tag='selmload')
    pr.emit('t', lambda e: e.matmul(pA[0:10, 0:1], stat[:, 0:10], ocf,
                                    start=True, stop=True),
            reads=[bstat, bconsts], writes=[bpA], tag='etotmm')
    pr.emit('v', lambda e: e.tensor_copy(out=etot, in_=pA[0:10, 0:1]),
            reads=[bpA], writes=[bstat], tag='etotcp')
    pr.emit('t', lambda e: e.matmul(pA[0:1, 0:4], etot, selT_s,
                                    start=True, stop=True),
            reads=[bstat, bconsts], writes=[bpA], tag='partmm')
    pr.emit('v', lambda e: e.tensor_copy(out=tot[:1, 0:3], in_=pA[0:1, 0:3]),
            reads=[bpA], writes=[btot], tag='partcp')
    pr.emit('t', lambda e: e.matmul(pB[:, 0:10], orf, rows[:1, 0:10],
                                     start=True, stop=True),
            reads=[brows[0], bconsts], writes=[bpB], tag='selmaxb')
    pr.emit('v', lambda e: e.tensor_tensor(out=statm[:, 0:10], in0=statm[:, 0:10],
                                           in1=pB[:, 0:10], op=ALU.mult),
            reads=[bstat, bpB], writes=[bstat], tag='statmask')
    pr.emit('v', lambda e: e.tensor_reduce(out=statm[:, 0:1], in_=statm[:, 0:10],
                                           axis=AX.X, op=ALU.max),
            reads=[bstat], writes=[bstat])
    pr.dma('g', 'b1', bounce, statm[:, 0:1], reads=[bstat], writes=[bbounce],
           tag='bounce1')
    pr.dma('g', 'b2', rows[:1, 128:256],
           bounce.rearrange("(a b) -> a b", a=1), reads=[bbounce], writes=[brows[0]],
           tag='bounce2')
    pr.emit('v', lambda e: e.tensor_reduce(out=tot[:1, 7:8], in_=rows[:1, 128:256],
                                           axis=AX.X, op=ALU.max),
            reads=[brows[0]], writes=[btot])
    pr.dma('g', 'b1', ar_in[:1, 0:3], tot[:1, 0:3], reads=[btot], writes=[bar1],
           tag='arin')
    pr.dma('g', 'b2', arm_in, tot[:1, 7:8], reads=[btot], writes=[bar2], tag='armin')
    pr.emit('g', lambda e: e.collective_compute(
        "AllReduce", ALU.add, replica_groups=[list(range(B))],
        ins=[ar_in.opt()], outs=[ar_out.opt()]),
        reads=[bar1], writes=[bar3], tag='arsum')
    pr.emit('g', lambda e: e.collective_compute(
        "AllReduce", ALU.max, replica_groups=[list(range(B))],
        ins=[arm_in.opt()], outs=[arm_out.opt()]),
        reads=[bar2], writes=[bar4], tag='armax')
    pr.dma('g', 'b1', tot[:1, 0:3], ar_out[:1, 0:3], reads=[bar3], writes=[btot],
           tag='arload')
    pr.dma('g', 'b2', tot[:1, 7:8], arm_out, reads=[bar4], writes=[btot],
           tag='armload')

    pr.emit('v', lambda e: e.tensor_scalar(out=tot[:1, 3:4], in0=tot[:1, 0:1],
                                           scalar1=1.0 / (DQ * D), scalar2=1e-5,
                                           op0=ALU.mult, op1=ALU.max),
            reads=[btot], writes=[btot])
    pr.emit('v', lambda e: e.tensor_scalar(out=tot[:1, 4:5], in0=tot[:1, 1:2],
                                           scalar1=1.0 / (D * D), scalar2=1e-5,
                                           op0=ALU.mult, op1=ALU.max),
            reads=[btot], writes=[btot])
    pr.emit('v', lambda e: e.reciprocal(out=tot[:1, 5:6], in_=tot[:1, 3:4]),
            reads=[btot], writes=[btot])
    pr.emit('v', lambda e: e.reciprocal(out=tot[:1, 6:7], in_=tot[:1, 4:5]),
            reads=[btot], writes=[btot])
    for j, src in enumerate((5, 3, 6, 4)):
        pr.emit('t', lambda e, j=j, src=src: e.matmul(
            pA[:, 4 + j:5 + j], orf, tot[:1, src:src + 1], start=True, stop=True),
            reads=[btot, bconsts], writes=[bpA])
    pr.emit('v', lambda e: e.tensor_copy(out=mcols[:, 0:4], in_=pA[:, 4:8]),
            reads=[bpA], writes=[bcols])
    pr.emit('t', lambda e: e.matmul(pA[:, 8:9], orf, tot[:1, 7:8], start=True, stop=True),
            reads=[btot, bconsts], writes=[bpA])
    pr.emit('v', lambda e: e.tensor_copy(out=sfcol, in_=pA[:, 8:9]),
            reads=[bpA], writes=[bcols])

    pr.emit('v', lambda e: e.tensor_scalar_mul(out=cols["qnm"], in0=cols["qn"],
                                               scalar1=mcols[:, 1:2]),
            reads=[bcols], writes=[bcols])
    pr.emit('v', lambda e: e.tensor_tensor(out=cols["g"], in0=cols["nw"], in1=cols["pn"],
                                           op=ALU.mult),
            reads=[bcols], writes=[bcols])
    pr.emit('v', lambda e: e.tensor_scalar_mul(out=cols["g"], in0=cols["g"],
                                               scalar1=mcols[:, 3:4]),
            reads=[bcols], writes=[bcols])
    pr.emit('s', lambda e: e.activation(out=cols["nw2"], in_=cols["nw"], func=AF.Square),
            reads=[bcols], writes=[bcols])
    pr.emit('v', lambda e: e.tensor_copy(out=nw2b, in_=cols["nw2"]),
            reads=[bcols], writes=[bcols])
    pr.emit('v', lambda e: e.tensor_copy(out=lbbf, in_=cols["lb"]),
            reads=[bcols], writes=[bcols])
    pr.emit('v', lambda e: e.tensor_scalar_mul(out=cols["gs"], in0=cols["ga"],
                                               scalar1=sfcol[:, 0:1]),
            reads=[bcols], writes=[bcols])

    def quant_tile(src_ap, src_bufs, dst_ap, dst_bufs, scale_col, extra=()):
        src_bufs = list(src_bufs)
        dst_bufs = list(dst_bufs)
        pr.emit('s', lambda e: e.activation(out=src_ap, in_=src_ap, func=AF.Identity,
                                            scale=scale_col, bias=cst[:, 0:1]),
                reads=src_bufs + [bcols, bconsts], writes=src_bufs, also=list(extra),
                tag='q1')
        pr.emit('s', lambda e: e.activation(out=src_ap, in_=src_ap, func=AF.Identity,
                                            bias=cst[:, 1:2]),
                reads=src_bufs + [bconsts], writes=src_bufs, tag='q2')
        pr.emit('v', lambda e: e.tensor_scalar_max(out=src_ap, in0=src_ap, scalar1=-1.0),
                reads=src_bufs, writes=src_bufs, tag='q3')
        pr.emit('v', lambda e: e.tensor_scalar_min(out=dst_ap, in0=src_ap, scalar1=1.0),
                reads=src_bufs, writes=dst_bufs, tag='q4')

    def quant2(i, wr, mg0, scale_col, kind):
        hl = i % 2
        sap = xB[:, hl * 4096:(hl + 1) * 4096]
        sbufs = bxB[hl * 8:(hl + 1) * 8]
        sqsem = 'qb0' if hl == 0 else 'qb1'
        dap = (kb if hl == 0 else vb)[:, 0:4096]
        dbufs = (bkb if hl == 0 else bvb)[0:8]
        wsem = f'qi{i}'
        wq_queue = 'y' if i % 2 == 0 else 's'
        dt = pr.dma(wq_queue, sqsem,
                    sap.rearrange("p (a c) -> p a c", a=2),
                    wr[mg0:mg0 + 2].rearrange("a p c -> p a c"),
                    writes=sbufs, tag=f'qin{kind}{mg0}')
        if kind == 'f':
            pr.emit('s', lambda e: e.sign(out=dap, in_=sap),
                    reads=list(sbufs), writes=list(dbufs), also=[dt], tag='fsign')
        else:
            pr.emit('s', lambda e: e.activation(out=sap, in_=sap, func=AF.Identity,
                                                scale=scale_col, bias=cst[:, 0:1]),
                    reads=list(sbufs) + [bcols, bconsts], writes=list(sbufs),
                    also=[dt], tag='q1')
            pr.emit('s', lambda e: e.activation(out=sap, in_=sap, func=AF.Identity,
                                                bias=cst[:, 1:2]),
                    reads=list(sbufs) + [bconsts], writes=list(sbufs), tag='q2')
            pr.emit('v', lambda e: e.tensor_scalar_max(out=sap, in0=sap, scalar1=-1.0),
                    reads=list(sbufs), writes=list(sbufs), tag='q3')
            pr.emit('v', lambda e: e.tensor_scalar_min(out=dap, in0=sap, scalar1=1.0),
                    reads=list(sbufs), writes=list(dbufs), tag='q4')
        for j in range(2):
            mg = mg0 + j
            dsl = dap[:, j * 2048:(j + 1) * 2048]
            if kind == 'q':
                pr.dma(wq_queue, wsem, wq_t[mg], dsl, reads=list(dbufs),
                       writes=[bwq_t[mg]], tag=f'qout{mg}')
            elif kind == 'v':
                vm = mg - MQK
                heg = vm // 2
                eo = (vm % 2) * 128
                for kc in range(FC):
                    pr.dma(wq_queue, wsem,
                           wv_t[heg, kc // 8, :, (kc % 8) * 256 + eo:(kc % 8) * 256 + eo + 128],
                           dsl[:, kc * 128:(kc + 1) * 128],
                           reads=list(dbufs), writes=[bwv_t[heg * 2 + kc // 8]],
                           tag=f'vout{mg}_{kc}')
            elif kind == 'p':
                pr.dma(wq_queue, wsem, wp_t[mg], dsl, reads=list(dbufs),
                       writes=[bwp_t[mg]], tag=f'pout{mg}')
            else:
                pr.dma(wq_queue, wsem, wf_t[mg], dsl, reads=list(dbufs),
                       writes=[bwf_t[mg]], tag=f'fout{mg}')

    quant_sched = []
    qit = [0]

    def qthunk(wr, mg0, sc, kind):
        def run():
            quant2(qit[0], wr, mg0, sc, kind)
            qit[0] += 1
        return run

    qthunks = {'q': [], 'k': [], 'v': [], 'p': [], 'f': []}
    for mg0 in range(0, 16, 2):
        qthunks['q'].append(qthunk(wq_r, mg0, mcols[:, 0:1], 'q'))
    for mg0 in range(16, MQK, 2):
        qthunks['k'].append(qthunk(wq_r, mg0, mcols[:, 0:1], 'q'))
    for mg0 in range(0, 16, 2):
        qthunks['v'].append(qthunk(wq_r, MQK + mg0, mcols[:, 0:1], 'v'))
    for mg0 in range(0, MP, 2):
        qthunks['p'].append(qthunk(wp_r, mg0, mcols[:, 2:3], 'p'))
    for mg0 in range(0, MF, 2):
        qthunks['f'].append(qthunk(wf_r, mg0, None, 'f'))

    pr.pe_strict = STRICT_MAIN
    x_of = lambda t: (xA, bxA) if t % 2 == 0 else (xB, bxB)

    def emit_xload(t):
        xs, bxs = x_of(t)
        for fc in range(FC):
            pr.dma('g', 'xA' if t % 2 == 0 else 'xB',
                   xs[:, fc * TILE:(fc + 1) * TILE],
                   xT[fc * 128:(fc + 1) * 128, t * TILE:(t + 1) * TILE],
                   writes=[bxs[fc]], tag=f'xload{t}_{fc}')

    def emit_ln1_stats(t):
        xs, bxs = x_of(t)
        def ln1_stat(j):
            sl_ = bfp[:, (j % 2) * TILE:(j % 2) * TILE + TILE]
            sb_ = bbfp[j % 2]
            sq_ = bfp[:, (2 + j % 2) * TILE:(2 + j % 2) * TILE + TILE]
            sqb_ = bbfp[2 + j % 2]
            pr.emit('t', lambda e, sl_=sl_, j=j: e.matmul(
                pS1[:1, :], ocb, sl_, start=(j == 0), stop=(j == FC - 1)),
                reads=[sb_, bconsts], writes=[bpS1], tag='ln1sum')
            pr.emit('t', lambda e, sq_=sq_, j=j: e.matmul(
                pS2[:1, :], ocb, sq_, start=(j == 0), stop=(j == FC - 1)),
                reads=[sqb_, bconsts], writes=[bpS2], tag='ln1sum2')

        for fc in range(FC):
            sl = bfp[:, (fc % 2) * TILE:(fc % 2) * TILE + TILE]
            sbuf = bbfp[fc % 2]
            sq = bfp[:, (2 + fc % 2) * TILE:(2 + fc % 2) * TILE + TILE]
            sqb = bbfp[2 + fc % 2]
            pr.emit('g', lambda e, sl=sl, xs=xs, fc=fc: e.tensor_copy(
                out=sl, in_=xs[:, fc * TILE:(fc + 1) * TILE]),
                reads=[bxs[fc]], writes=[sbuf], tag='xbcast')
            pr.emit('s', lambda e, sq=sq, xs=xs, fc=fc: e.activation(
                out=sq, in_=xs[:, fc * TILE:(fc + 1) * TILE], func=AF.Square),
                reads=[bxs[fc]], writes=[sqb], tag='xsq')
            if fc > 0:
                ln1_stat(fc - 1)
        ln1_stat(FC - 1)
        mu = rows[:1, 0:TILE]
        m2 = rows[:1, TILE:2 * TILE]
        beta = rows[:1, 2 * TILE:3 * TILE]
        k1 = (1.0 + EPS32) / D
        c0 = 1e-5 * EPS32
        pr.emit('v', lambda e: e.tensor_scalar_mul(out=mu, in0=pS1[:1, :], scalar1=1.0 / D),
                reads=[bpS1], writes=[brows[0]], tag='mu')
        pr.emit('g', lambda e: e.tensor_tensor(out=m2, in0=mu, in1=mu, op=ALU.mult),
                reads=[brows[0]], writes=[brows[1]])
        pr.emit('g', lambda e: e.tensor_scalar(out=m2, in0=m2, scalar1=1.0 + EPS32,
                                               scalar2=-c0, op0=ALU.mult, op1=ALU.add),
                reads=[brows[1]], writes=[brows[1]])
        pr.emit('v', lambda e: e.scalar_tensor_tensor(out=beta, in0=pS2[:1, :], scalar=k1,
                                                      in1=m2, op0=ALU.mult, op1=ALU.subtract),
                reads=[bpS2, brows[1]], writes=[brows[2]])
        pr.emit('s', lambda e: e.activation(out=beta, in_=beta, func=AF.Ln),
                reads=[brows[2]], writes=[brows[2]])
        pr.emit('s', lambda e: e.activation(out=beta, in_=beta, func=AF.Exp,
                                            scale=cst[0:1, 2:3]),
                reads=[brows[2]], writes=[brows[2]])

    def emit_ln1_apply(t):
        xs, bxs = x_of(t)
        mu = rows[:1, 0:TILE]
        beta = rows[:1, 2 * TILE:3 * TILE]
        pr.emit('t', lambda e: e.matmul(pC, orf, mu, start=True, stop=True),
                reads=[brows[0], bconsts], writes=[bpC], tag='mub')
        pr.emit('t', lambda e: e.matmul(pE, orf, beta, start=True, stop=True),
                reads=[brows[2], bconsts], writes=[bpE], tag='betab')
        pr.emit('v', lambda e: e.tensor_copy(out=brow[:, 0:TILE], in_=pE),
                reads=[bpE], writes=[bbrow[0]], tag='betabf')
        for fc in range(FC):
            tsl = t1s[:, (fc % 2) * TILE:(fc % 2) * TILE + TILE]
            tb = bt1[fc % 2]
            pr.emit('v', lambda e, tsl=tsl, xs=xs, fc=fc: e.tensor_tensor(
                out=tsl, in0=xs[:, fc * TILE:(fc + 1) * TILE], in1=pC, op=ALU.subtract),
                reads=[bxs[fc], bpC], writes=[tb], tag='t1')
            pr.emit('v', lambda e, tsl=tsl, fc=fc: e.scalar_tensor_tensor(
                out=h[:, fc * TILE:(fc + 1) * TILE], in0=tsl,
                scalar=cols["qnm"][:, fc:fc + 1],
                in1=brow[:, 0:TILE], op0=ALU.mult, op1=ALU.mult),
                reads=[tb, bbrow[0], bcols], writes=[bh[fc]], tag='happ')

    emit_xload(0)
    emit_ln1_stats(0)
    emit_ln1_apply(0)

    for t in range(NT):
        xs, bxs = x_of(t)
        t0 = t * TILE

        for g in range(NEG):
            if t == 0:
                qthunks['q'][2 * g](); qthunks['q'][2 * g + 1]()
                qthunks['k'][2 * g](); qthunks['k'][2 * g + 1]()
                qthunks['v'][2 * g](); qthunks['v'][2 * g + 1]()
                qthunks['p'][g](); qthunks['f'][g]()
            for j in range(8):
                mg = (4 * g + j) if j < 4 else (16 + 4 * g + j - 4)
                slot = j % 2
                wsl = wpool[:, slot * 2048:(slot + 1) * 2048]
                psum, bps = (pA, bpA) if slot == 0 else (pB, bpB)
                pr.dma('y', 'wpA' if slot == 0 else 'wpB', wsl, wq_t[mg],
                       reads=[bwq_t[mg]], writes=[bwp[slot]], tag=f'wq{t}_{mg}')
                for kc in range(FC):
                    pr.emit('t', lambda e, wsl=wsl, kc=kc, psum=psum: e.matmul(
                        psum, wsl[:, kc * 128:(kc + 1) * 128],
                        h[:, kc * TILE:(kc + 1) * TILE],
                        start=(kc == 0), stop=(kc == FC - 1)),
                        reads=[bwp[slot], bh[kc]], writes=[bps], tag=f'qkmm{mg}')
                if mg < 16:
                    pr.emit('s', lambda e, mg=mg, psum=psum: e.copy(
                        out=qb[:, mg * TILE:(mg + 1) * TILE], in_=psum),
                        reads=[bps], writes=[bqb[mg]], tag='qevac')
                else:
                    hh = mg - 16
                    pr.emit('s', lambda e, hh=hh, psum=psum: e.copy(
                        out=kb[:, hh * TILE:(hh + 1) * TILE], in_=psum),
                        reads=[bps], writes=[bkb[hh]], tag='kevac')
            if g > 0:
                for hh in range((g - 1) * 4, g * 4):
                    sq = bfp[:, (hh % 4) * TILE:(hh % 4) * TILE + TILE]
                    sqb = bbfp[hh % 4]
                    pr.emit('t', lambda e, sq=sq, hh=hh: e.matmul(
                        pS1[:1, :], ocb, sq, start=(hh == 0), stop=(hh == FC - 1)),
                        reads=[sqb, bconsts], writes=[bpS1], tag='osumA')
                    pr.emit('t', lambda e, sq=sq, hh=hh: e.matmul(
                        pS2[:1, :], nw2b[:, hh:hh + 1], sq,
                        start=(hh == 0), stop=(hh == FC - 1)),
                        reads=[sqb, bcols, bconsts], writes=[bpS2], tag='osumB')

            for heg in (2 * g, 2 * g + 1):
                wslot = heg % 2
                wsl = wpool[:, wslot * 2048:(wslot + 1) * 2048]
                pr.dma('y', 'vs0', vsl, wv_t[heg, 0],
                       reads=[bwv_t[heg * 2]], writes=[bvsl[0]], tag=f'wvlo{t}_{heg}')
                pr.dma('y', 'wpA' if wslot == 0 else 'wpB', wsl, wv_t[heg, 1],
                       reads=[bwv_t[heg * 2 + 1]], writes=[bwp[wslot]],
                       tag=f'wvhi{t}_{heg}')
                for tc in range(NCH):
                    psum, bps = ((pA, bpA), (pB, bpB))[tc % 2]
                    po = (tc // 2) * 256
                    for kc in range(FC):
                        rsl = (vsl[:, (kc % 8) * 256:(kc % 8) * 256 + 256] if kc < 8
                               else wsl[:, (kc - 8) * 256:(kc - 8) * 256 + 256])
                        rb = bvsl[0] if kc < 8 else bwp[wslot]
                        pr.emit('t', lambda e, kc=kc, tc=tc, psum=psum, po=po,
                                rsl=rsl: e.matmul(
                            psum[:, po:po + 256],
                            h[:, kc * TILE + tc * 128: kc * TILE + tc * 128 + 128],
                            rsl, start=(kc == 0), stop=(kc == FC - 1)),
                            reads=[rb, bh[kc]], writes=[bps], tag=f'vmm{heg}_{tc}')
                    pr.emit('s', lambda e, tc=tc, heg=heg, psum=psum, po=po: e.copy(
                        out=vb[:, tc * D + heg * 256: tc * D + heg * 256 + 256],
                        in_=psum[:, po:po + 256]),
                        reads=[bps], writes=[bvb[tc * 4 + heg // 2]], tag='vevac')

            hs = range(g * 4, (g + 1) * 4)
            for c in range(NCH):
                cg = t * NCH + c
                for i, hh in enumerate(hs):
                    pr.emit('t', lambda e, hh=hh, i=i, c=c: e.matmul(
                        pC[:, i * 128:(i + 1) * 128],
                        kb[:, hh * TILE + c * 128: hh * TILE + c * 128 + 128],
                        qb[:, hh * TILE + c * 128: hh * TILE + c * 128 + 128],
                        start=True, stop=True),
                        reads=[bkb[hh], bqb[hh]], writes=[bpC], tag=f'score{c}_{hh}')
                sc = scm[:, (c % 2) * TILE:(c % 2) * TILE + TILE]
                scb = bscm[c % 2]
                pr.emit('v', lambda e, sc=sc, g=g: e.tensor_tensor(
                    out=sc, in0=pC, in1=maskT[:, g * 512:(g + 1) * 512], op=ALU.mult),
                    reads=[bpC, bconsts], writes=[scb], tag='scevac')
                qt = qts[:, (c % 2) * TILE:(c % 2) * TILE + TILE]
                qtb = bqt[c % 2]
                pr.emit('v', lambda e, qt=qt, g=g, c=c: e.tensor_tensor(
                    out=qt.rearrange("p (r n) -> p r n", r=4),
                    in0=_ap3(qb, g, c),
                    in1=_ap3c(ltrep, g), op=ALU.mult),
                    reads=[bqb[hs[0]], bqb[hs[1]], bqb[hs[2]], bqb[hs[3]], bconsts],
                    writes=[qtb], tag='qtilde')
                pdo = (c % 2) * 512
                for i, hh in enumerate(hs):
                    pr.emit('t', lambda e, hh=hh, i=i, c=c, pdo=pdo: e.transpose(
                        pD[:, pdo + i * 128: pdo + (i + 1) * 128],
                        kb[:, hh * TILE + c * 128: hh * TILE + c * 128 + 128], iden),
                        reads=[bkb[hh], bconsts], writes=[bpD[c % 2]], tag='ktr', tr=True)
                kt = kts[:, (c % 2) * TILE:(c % 2) * TILE + TILE]
                ktb = bkt[c % 2]
                pr.emit('v', lambda e, kt=kt, g=g, pdo=pdo, c=c: e.tensor_tensor(
                    out=kt, in0=pD[:, pdo:pdo + 512], in1=lrT[:, g * 512:(g + 1) * 512],
                    op=ALU.mult),
                    reads=[bpD[c % 2], bconsts], writes=[ktb], tag='ktevac')
                for i, hh in enumerate(hs):
                    pr.emit('t', lambda e, hh=hh, i=i, c=c, sc=sc: e.matmul(
                        pE[:, i * 128:(i + 1) * 128],
                        vb[:, c * D + hh * 128: c * D + hh * 128 + 128],
                        sc[:, i * 128:(i + 1) * 128], start=True, stop=(cg == 0)),
                        reads=[bvb[c * 4 + hh // 4], scb], writes=[bpE],
                        tag=f'intra{c}_{hh}')
                    if cg > 0:
                        pr.emit('t', lambda e, hh=hh, i=i, qt=qt: e.matmul(
                            pE[:, i * 128:(i + 1) * 128],
                            Sb[:, hh * 128:(hh + 1) * 128],
                            qt[:, i * 128:(i + 1) * 128], start=False, stop=True),
                            reads=[bSb[hh], qtb], writes=[bpE], tag=f'cross{c}_{hh}')
                pr.emit('v', lambda e, g=g, c=c: e.tensor_copy(
                    out=_ap3o(pb, g, c), in_=pE.rearrange("p (r n) -> p r n", r=4)),
                    reads=[bpE], writes=[bpb[g * 4], bpb[g * 4 + 1], bpb[g * 4 + 2],
                                         bpb[g * 4 + 3]], tag='oevac')
                for i, hh in enumerate(hs):
                    pr.emit('t', lambda e, hh=hh, i=i, c=c, kt=kt: e.matmul(
                        pF[:, i * 128:(i + 1) * 128],
                        kt[:, i * 128:(i + 1) * 128],
                        vb[:, c * D + hh * 128: c * D + hh * 128 + 128],
                        start=True, stop=True),
                        reads=[ktb, bvb[c * 4 + hh // 4]], writes=[bpF], tag=f'kv{c}_{hh}')
                for i, hh in enumerate(hs):
                    if cg == 0:
                        pr.emit('v', lambda e, hh=hh, i=i: e.tensor_copy(
                            out=Sf[:, hh * 128:(hh + 1) * 128],
                            in_=pF[:, i * 128:(i + 1) * 128]),
                            reads=[bpF], writes=[bSf[hh]], tag='Sinit')
                    else:
                        pr.emit('v', lambda e, hh=hh, i=i: e.scalar_tensor_tensor(
                            out=Sf[:, hh * 128:(hh + 1) * 128],
                            in0=Sf[:, hh * 128:(hh + 1) * 128],
                            scalar=lamT_c[:, hh:hh + 1],
                            in1=pF[:, i * 128:(i + 1) * 128],
                            op0=ALU.mult, op1=ALU.add),
                            reads=[bSf[hh], bpF, bconsts], writes=[bSf[hh]], tag='Supd')
                pr.emit('s', lambda e, g=g: e.copy(
                    out=Sb[:, g * 512:(g + 1) * 512], in_=Sf[:, g * 512:(g + 1) * 512]),
                    reads=[bSf[hs[0]], bSf[hs[1]], bSf[hs[2]], bSf[hs[3]]],
                    writes=[bSb[hs[0]], bSb[hs[1]], bSb[hs[2]], bSb[hs[3]]], tag='Sbcopy')

            for hh in hs:
                sq = bfp[:, (hh % 4) * TILE:(hh % 4) * TILE + TILE]
                sqb = bbfp[hh % 4]
                pr.emit('s', lambda e, sq=sq, hh=hh: e.activation(
                    out=sq, in_=pb[:, hh * TILE:(hh + 1) * TILE], func=AF.Square),
                    reads=[bpb[hh]], writes=[sqb], tag='psq')

        if DEBUG_O:
            for fc in range(FC):
                pr.dma('g', 'qs', odbg[fc * 128:(fc + 1) * 128, t0:t0 + TILE],
                       pb[:, fc * TILE:(fc + 1) * TILE],
                       reads=[bpb[fc]], writes=[bout], tag=f'odbg{t}_{fc}')
            if t < 2:
                pr.dma('g', 'qs', qdbg[t, 0], qb, reads=list(bqb), writes=[bout], tag=f'qdbg{t}')
                pr.dma('g', 'qs', qdbg[t, 1], kb, reads=list(bkb), writes=[bout], tag=f'kdbg{t}')
                pr.dma('g', 'qs', qdbg[t, 2], vb, reads=list(bvb), writes=[bout], tag=f'vdbg{t}')
                pr.dma('g', 'qs', qdbg[t, 3], h, reads=list(bh), writes=[bout], tag=f'hdbg{t}')

        if t == 0:
            for j in range(4, 8):
                qthunks['p'][j]()
        for hh in range(12, 16):
            sq = bfp[:, (hh % 4) * TILE:(hh % 4) * TILE + TILE]
            sqb = bbfp[hh % 4]
            pr.emit('t', lambda e, sq=sq, hh=hh: e.matmul(
                pS1[:1, :], ocb, sq, start=(hh == 0), stop=(hh == FC - 1)),
                reads=[sqb, bconsts], writes=[bpS1], tag='osumA')
            pr.emit('t', lambda e, sq=sq, hh=hh: e.matmul(
                pS2[:1, :], nw2b[:, hh:hh + 1], sq,
                start=(hh == 0), stop=(hh == FC - 1)),
                reads=[sqb, bcols, bconsts], writes=[bpS2], tag='osumB')

        eA = 1e-6 * 128.0
        al = rows[:1, 0:TILE]
        tmp = rows[:1, TILE:2 * TILE]

        pr.emit('v', lambda e: e.tensor_scalar(out=tmp, in0=pS1[:1, :],
                                               scalar1=EPS32 / D,
                                               scalar2=EPS32 * eA, op0=ALU.mult,
                                               op1=ALU.add),
                reads=[bpS1], writes=[brows[1]], tag='alp1')
        pr.emit('v', lambda e: e.scalar_tensor_tensor(out=al, in0=pS2[:1, :],
                                                      scalar=1.0 / D, in1=tmp,
                                                      op0=ALU.mult, op1=ALU.add),
                reads=[bpS2, brows[1]], writes=[brows[0]])
        pr.emit('s', lambda e: e.activation(out=al, in_=al, func=AF.Ln),
                reads=[brows[0]], writes=[brows[0]])
        pr.emit('s', lambda e: e.activation(out=al, in_=al, func=AF.Exp,
                                            scale=cst[0:1, 2:3]),
                reads=[brows[0]], writes=[brows[0]])

        def emit_alpha_bcast():
            pr.emit('t', lambda e: e.matmul(pC, orf, al, start=True, stop=True),
                    reads=[brows[0], bconsts], writes=[bpC], tag='alb')
            pr.emit('v', lambda e: e.tensor_copy(out=brow[:, 2 * TILE:3 * TILE], in_=pC),
                    reads=[bpC], writes=[bbrow[2]], tag='albf')
        for fc in range(FC):
            pr.emit('v', lambda e, fc=fc: e.tensor_scalar_mul(
                out=h[:, fc * TILE:(fc + 1) * TILE],
                in0=pb[:, fc * TILE:(fc + 1) * TILE],
                scalar1=cols["g"][:, fc:fc + 1]),
                reads=[bpb[fc], bcols], writes=[bh[fc]], tag='zbuild')

        def emit_ln2_stat(j):
            sl_ = bfp[:, (j % 2) * TILE:(j % 2) * TILE + TILE]
            sb_ = bbfp[j % 2]
            sq_ = bfp[:, (2 + j % 2) * TILE:(2 + j % 2) * TILE + TILE]
            sqb_ = bbfp[2 + j % 2]
            pr.emit('t', lambda e, sl_=sl_, j=j: e.matmul(
                pS1[:1, :], ocb, sl_, start=(j == 0), stop=(j == FC - 1)),
                reads=[sb_, bconsts], writes=[bpS1], tag='ln2sum')
            pr.emit('t', lambda e, sq_=sq_, j=j: e.matmul(
                pS2[:1, :], ocb, sq_, start=(j == 0), stop=(j == FC - 1)),
                reads=[sqb_, bconsts], writes=[bpS2], tag='ln2sum2')

        if t + 1 < NT and t > 0:
            emit_xload(t + 1)
        for mg in range(MP):
            slot = mg % 2
            wsl = wpool[:, slot * 2048:(slot + 1) * 2048]
            psum, bps = (pA, bpA) if slot == 0 else (pB, bpB)
            pr.dma('y', 'wpA' if slot == 0 else 'wpB', wsl, wp_t[mg],
                   reads=[bwp_t[mg]], writes=[bwp[slot]], tag=f'wp{t}_{mg}')
            for kc in range(FC):
                pr.emit('t', lambda e, wsl=wsl, kc=kc, psum=psum: e.matmul(
                    psum, wsl[:, kc * 128:(kc + 1) * 128],
                    h[:, kc * TILE:(kc + 1) * TILE],
                    start=(kc == 0), stop=(kc == FC - 1)),
                    reads=[bwp[slot], bh[kc]], writes=[bps], tag=f'pmm{mg}')
            if mg == 0:
                emit_alpha_bcast()

            def proj_evac(j, psum_, bps_):
                pr.emit('v', lambda e, j=j, psum_=psum_: e.tensor_tensor(
                    out=t1s[:, (j % 2) * TILE:(j % 2) * TILE + TILE], in0=psum_,
                    in1=brow[:, 2 * TILE:3 * TILE], op=ALU.mult),
                    reads=[bps_, bbrow[2]], writes=[bt1[j % 2]], tag='res1a')
                pr.emit('v', lambda e, j=j, xs=xs: e.tensor_tensor(
                    out=xs[:, j * TILE:(j + 1) * TILE],
                    in0=t1s[:, (j % 2) * TILE:(j % 2) * TILE + TILE],
                    in1=xs[:, j * TILE:(j + 1) * TILE], op=ALU.add),
                    reads=[bt1[j % 2], bxs[j]], writes=[bxs[j]], tag='res1')
            proj_evac(mg, psum, bps)
            sl = bfp[:, (mg % 2) * TILE:(mg % 2) * TILE + TILE]
            sbuf = bbfp[mg % 2]
            sq = bfp[:, (2 + mg % 2) * TILE:(2 + mg % 2) * TILE + TILE]
            sqb = bbfp[2 + mg % 2]
            pr.emit('g', lambda e, sl=sl, xs=xs, mg=mg: e.tensor_copy(
                out=sl, in_=xs[:, mg * TILE:(mg + 1) * TILE]),
                reads=[bxs[mg]], writes=[sbuf], tag='x1b')
            pr.emit('s', lambda e, sq=sq, xs=xs, mg=mg: e.activation(
                out=sq, in_=xs[:, mg * TILE:(mg + 1) * TILE], func=AF.Square),
                reads=[bxs[mg]], writes=[sqb], tag='x1sq')
            if mg > 0:
                emit_ln2_stat(mg - 1)
        emit_ln2_stat(FC - 1)

        mu2 = rows[:1, 0:TILE]
        m22 = rows[:1, TILE:2 * TILE]
        a2 = rows[:1, 2 * TILE:3 * TILE]
        k2 = (1.0 + 1e-5) / D
        c2 = 1e-5 * 1e-5
        pr.emit('v', lambda e: e.tensor_scalar_mul(out=mu2, in0=pS1[:1, :], scalar1=1.0 / D),
                reads=[bpS1], writes=[brows[0]], tag='mu2')
        pr.emit('g', lambda e: e.tensor_tensor(out=m22, in0=mu2, in1=mu2, op=ALU.mult),
                reads=[brows[0]], writes=[brows[1]])
        pr.emit('g', lambda e: e.tensor_scalar(out=m22, in0=m22, scalar1=1.0 + 1e-5,
                                               scalar2=-c2, op0=ALU.mult, op1=ALU.add),
                reads=[brows[1]], writes=[brows[1]])
        pr.emit('v', lambda e: e.scalar_tensor_tensor(out=a2, in0=pS2[:1, :], scalar=k2,
                                                      in1=m22, op0=ALU.mult,
                                                      op1=ALU.subtract),
                reads=[bpS2, brows[1]], writes=[brows[2]])
        pr.emit('s', lambda e: e.activation(out=a2, in_=a2, func=AF.Ln),
                reads=[brows[2]], writes=[brows[2]])
        pr.emit('s', lambda e: e.activation(out=a2, in_=a2, func=AF.Exp,
                                            scale=cst[0:1, 2:3]),
                reads=[brows[2]], writes=[brows[2]])
        pr.emit('t', lambda e: e.matmul(pC, orf, mu2, start=True, stop=True),
                reads=[brows[0], bconsts], writes=[bpC], tag='mu2b')
        pr.emit('t', lambda e: e.matmul(pE, orf, a2, start=True, stop=True),
                reads=[brows[2], bconsts], writes=[bpE], tag='a2b')
        pr.emit('v', lambda e: e.tensor_copy(out=brow[:, TILE:2 * TILE], in_=pE),
                reads=[bpE], writes=[bbrow[1]], tag='a2bf')
        for fc in range(FC):
            tsl = t1s[:, (fc % 2) * TILE:(fc % 2) * TILE + TILE]
            tb = bt1[fc % 2]
            pr.emit('v', lambda e, tsl=tsl, xs=xs, fc=fc: e.tensor_tensor(
                out=tsl, in0=xs[:, fc * TILE:(fc + 1) * TILE], in1=pC, op=ALU.subtract),
                reads=[bxs[fc], bpC], writes=[tb], tag='t3')
            pr.emit('v', lambda e, tsl=tsl, fc=fc: e.tensor_scalar_mul(
                out=qb[:, fc * TILE:(fc + 1) * TILE], in0=tsl,
                scalar1=cols["lw"][:, fc:fc + 1]),
                reads=[tb, bcols], writes=[bqb[fc]], tag='xnapp')

        if t == 0:
            for j in range(4, 8):
                qthunks['f'][j]()
            emit_xload(1)
        if t + 1 < NT:
            emit_ln1_stats(t + 1)

        for mg in range(MF):
            slot = mg % 2
            wsl = wpool[:, slot * 2048:(slot + 1) * 2048]
            psum, bps = (pA, bpA) if slot == 0 else (pB, bpB)
            pr.dma('y', 'wpA' if slot == 0 else 'wpB', wsl, wf_t[mg],
                   reads=[bwf_t[mg]], writes=[bwp[slot]], tag=f'wf{t}_{mg}')
            for kc in range(FC):
                pr.emit('t', lambda e, wsl=wsl, kc=kc, psum=psum: e.matmul(
                    psum, wsl[:, kc * 128:(kc + 1) * 128],
                    qb[:, kc * TILE:(kc + 1) * TILE],
                    start=(kc == 0), stop=(kc == FC - 1)),
                    reads=[bwp[slot], bqb[kc]], writes=[bps], tag=f'fmm{mg}')
            if t == 0:
                pr.emit('t', lambda e, wsl=wsl: e.matmul(
                    pD.bitcast(F32)[:, 0:1] if False else pF[:, 0:1],
                    wsl[:, 0:128], lbbf[:, 0:1], start=True, stop=False),
                    reads=[bwp[slot], bcols], writes=[bpF], tag='lnbp')
                for kc in range(1, FC):
                    pr.emit('t', lambda e, wsl=wsl, kc=kc: e.matmul(
                        pF[:, 0:1], wsl[:, kc * 128:(kc + 1) * 128],
                        lbbf[:, kc:kc + 1], start=False, stop=(kc == FC - 1)),
                        reads=[bwp[slot], bcols], writes=[bpF], tag='lnbp')
                pr.emit('v', lambda e, mg=mg: e.tensor_scalar_mul(
                    out=cols["b2"][:, mg:mg + 1], in0=pF[:, 0:1],
                    scalar1=cols["gs"][:, mg:mg + 1]),
                    reads=[bpF, bcols], writes=[bcols], tag='b2col')
                pr.emit('v', lambda e, mg=mg: e.tensor_scalar_add(
                    out=cols["b2"][:, mg:mg + 1], in0=cols["b2"][:, mg:mg + 1],
                    scalar1=cols["be"][:, mg:mg + 1]),
                    reads=[bcols], writes=[bcols], tag='b2col2')
            pr.emit('v', lambda e, mg=mg, psum=psum: e.tensor_tensor(
                out=t1s[:, (mg % 2) * TILE:(mg % 2) * TILE + TILE], in0=psum,
                in1=brow[:, TILE:2 * TILE], op=ALU.mult),
                reads=[bps, bbrow[1]], writes=[bt1[mg % 2]], tag='fevA')
            pr.emit('v', lambda e, mg=mg: e.tensor_scalar_mul(
                out=t1s[:, (mg % 2) * TILE:(mg % 2) * TILE + TILE],
                in0=t1s[:, (mg % 2) * TILE:(mg % 2) * TILE + TILE],
                scalar1=cols["gs"][:, mg:mg + 1]),
                reads=[bt1[mg % 2], bcols], writes=[bt1[mg % 2]], tag='fevA2')
            pr.emit('v', lambda e, mg=mg, xs=xs: e.tensor_tensor(
                out=xs[:, mg * TILE:(mg + 1) * TILE],
                in0=t1s[:, (mg % 2) * TILE:(mg % 2) * TILE + TILE],
                in1=xs[:, mg * TILE:(mg + 1) * TILE], op=ALU.add),
                reads=[bt1[mg % 2], bxs[mg]], writes=[bxs[mg]], tag='fevB')
            pr.emit('v', lambda e, mg=mg, xs=xs: e.tensor_scalar_add(
                out=xs[:, mg * TILE:(mg + 1) * TILE],
                in0=xs[:, mg * TILE:(mg + 1) * TILE],
                scalar1=cols["b2"][:, mg:mg + 1]),
                reads=[bxs[mg], bcols], writes=[bxs[mg]], tag='fbias')
            pr.dma('g', 'st', outT[mg * 128:(mg + 1) * 128, t0:t0 + TILE],
                   xs[:, mg * TILE:(mg + 1) * TILE],
                   reads=[bxs[mg]], writes=[bout], tag=f'store{t}_{mg}')
            if mg == 3 and t + 1 < NT:
                emit_ln1_apply(t + 1)

    n_s = pr.dma_cnt['st']
    pr.emit('g', lambda e: e.nop(), also=[('d', 'st', n_s)], tag='final_wait')

    with nc.Block() as block:
        pr.finalize(block, sems, dsems)
    es.close()
    return nc


def _one(bufs):
    return bufs[0]


def _ap3(qb, g, c):
    return qb.rearrange("p (hh t) -> p hh t", hh=H)[:, g * 4:(g + 1) * 4,
                                                   c * 128:c * 128 + 128]


def _ap3c(ltrep, g):
    return ltrep[:, g * 512:(g + 1) * 512].rearrange("p (r n) -> p r n", r=4)


def _ap3o(pb, g, c):
    return pb.rearrange("p (hh t) -> p hh t", hh=FC)[:, g * 4:(g + 1) * 4,
                                                     c * 128:c * 128 + 128]


_NC_CACHE = None


def kernel(L0_attn_x, L0_attn_qkv_w, L0_attn_qkv_norm_w, L0_attn_proj_w,
           L0_attn_proj_norm_w, L0_attn_norm_w, L0_attn_s, L0_ffn_weight,
           L0_ffn_gamma, L0_ffn_beta, L0_ffn_ln_weight, L0_ffn_ln_bias):
    global _NC_CACHE
    x = np.asarray(L0_attn_x, np.float32)

    def rearr(w, nmg):
        return np.ascontiguousarray(
            np.asarray(w, np.float32).reshape(nmg, 128, FC, 128).transpose(0, 3, 2, 1)
            .reshape(nmg, 128, 2048))

    wq = rearr(L0_attn_qkv_w, 48)
    wp = rearr(L0_attn_proj_w, 16)
    wf = rearr(L0_ffn_weight, 16)

    def recol(v):
        return np.ascontiguousarray(np.asarray(v, np.float32).reshape(FC, 128).T)

    base = {
        "wq_r": wq, "wp_r": wp, "wf_r": wf,
        "qn_r": recol(L0_attn_qkv_norm_w), "pn_r": recol(L0_attn_proj_norm_w),
        "nw_r": recol(L0_attn_norm_w), "ga_r": recol(L0_ffn_gamma),
        "be_r": recol(L0_ffn_beta), "lw_r": recol(L0_ffn_ln_weight),
        "lb_r": recol(L0_ffn_ln_bias),
        "s_in": np.asarray(L0_attn_s, np.float32).reshape(1, H),
        "iota_c": np.arange(128, dtype=np.float32).reshape(128, 1),
        "iota_r": np.arange(128, dtype=np.float32).reshape(1, 128),
        "diffp": np.maximum(np.arange(128)[None, :] - np.arange(128)[:, None], 0
                            ).astype(np.float32),
        "causal": (np.arange(128)[None, :] >= np.arange(128)[:, None]
                   ).astype(np.float32),
        "ones128": np.ones((128, 128), np.float32),
        "iden_in": np.eye(128).astype(ml_dtypes.bfloat16),
        "onescol_bf": np.ones((128, 1), ml_dtypes.bfloat16),
        "onescol_f": np.ones((128, 1), np.float32),
        "onesrow_f": np.ones((1, 128), np.float32),
        "cvals_in": np.tile(np.array([[RND_C, -RND_C, -0.5, 0.0]], np.float32),
                            (128, 1)),
    }
    wall = np.concatenate([wq, wp, wf], axis=0)
    in_maps = []
    for b in range(B):
        m = dict(base)
        m["xT"] = np.ascontiguousarray(x[b].T)
        rows_b = np.ascontiguousarray(wall[10 * b:10 * (b + 1)])
        m["wstat"] = rows_b
        st_ = np.zeros((10, 4), np.float32)
        sm_ = np.zeros((1, 10), np.float32)
        for j in range(10):
            e = 10 * b + j
            k = 0 if e < 48 else (1 if e < 64 else 2)
            st_[j, k] = 1.0
            if k == 2:
                sm_[0, j] = 1.0
        m["selT"] = st_
        m["selmax"] = sm_
        in_maps.append(m)

    if _NC_CACHE is None:
        _NC_CACHE = build_nc()
    res = run_bass_kernel_spmd(_NC_CACHE, in_maps, core_ids=list(range(B)))
    out = np.stack([np.asarray(r["outT"]).T for r in res.results], axis=0)
    return out.astype(np.float32)


# version 16 (final)
# speedup vs baseline: 1.0085x; 1.0085x over previous
import numpy as np
import ml_dtypes
from contextlib import ExitStack

import concourse.bass as bass
import concourse.mybir as mybir
from concourse.bass_utils import run_bass_kernel_spmd

F32 = mybir.dt.float32
BF16 = mybir.dt.bfloat16
AF = mybir.ActivationFunctionType
ALU = mybir.AluOpType
AX = mybir.AxisListType

B = 8
SEQ = 4096
D = 2048
FC = 16
H = 16
HD = 128
CH = 128
TILE = 512
NT = SEQ // TILE
NCH = TILE // CH
DQ = 3 * D
MQK = 32
MP = 16
MF = 16
NEG = 4
RND_C = float(3 * 2**22)
EPS32 = float(np.finfo(np.float32).eps)

ENG_NAMES = {'t': 'tensor', 'v': 'vector', 's': 'scalar', 'g': 'gpsimd', 'y': 'sync'}
PARANOID_PE = False
STRICT_MAIN = False
DEBUG_O = False


class Buf:
    __slots__ = ('last_w', 'readers', 'name')

    def __init__(self, name=''):
        self.last_w = None
        self.readers = []
        self.name = name


DMA_SEMS = (('xA', 'xB', 'wpA', 'wpB', 'st', 'qs', 'cl', 'b1', 'b2', 'vs0',
             'qa0', 'qa1', 'qb0', 'qb1')
            + tuple(f'qi{i}' for i in range(40)))


class Prog:
    def __init__(self, nc):
        self.nc = nc
        self.recs = {k: [] for k in ENG_NAMES}
        self.dma_cnt = {k: 0 for k in DMA_SEMS}
        self.pe_strict = True

    def emit(self, ek, fn, reads=(), writes=(), also=(), tag='', tr=False):
        deps = []
        for b in reads:
            if b.last_w is not None:
                deps.append(b.last_w)
        for b in writes:
            if b.last_w is not None:
                deps.append(b.last_w)
            deps.extend(b.readers)
        deps.extend(also)
        ndeps = []
        seen = set()
        for d in deps:
            if d is None:
                continue
            kind, e2, i2 = d
            if (kind == 'c' and e2 == ek == 't' and not PARANOID_PE
                    and not self.pe_strict
                    and not tr and not self.recs['t'][i2].get('tr')):
                continue
            if kind == 'd':
                d = (kind, e2, self.dma_cnt[e2])
            if d in seen:
                continue
            seen.add(d)
            if kind == 'c':
                self.recs[e2][i2]['inc'] = True
            ndeps.append(d)
        idx = len(self.recs[ek])
        self.recs[ek].append(dict(fn=fn, waits=ndeps, inc=False, dma=None, tag=tag,
                                  tr=tr))
        tok = ('c', ek, idx)
        for b in reads:
            b.readers.append(tok)
        for b in writes:
            b.last_w = tok
            b.readers = []
        return tok

    def dma(self, qk, sem, out_ap, in_ap, reads=(), writes=(), also=(), tag='', **kw):
        deps = []
        for b in reads:
            if b.last_w is not None:
                deps.append(b.last_w)
        for b in writes:
            if b.last_w is not None:
                deps.append(b.last_w)
            deps.extend(b.readers)
        deps.extend(also)
        ndeps = []
        seen = set()
        for d in deps:
            if d is None:
                continue
            kind, e2, i2 = d
            if kind == 'd':
                d = (kind, e2, self.dma_cnt[e2])
            if d in seen:
                continue
            seen.add(d)
            if kind == 'c':
                self.recs[e2][i2]['inc'] = True
            ndeps.append(d)
        self.dma_cnt[sem] += 1
        n = self.dma_cnt[sem]
        fn = lambda eng, o=out_ap, i=in_ap, kw=kw: eng.dma_start(out=o, in_=i, **kw)
        self.recs[qk].append(dict(fn=fn, waits=ndeps, inc=False, dma=sem, tag=tag))
        tok = ('d', sem, n)
        for b in reads:
            b.readers.append(tok)
        for b in writes:
            b.last_w = tok
            b.readers = []
        return tok

    def finalize(self, block, sems, dsems):
        pref = {}
        for ek, rs in self.recs.items():
            c = 0
            p = []
            for r in rs:
                if r['inc']:
                    c += 1
                p.append(c)
            pref[ek] = p

        def mk_body(ek, rs):
            def body(eng):
                watermark = {}
                for r in rs:
                    for (kind, e2, i2) in r['waits']:
                        if kind == 'c':
                            sem = sems[e2]
                            val = pref[e2][i2]
                            if val == 0:
                                continue
                        else:
                            sem = dsems[e2]
                            val = 16 * i2
                        key = (kind, e2)
                        if watermark.get(key, -1) >= val:
                            continue
                        eng.wait_ge(sem, val)
                        watermark[key] = val
                    inst = r['fn'](eng)
                    if r['dma'] is not None:
                        inst.then_inc(dsems[r['dma']], 16)
                    elif r['inc']:
                        inst.then_inc(sems[ek], 1)
            return body

        for ek in ENG_NAMES:
            getattr(block, ENG_NAMES[ek])(mk_body(ek, self.recs[ek]))


def build_nc():
    nc = bass.Bass("TRN2", target_bir_lowering=False)

    xT = nc.dram_tensor("xT", [D, SEQ], F32, kind="ExternalInput").ap()
    wq_r = nc.dram_tensor("wq_r", [48, 128, 2048], F32, kind="ExternalInput").ap()
    wp_r = nc.dram_tensor("wp_r", [16, 128, 2048], F32, kind="ExternalInput").ap()
    wf_r = nc.dram_tensor("wf_r", [16, 128, 2048], F32, kind="ExternalInput").ap()
    vin = {}
    for nm in ("qn", "pn", "nw", "ga", "be", "lw", "lb"):
        vin[nm] = nc.dram_tensor(nm + "_r", [128, FC], F32, kind="ExternalInput").ap()
    s_in = nc.dram_tensor("s_in", [1, H], F32, kind="ExternalInput").ap()
    iota_c = nc.dram_tensor("iota_c", [128, 1], F32, kind="ExternalInput").ap()
    iota_r = nc.dram_tensor("iota_r", [1, 128], F32, kind="ExternalInput").ap()
    diffp = nc.dram_tensor("diffp", [128, 128], F32, kind="ExternalInput").ap()
    causal = nc.dram_tensor("causal", [128, 128], F32, kind="ExternalInput").ap()
    ones128 = nc.dram_tensor("ones128", [128, 128], F32, kind="ExternalInput").ap()
    iden_in = nc.dram_tensor("iden_in", [128, 128], BF16, kind="ExternalInput").ap()
    onescol_bf = nc.dram_tensor("onescol_bf", [128, 1], BF16, kind="ExternalInput").ap()
    onescol_f = nc.dram_tensor("onescol_f", [128, 1], F32, kind="ExternalInput").ap()
    onesrow_f = nc.dram_tensor("onesrow_f", [1, 128], F32, kind="ExternalInput").ap()
    cvals_in = nc.dram_tensor("cvals_in", [128, 4], F32, kind="ExternalInput").ap()
    outT = nc.dram_tensor("outT", [D, SEQ], F32, kind="ExternalOutput").ap()
    odbg = (nc.dram_tensor("odbg", [D, SEQ], BF16, kind="ExternalOutput").ap()
            if DEBUG_O else None)
    qdbg = (nc.dram_tensor("qdbg", [2, 4, 128, FC * TILE], BF16,
                           kind="ExternalOutput").ap() if DEBUG_O else None)

    wq_t = nc.dram_tensor("wq_t", [MQK, 128, FC * 128], BF16, kind="Internal").ap()
    wv_t = nc.dram_tensor("wv_t", [8, 2, 128, 2048], BF16, kind="Internal").ap()
    wp_t = nc.dram_tensor("wp_t", [MP, 128, FC * 128], BF16, kind="Internal").ap()
    wf_t = nc.dram_tensor("wf_t", [MF, 128, FC * 128], BF16, kind="Internal").ap()
    bounce = nc.dram_tensor("bounce", [128], F32, kind="Internal").ap()

    A = nc.alloc_sbuf_tensor
    xA = A("xA", [128, FC * TILE], F32).ap()
    xB = A("xB", [128, FC * TILE], F32).ap()
    h = A("h", [128, FC * TILE], BF16).ap()
    qb = A("qb", [128, H * TILE], BF16).ap()
    kb = A("kb", [128, H * TILE], BF16).ap()
    vb = A("vb", [128, NCH * D], BF16).ap()
    pb = A("pb", [128, FC * TILE], BF16).ap()
    Sf = A("Sf", [128, H * HD], F32).ap()
    Sb = A("Sb", [128, H * HD], BF16).ap()
    wpool = A("wpool", [128, 2 * FC * 128], BF16).ap()
    vsl = A("vsl", [128, 8 * 256], BF16).ap()
    maskT = A("maskT", [128, H * 128], BF16).ap()
    ltrep = A("ltrep", [128, H * 128], BF16).ap()
    lrT = A("lrT", [128, H * 128], BF16).ap()
    o128 = A("o128", [128, 128], F32).ap()
    iden = A("iden", [128, 128], BF16).ap()
    dfp = A("dfp", [128, 128], F32).ap()
    cau = A("cau", [128, 128], F32).ap()
    ioc = A("ioc", [128, 1], F32).ap()
    ior = A("ior", [1, 128], F32).ap()
    ocb = A("ocb", [128, 1], BF16).ap()
    ocf = A("ocf", [128, 1], F32).ap()
    orf = A("orf", [1, 128], F32).ap()
    cst = A("cst", [128, 4], F32).ap()
    cols = {nm: A("c_" + nm, [128, FC], F32).ap()
            for nm in ("qn", "pn", "nw", "ga", "be", "lw", "lb",
                       "qnm", "g", "nw2", "gs", "b2")}
    srow = A("srow", [1, H], F32).ap()
    nw2b = A("nw2b", [128, FC], BF16).ap()
    lbbf = A("lbbf", [128, FC], BF16).ap()
    sbc = A("sbc", [128, H], F32).ap()
    snb = A("snb", [128, H], F32).ap()
    sm128 = A("sm128", [128, H], F32).ap()
    lamT_c = A("lamT_c", [128, H], F32).ap()
    lrev_c = A("lrev_c", [128, H], F32).ap()
    mcols = A("mcols", [128, 4], F32).ap()
    sfcol = A("sfcol", [128, 1], F32).ap()
    stat = A("stat", [128, 3 * FC], F32).ap()
    statm = A("statm", [128, FC], F32).ap()
    rows = A("rows", [1, 3 * TILE], F32).ap()
    tot = A("tot", [1, 8], F32).ap()
    bfp = A("bfp", [128, 4 * TILE], BF16).ap()
    t1s = A("t1s", [128, 2 * TILE], BF16).ap()
    qts = A("qts", [128, 2 * TILE], BF16).ap()
    kts = A("kts", [128, 2 * TILE], BF16).ap()
    scm = A("scm", [128, 2 * TILE], BF16).ap()
    brow = A("brow", [128, 3 * TILE], BF16).ap()
    rbf = A("rbf", [1, TILE], BF16).ap()

    P = nc.alloc_psum_tensor
    pA = P("pA", [128, 512], F32).ap()
    pB = P("pB", [128, 512], F32).ap()
    pC = P("pC", [128, 512], F32).ap()
    pD = P("pD", [128, 1024], BF16).ap()
    pE = P("pE", [128, 512], F32).ap()
    pF = P("pF", [128, 512], F32).ap()
    pS1 = P("pS1", [1, 512], F32).ap()
    pS2 = P("pS2", [1, 512], F32).ap()

    es = ExitStack()
    sems = {k: es.enter_context(nc.semaphore("sem_" + k)) for k in ENG_NAMES}
    dsems = {k: es.enter_context(nc.semaphore("dma_" + k)) for k in DMA_SEMS}

    pr = Prog(nc)

    bxA = [Buf(f"xA{i}") for i in range(FC)]
    bxB = [Buf(f"xB{i}") for i in range(FC)]
    bh = [Buf(f"h{i}") for i in range(FC)]
    bqb = [Buf(f"qb{i}") for i in range(H)]
    bkb = [Buf(f"kb{i}") for i in range(H)]
    bvb = [Buf(f"vb{i}") for i in range(NCH * 4)]
    bpb = [Buf(f"pb{i}") for i in range(FC)]
    bSf = [Buf(f"Sf{i}") for i in range(H)]
    bSb = [Buf(f"Sb{i}") for i in range(H)]
    bwp = [Buf("wp0"), Buf("wp1")]
    bvsl = [Buf(f"vsl{i}") for i in range(8)]
    bconsts = Buf("consts")
    bcols = Buf("cols")
    brows = [Buf(f"row{i}") for i in range(4)]
    btot = Buf("tot")
    bbfp = [Buf(f"bfp{i}") for i in range(4)]
    bt1 = [Buf("t1a"), Buf("t1b")]
    bqt = [Buf("qta"), Buf("qtb")]
    bkt = [Buf("kta"), Buf("ktb")]
    bscm = [Buf("scma"), Buf("scmb")]
    bbrow = [Buf("brw0"), Buf("brw1"), Buf("brw2")]
    brbf = Buf("rbf")
    bstat = Buf("stat")
    bpA = Buf("pA"); bpB = Buf("pB"); bpC = Buf("pC"); bpD = [Buf("pD0"), Buf("pD1")]
    bpE = Buf("pE"); bpF = Buf("pF"); bpS1 = Buf("pS1"); bpS2 = Buf("pS2")
    bwq_t = [Buf(f"wqt{i}") for i in range(MQK)]
    bwv_t = [Buf(f"wvt{i}") for i in range(16)]
    bwp_t = [Buf(f"wpt{i}") for i in range(MP)]
    bwf_t = [Buf(f"wft{i}") for i in range(MF)]
    bbounce = Buf("bounce")
    bout = Buf("outT")

    tload = []
    for (dst, src) in ((dfp, diffp), (cau, causal), (o128, ones128), (iden, iden_in),
                       (ioc, iota_c), (ior, iota_r), (ocb, onescol_bf), (ocf, onescol_f),
                       (orf, onesrow_f), (cst, cvals_in), (srow[:1, :], s_in)):
        tload.append(pr.dma('g', 'cl', dst, src, writes=[bconsts], tag='cload'))
    for nm in ("qn", "pn", "nw", "ga", "be", "lw", "lb"):
        tload.append(pr.dma('g', 'cl', cols[nm], vin[nm], writes=[bcols], tag='vload'))

    tok = pr.emit('t', lambda e: e.matmul(pA[:, 0:H], orf, srow[:1, :], start=True, stop=True),
                  reads=[bconsts], writes=[bpA], tag='sbcast')
    pr.emit('v', lambda e: e.tensor_copy(out=sbc, in_=pA[:, 0:H]), reads=[bpA], writes=[bconsts])
    pr.emit('v', lambda e: e.tensor_scalar_mul(out=snb, in0=sbc, scalar1=-1.0),
            reads=[bconsts], writes=[bconsts])
    pr.emit('v', lambda e: e.tensor_scalar_mul(out=sm128, in0=sbc, scalar1=-128.0),
            reads=[bconsts], writes=[bconsts])
    pr.emit('s', lambda e: e.activation(out=lamT_c, in_=sm128, func=AF.Exp),
            reads=[bconsts], writes=[bconsts])
    for hh in range(H):
        pr.emit('s', lambda e, hh=hh: e.activation(
            out=lrev_c[:, hh:hh + 1], in_=ioc, func=AF.Exp,
            scale=sbc[:, hh:hh + 1], bias=sm128[:, hh:hh + 1]),
            reads=[bconsts], writes=[bconsts])
    for hh in range(H):
        pr.emit('v', lambda e, hh=hh: e.tensor_scalar_mul(
            out=lrT[:, hh * 128:(hh + 1) * 128], in0=o128, scalar1=lrev_c[:, hh:hh + 1]),
            reads=[bconsts], writes=[bconsts])
    for hh in range(H):
        pr.emit('s', lambda e, hh=hh: e.activation(
            out=rows[:1, (hh % 3) * 512:(hh % 3) * 512 + 128],
            in_=ior, func=AF.Exp, scale=snb[0:1, hh:hh + 1]),
            reads=[bconsts], writes=[brows[hh % 3]], tag='ltrow')
        pr.emit('t', lambda e, hh=hh: e.matmul(
            pB[:, 0:128], orf, rows[:1, (hh % 3) * 512:(hh % 3) * 512 + 128], start=True, stop=True),
            reads=[brows[hh % 3]], writes=[bpB])
        pr.emit('v', lambda e, hh=hh: e.tensor_copy(
            out=ltrep[:, hh * 128:(hh + 1) * 128], in_=pB[:, 0:128]),
            reads=[bpB], writes=[bconsts])
    for hh in range(H):
        pr.emit('s', lambda e, hh=hh: e.activation(
            out=pE[:, 0:128],
            in_=dfp, func=AF.Exp, scale=snb[:, hh:hh + 1]),
            reads=[bconsts], writes=[bpE], tag='maskexp')
        pr.emit('v', lambda e, hh=hh: e.tensor_tensor(
            out=maskT[:, hh * 128:(hh + 1) * 128], in0=pE[:, 0:128], in1=cau, op=ALU.mult),
            reads=[bpE, bconsts], writes=[bconsts])

    stat_it = [0]

    def stat_pass(wr, nmg, col_off, do_max):
        for i in range(nmg // 2):
            hl = stat_it[0] % 2
            sl = xB[:, hl * 4096:(hl + 1) * 4096]
            buf = bxB[hl * 8:(hl + 1) * 8]
            qk = 'y' if i % 2 == 0 else 's'
            dt = pr.dma(qk, 'qb0' if hl == 0 else 'qb1',
                        sl.rearrange("p (a c) -> p a c", a=2),
                        wr[i * 2:(i + 1) * 2].rearrange("a p c -> p a c"),
                        writes=buf, tag='statload')
            pr.emit('v', lambda e, sl=sl, i=i, col_off=col_off: e.tensor_reduce(
                out=stat[:, col_off + i:col_off + i + 1], in_=sl,
                axis=AX.X, op=ALU.add, apply_absolute_value=True),
                reads=buf, writes=[bstat], also=[dt], tag='statred')
            if do_max:
                pr.emit('v', lambda e, sl=sl, i=i: e.tensor_reduce(
                    out=statm[:, i:i + 1], in_=sl,
                    axis=AX.X, op=ALU.max, apply_absolute_value=True),
                    reads=buf, writes=[bstat], also=[dt], tag='statmax')
            stat_it[0] += 1

    stat_pass(wq_r, 48, 0, False)
    stat_pass(wp_r, 16, 24, False)
    stat_pass(wf_r, 16, 32, True)
    stat_rng = {0: (0, 24), 1: (24, 32), 2: (32, 40)}
    for j, name in enumerate(("q", "p", "f")):
        lo, hi = stat_rng[j]
        pr.emit('v', lambda e, j=j, lo=lo, hi=hi: e.tensor_reduce(
            out=mcols[:, j:j + 1],
            in_=stat[:, lo:hi], axis=AX.X, op=ALU.add),
            reads=[bstat], writes=[bcols], tag='stot')
        pr.emit('t', lambda e, j=j: e.matmul(
            pA[0:1, j:j + 1],
            mcols[:, j:j + 1], ocf, start=True, stop=True),
            reads=[bcols, bconsts], writes=[bpA], tag='stotmm')
    pr.emit('v', lambda e: e.tensor_copy(out=tot[:1, 0:3], in_=pA[0:1, 0:3]),
            reads=[bpA], writes=[btot])
    pr.emit('v', lambda e: e.tensor_scalar(out=tot[:1, 3:4], in0=tot[:1, 0:1],
                                           scalar1=1.0 / (DQ * D), scalar2=1e-5,
                                           op0=ALU.mult, op1=ALU.max),
            reads=[btot], writes=[btot])
    pr.emit('v', lambda e: e.tensor_scalar(out=tot[:1, 4:5], in0=tot[:1, 1:2],
                                           scalar1=1.0 / (D * D), scalar2=1e-5,
                                           op0=ALU.mult, op1=ALU.max),
            reads=[btot], writes=[btot])
    pr.emit('v', lambda e: e.reciprocal(out=tot[:1, 5:6], in_=tot[:1, 3:4]),
            reads=[btot], writes=[btot])
    pr.emit('v', lambda e: e.reciprocal(out=tot[:1, 6:7], in_=tot[:1, 4:5]),
            reads=[btot], writes=[btot])
    for j, src in enumerate((5, 3, 6, 4)):
        pr.emit('t', lambda e, j=j, src=src: e.matmul(
            pA[:, 4 + j:5 + j], orf, tot[:1, src:src + 1], start=True, stop=True),
            reads=[btot, bconsts], writes=[bpA])
    pr.emit('v', lambda e: e.tensor_copy(out=mcols[:, 0:4], in_=pA[:, 4:8]),
            reads=[bpA], writes=[bcols])
    pr.emit('v', lambda e: e.tensor_reduce(out=statm[:, 0:1], in_=statm[:, 0:8],
                                           axis=AX.X, op=ALU.max),
            reads=[bstat], writes=[bstat])
    pr.dma('g', 'b1', bounce, statm[:, 0:1], reads=[bstat], writes=[bbounce], tag='bounce1')
    pr.dma('g', 'b2', rows[:1, 0:128],
           bounce.rearrange("(a b) -> a b", a=1), reads=[bbounce], writes=[brows[0]], tag='bounce2')
    pr.emit('v', lambda e: e.tensor_reduce(out=tot[:1, 7:8], in_=rows[:1, 0:128],
                                           axis=AX.X, op=ALU.max),
            reads=[brows[0]], writes=[btot])
    pr.emit('t', lambda e: e.matmul(pA[:, 8:9], orf, tot[:1, 7:8], start=True, stop=True),
            reads=[btot, bconsts], writes=[bpA])
    pr.emit('v', lambda e: e.tensor_copy(out=sfcol, in_=pA[:, 8:9]),
            reads=[bpA], writes=[bcols])

    pr.emit('v', lambda e: e.tensor_scalar_mul(out=cols["qnm"], in0=cols["qn"],
                                               scalar1=mcols[:, 1:2]),
            reads=[bcols], writes=[bcols])
    pr.emit('v', lambda e: e.tensor_tensor(out=cols["g"], in0=cols["nw"], in1=cols["pn"],
                                           op=ALU.mult),
            reads=[bcols], writes=[bcols])
    pr.emit('v', lambda e: e.tensor_scalar_mul(out=cols["g"], in0=cols["g"],
                                               scalar1=mcols[:, 3:4]),
            reads=[bcols], writes=[bcols])
    pr.emit('s', lambda e: e.activation(out=cols["nw2"], in_=cols["nw"], func=AF.Square),
            reads=[bcols], writes=[bcols])
    pr.emit('v', lambda e: e.tensor_copy(out=nw2b, in_=cols["nw2"]),
            reads=[bcols], writes=[bcols])
    pr.emit('v', lambda e: e.tensor_copy(out=lbbf, in_=cols["lb"]),
            reads=[bcols], writes=[bcols])
    pr.emit('v', lambda e: e.tensor_scalar_mul(out=cols["gs"], in0=cols["ga"],
                                               scalar1=sfcol[:, 0:1]),
            reads=[bcols], writes=[bcols])

    def quant_tile(src_ap, src_bufs, dst_ap, dst_bufs, scale_col, extra=()):
        src_bufs = list(src_bufs)
        dst_bufs = list(dst_bufs)
        pr.emit('s', lambda e: e.activation(out=src_ap, in_=src_ap, func=AF.Identity,
                                            scale=scale_col, bias=cst[:, 0:1]),
                reads=src_bufs + [bcols, bconsts], writes=src_bufs, also=list(extra),
                tag='q1')
        pr.emit('s', lambda e: e.activation(out=src_ap, in_=src_ap, func=AF.Identity,
                                            bias=cst[:, 1:2]),
                reads=src_bufs + [bconsts], writes=src_bufs, tag='q2')
        pr.emit('v', lambda e: e.tensor_scalar_max(out=src_ap, in0=src_ap, scalar1=-1.0),
                reads=src_bufs, writes=src_bufs, tag='q3')
        pr.emit('v', lambda e: e.tensor_scalar_min(out=dst_ap, in0=src_ap, scalar1=1.0),
                reads=src_bufs, writes=dst_bufs, tag='q4')

    def quant2(i, wr, mg0, scale_col, kind):
        hl = i % 2
        sap = xB[:, hl * 4096:(hl + 1) * 4096]
        sbufs = bxB[hl * 8:(hl + 1) * 8]
        sqsem = 'qb0' if hl == 0 else 'qb1'
        dap = (kb if hl == 0 else vb)[:, 0:4096]
        dbufs = (bkb if hl == 0 else bvb)[0:8]
        wsem = f'qi{i}'
        wq_queue = 'y' if i % 2 == 0 else 's'
        dt = pr.dma(wq_queue, sqsem,
                    sap.rearrange("p (a c) -> p a c", a=2),
                    wr[mg0:mg0 + 2].rearrange("a p c -> p a c"),
                    writes=sbufs, tag=f'qin{kind}{mg0}')
        if kind == 'f':
            pr.emit('s', lambda e: e.sign(out=dap, in_=sap),
                    reads=list(sbufs), writes=list(dbufs), also=[dt], tag='fsign')
        else:
            pr.emit('s', lambda e: e.activation(out=sap, in_=sap, func=AF.Identity,
                                                scale=scale_col, bias=cst[:, 0:1]),
                    reads=list(sbufs) + [bcols, bconsts], writes=list(sbufs),
                    also=[dt], tag='q1')
            pr.emit('s', lambda e: e.activation(out=sap, in_=sap, func=AF.Identity,
                                                bias=cst[:, 1:2]),
                    reads=list(sbufs) + [bconsts], writes=list(sbufs), tag='q2')
            pr.emit('v', lambda e: e.tensor_scalar_max(out=sap, in0=sap, scalar1=-1.0),
                    reads=list(sbufs), writes=list(sbufs), tag='q3')
            pr.emit('v', lambda e: e.tensor_scalar_min(out=dap, in0=sap, scalar1=1.0),
                    reads=list(sbufs), writes=list(dbufs), tag='q4')
        for j in range(2):
            mg = mg0 + j
            dsl = dap[:, j * 2048:(j + 1) * 2048]
            if kind == 'q':
                pr.dma(wq_queue, wsem, wq_t[mg], dsl, reads=list(dbufs),
                       writes=[bwq_t[mg]], tag=f'qout{mg}')
            elif kind == 'v':
                vm = mg - MQK
                heg = vm // 2
                eo = (vm % 2) * 128
                for kc in range(FC):
                    pr.dma(wq_queue, wsem,
                           wv_t[heg, kc // 8, :, (kc % 8) * 256 + eo:(kc % 8) * 256 + eo + 128],
                           dsl[:, kc * 128:(kc + 1) * 128],
                           reads=list(dbufs), writes=[bwv_t[heg * 2 + kc // 8]],
                           tag=f'vout{mg}_{kc}')
            elif kind == 'p':
                pr.dma(wq_queue, wsem, wp_t[mg], dsl, reads=list(dbufs),
                       writes=[bwp_t[mg]], tag=f'pout{mg}')
            else:
                pr.dma(wq_queue, wsem, wf_t[mg], dsl, reads=list(dbufs),
                       writes=[bwf_t[mg]], tag=f'fout{mg}')

    quant_sched = []
    qit = [0]

    def qthunk(wr, mg0, sc, kind):
        def run():
            quant2(qit[0], wr, mg0, sc, kind)
            qit[0] += 1
        return run

    qthunks = {'q': [], 'k': [], 'v': [], 'p': [], 'f': []}
    for mg0 in range(0, 16, 2):
        qthunks['q'].append(qthunk(wq_r, mg0, mcols[:, 0:1], 'q'))
    for mg0 in range(16, MQK, 2):
        qthunks['k'].append(qthunk(wq_r, mg0, mcols[:, 0:1], 'q'))
    for mg0 in range(0, 16, 2):
        qthunks['v'].append(qthunk(wq_r, MQK + mg0, mcols[:, 0:1], 'v'))
    for mg0 in range(0, MP, 2):
        qthunks['p'].append(qthunk(wp_r, mg0, mcols[:, 2:3], 'p'))
    for mg0 in range(0, MF, 2):
        qthunks['f'].append(qthunk(wf_r, mg0, None, 'f'))

    pr.pe_strict = STRICT_MAIN
    x_of = lambda t: (xA, bxA) if t % 2 == 0 else (xB, bxB)

    def emit_xload(t):
        xs, bxs = x_of(t)
        for fc in range(FC):
            pr.dma('g', 'xA' if t % 2 == 0 else 'xB',
                   xs[:, fc * TILE:(fc + 1) * TILE],
                   xT[fc * 128:(fc + 1) * 128, t * TILE:(t + 1) * TILE],
                   writes=[bxs[fc]], tag=f'xload{t}_{fc}')

    def emit_ln1_stats(t):
        xs, bxs = x_of(t)
        def ln1_stat(j):
            sl_ = bfp[:, (j % 2) * TILE:(j % 2) * TILE + TILE]
            sb_ = bbfp[j % 2]
            sq_ = bfp[:, (2 + j % 2) * TILE:(2 + j % 2) * TILE + TILE]
            sqb_ = bbfp[2 + j % 2]
            pr.emit('t', lambda e, sl_=sl_, j=j: e.matmul(
                pS1[:1, :], ocb, sl_, start=(j == 0), stop=(j == FC - 1)),
                reads=[sb_, bconsts], writes=[bpS1], tag='ln1sum')
            pr.emit('t', lambda e, sq_=sq_, j=j: e.matmul(
                pS2[:1, :], ocb, sq_, start=(j == 0), stop=(j == FC - 1)),
                reads=[sqb_, bconsts], writes=[bpS2], tag='ln1sum2')

        for fc in range(FC):
            sl = bfp[:, (fc % 2) * TILE:(fc % 2) * TILE + TILE]
            sbuf = bbfp[fc % 2]
            sq = bfp[:, (2 + fc % 2) * TILE:(2 + fc % 2) * TILE + TILE]
            sqb = bbfp[2 + fc % 2]
            pr.emit('g', lambda e, sl=sl, xs=xs, fc=fc: e.tensor_copy(
                out=sl, in_=xs[:, fc * TILE:(fc + 1) * TILE]),
                reads=[bxs[fc]], writes=[sbuf], tag='xbcast')
            pr.emit('s', lambda e, sq=sq, xs=xs, fc=fc: e.activation(
                out=sq, in_=xs[:, fc * TILE:(fc + 1) * TILE], func=AF.Square),
                reads=[bxs[fc]], writes=[sqb], tag='xsq')
            if fc > 0:
                ln1_stat(fc - 1)
        ln1_stat(FC - 1)
        mu = rows[:1, 0:TILE]
        m2 = rows[:1, TILE:2 * TILE]
        beta = rows[:1, 2 * TILE:3 * TILE]
        k1 = (1.0 + EPS32) / D
        c0 = 1e-5 * EPS32
        pr.emit('v', lambda e: e.tensor_scalar_mul(out=mu, in0=pS1[:1, :], scalar1=1.0 / D),
                reads=[bpS1], writes=[brows[0]], tag='mu')
        pr.emit('g', lambda e: e.tensor_tensor(out=m2, in0=mu, in1=mu, op=ALU.mult),
                reads=[brows[0]], writes=[brows[1]])
        pr.emit('g', lambda e: e.tensor_scalar(out=m2, in0=m2, scalar1=1.0 + EPS32,
                                               scalar2=-c0, op0=ALU.mult, op1=ALU.add),
                reads=[brows[1]], writes=[brows[1]])
        pr.emit('v', lambda e: e.scalar_tensor_tensor(out=beta, in0=pS2[:1, :], scalar=k1,
                                                      in1=m2, op0=ALU.mult, op1=ALU.subtract),
                reads=[bpS2, brows[1]], writes=[brows[2]])
        pr.emit('s', lambda e: e.activation(out=beta, in_=beta, func=AF.Ln),
                reads=[brows[2]], writes=[brows[2]])
        pr.emit('s', lambda e: e.activation(out=beta, in_=beta, func=AF.Exp,
                                            scale=cst[0:1, 2:3]),
                reads=[brows[2]], writes=[brows[2]])

    def emit_ln1_apply(t):
        xs, bxs = x_of(t)
        mu = rows[:1, 0:TILE]
        beta = rows[:1, 2 * TILE:3 * TILE]
        pr.emit('t', lambda e: e.matmul(pC, orf, mu, start=True, stop=True),
                reads=[brows[0], bconsts], writes=[bpC], tag='mub')
        pr.emit('t', lambda e: e.matmul(pE, orf, beta, start=True, stop=True),
                reads=[brows[2], bconsts], writes=[bpE], tag='betab')
        pr.emit('v', lambda e: e.tensor_copy(out=brow[:, 0:TILE], in_=pE),
                reads=[bpE], writes=[bbrow[0]], tag='betabf')
        for fc in range(FC):
            tsl = t1s[:, (fc % 2) * TILE:(fc % 2) * TILE + TILE]
            tb = bt1[fc % 2]
            pr.emit('v', lambda e, tsl=tsl, xs=xs, fc=fc: e.tensor_tensor(
                out=tsl, in0=xs[:, fc * TILE:(fc + 1) * TILE], in1=pC, op=ALU.subtract),
                reads=[bxs[fc], bpC], writes=[tb], tag='t1')
            pr.emit('v', lambda e, tsl=tsl, fc=fc: e.scalar_tensor_tensor(
                out=h[:, fc * TILE:(fc + 1) * TILE], in0=tsl,
                scalar=cols["qnm"][:, fc:fc + 1],
                in1=brow[:, 0:TILE], op0=ALU.mult, op1=ALU.mult),
                reads=[tb, bbrow[0], bcols], writes=[bh[fc]], tag='happ')

    emit_xload(0)
    emit_ln1_stats(0)
    emit_ln1_apply(0)

    for t in range(NT):
        xs, bxs = x_of(t)
        t0 = t * TILE

        for g in range(NEG):
            if t == 0:
                qthunks['q'][2 * g](); qthunks['q'][2 * g + 1]()
                qthunks['k'][2 * g](); qthunks['k'][2 * g + 1]()
                qthunks['v'][2 * g](); qthunks['v'][2 * g + 1]()
                qthunks['p'][g](); qthunks['f'][g]()
            for j in range(8):
                mg = (4 * g + j) if j < 4 else (16 + 4 * g + j - 4)
                slot = j % 2
                wsl = wpool[:, slot * 2048:(slot + 1) * 2048]
                psum, bps = (pA, bpA) if slot == 0 else (pB, bpB)
                pr.dma('y', 'wpA' if slot == 0 else 'wpB', wsl, wq_t[mg],
                       reads=[bwq_t[mg]], writes=[bwp[slot]], tag=f'wq{t}_{mg}')
                for kc in range(FC):
                    pr.emit('t', lambda e, wsl=wsl, kc=kc, psum=psum: e.matmul(
                        psum, wsl[:, kc * 128:(kc + 1) * 128],
                        h[:, kc * TILE:(kc + 1) * TILE],
                        start=(kc == 0), stop=(kc == FC - 1)),
                        reads=[bwp[slot], bh[kc]], writes=[bps], tag=f'qkmm{mg}')
                if mg < 16:
                    pr.emit('s', lambda e, mg=mg, psum=psum: e.copy(
                        out=qb[:, mg * TILE:(mg + 1) * TILE], in_=psum),
                        reads=[bps], writes=[bqb[mg]], tag='qevac')
                else:
                    hh = mg - 16
                    pr.emit('s', lambda e, hh=hh, psum=psum: e.copy(
                        out=kb[:, hh * TILE:(hh + 1) * TILE], in_=psum),
                        reads=[bps], writes=[bkb[hh]], tag='kevac')
            if g > 0:
                for hh in range((g - 1) * 4, g * 4):
                    sq = bfp[:, (hh % 4) * TILE:(hh % 4) * TILE + TILE]
                    sqb = bbfp[hh % 4]
                    pr.emit('t', lambda e, sq=sq, hh=hh: e.matmul(
                        pS1[:1, :], ocb, sq, start=(hh == 0), stop=(hh == FC - 1)),
                        reads=[sqb, bconsts], writes=[bpS1], tag='osumA')
                    pr.emit('t', lambda e, sq=sq, hh=hh: e.matmul(
                        pS2[:1, :], nw2b[:, hh:hh + 1], sq,
                        start=(hh == 0), stop=(hh == FC - 1)),
                        reads=[sqb, bcols, bconsts], writes=[bpS2], tag='osumB')

            for heg in (2 * g, 2 * g + 1):
                wslot = heg % 2
                wsl = wpool[:, wslot * 2048:(wslot + 1) * 2048]
                pr.dma('y', 'vs0', vsl, wv_t[heg, 0],
                       reads=[bwv_t[heg * 2]], writes=[bvsl[0]], tag=f'wvlo{t}_{heg}')
                pr.dma('y', 'wpA' if wslot == 0 else 'wpB', wsl, wv_t[heg, 1],
                       reads=[bwv_t[heg * 2 + 1]], writes=[bwp[wslot]],
                       tag=f'wvhi{t}_{heg}')
                for tc in range(NCH):
                    psum, bps = ((pA, bpA), (pB, bpB))[tc % 2]
                    po = (tc // 2) * 256
                    for kc in range(FC):
                        rsl = (vsl[:, (kc % 8) * 256:(kc % 8) * 256 + 256] if kc < 8
                               else wsl[:, (kc - 8) * 256:(kc - 8) * 256 + 256])
                        rb = bvsl[0] if kc < 8 else bwp[wslot]
                        pr.emit('t', lambda e, kc=kc, tc=tc, psum=psum, po=po,
                                rsl=rsl: e.matmul(
                            psum[:, po:po + 256],
                            h[:, kc * TILE + tc * 128: kc * TILE + tc * 128 + 128],
                            rsl, start=(kc == 0), stop=(kc == FC - 1)),
                            reads=[rb, bh[kc]], writes=[bps], tag=f'vmm{heg}_{tc}')
                    pr.emit('s', lambda e, tc=tc, heg=heg, psum=psum, po=po: e.copy(
                        out=vb[:, tc * D + heg * 256: tc * D + heg * 256 + 256],
                        in_=psum[:, po:po + 256]),
                        reads=[bps], writes=[bvb[tc * 4 + heg // 2]], tag='vevac')

            hs = range(g * 4, (g + 1) * 4)
            for c in range(NCH):
                cg = t * NCH + c
                for i, hh in enumerate(hs):
                    pr.emit('t', lambda e, hh=hh, i=i, c=c: e.matmul(
                        pC[:, i * 128:(i + 1) * 128],
                        kb[:, hh * TILE + c * 128: hh * TILE + c * 128 + 128],
                        qb[:, hh * TILE + c * 128: hh * TILE + c * 128 + 128],
                        start=True, stop=True),
                        reads=[bkb[hh], bqb[hh]], writes=[bpC], tag=f'score{c}_{hh}')
                sc = scm[:, (c % 2) * TILE:(c % 2) * TILE + TILE]
                scb = bscm[c % 2]
                pr.emit('v', lambda e, sc=sc, g=g: e.tensor_tensor(
                    out=sc, in0=pC, in1=maskT[:, g * 512:(g + 1) * 512], op=ALU.mult),
                    reads=[bpC, bconsts], writes=[scb], tag='scevac')
                qt = qts[:, (c % 2) * TILE:(c % 2) * TILE + TILE]
                qtb = bqt[c % 2]
                pr.emit('v', lambda e, qt=qt, g=g, c=c: e.tensor_tensor(
                    out=qt.rearrange("p (r n) -> p r n", r=4),
                    in0=_ap3(qb, g, c),
                    in1=_ap3c(ltrep, g), op=ALU.mult),
                    reads=[bqb[hs[0]], bqb[hs[1]], bqb[hs[2]], bqb[hs[3]], bconsts],
                    writes=[qtb], tag='qtilde')
                pdo = (c % 2) * 512
                for i, hh in enumerate(hs):
                    pr.emit('t', lambda e, hh=hh, i=i, c=c, pdo=pdo: e.transpose(
                        pD[:, pdo + i * 128: pdo + (i + 1) * 128],
                        kb[:, hh * TILE + c * 128: hh * TILE + c * 128 + 128], iden),
                        reads=[bkb[hh], bconsts], writes=[bpD[c % 2]], tag='ktr', tr=True)
                kt = kts[:, (c % 2) * TILE:(c % 2) * TILE + TILE]
                ktb = bkt[c % 2]
                pr.emit('v', lambda e, kt=kt, g=g, pdo=pdo, c=c: e.tensor_tensor(
                    out=kt, in0=pD[:, pdo:pdo + 512], in1=lrT[:, g * 512:(g + 1) * 512],
                    op=ALU.mult),
                    reads=[bpD[c % 2], bconsts], writes=[ktb], tag='ktevac')
                for i, hh in enumerate(hs):
                    pr.emit('t', lambda e, hh=hh, i=i, c=c, sc=sc: e.matmul(
                        pE[:, i * 128:(i + 1) * 128],
                        vb[:, c * D + hh * 128: c * D + hh * 128 + 128],
                        sc[:, i * 128:(i + 1) * 128], start=True, stop=(cg == 0)),
                        reads=[bvb[c * 4 + hh // 4], scb], writes=[bpE],
                        tag=f'intra{c}_{hh}')
                    if cg > 0:
                        pr.emit('t', lambda e, hh=hh, i=i, qt=qt: e.matmul(
                            pE[:, i * 128:(i + 1) * 128],
                            Sb[:, hh * 128:(hh + 1) * 128],
                            qt[:, i * 128:(i + 1) * 128], start=False, stop=True),
                            reads=[bSb[hh], qtb], writes=[bpE], tag=f'cross{c}_{hh}')
                pr.emit('v', lambda e, g=g, c=c: e.tensor_copy(
                    out=_ap3o(pb, g, c), in_=pE.rearrange("p (r n) -> p r n", r=4)),
                    reads=[bpE], writes=[bpb[g * 4], bpb[g * 4 + 1], bpb[g * 4 + 2],
                                         bpb[g * 4 + 3]], tag='oevac')
                for i, hh in enumerate(hs):
                    pr.emit('t', lambda e, hh=hh, i=i, c=c, kt=kt: e.matmul(
                        pF[:, i * 128:(i + 1) * 128],
                        kt[:, i * 128:(i + 1) * 128],
                        vb[:, c * D + hh * 128: c * D + hh * 128 + 128],
                        start=True, stop=True),
                        reads=[ktb, bvb[c * 4 + hh // 4]], writes=[bpF], tag=f'kv{c}_{hh}')
                for i, hh in enumerate(hs):
                    if cg == 0:
                        pr.emit('v', lambda e, hh=hh, i=i: e.tensor_copy(
                            out=Sf[:, hh * 128:(hh + 1) * 128],
                            in_=pF[:, i * 128:(i + 1) * 128]),
                            reads=[bpF], writes=[bSf[hh]], tag='Sinit')
                    else:
                        pr.emit('v', lambda e, hh=hh, i=i: e.scalar_tensor_tensor(
                            out=Sf[:, hh * 128:(hh + 1) * 128],
                            in0=Sf[:, hh * 128:(hh + 1) * 128],
                            scalar=lamT_c[:, hh:hh + 1],
                            in1=pF[:, i * 128:(i + 1) * 128],
                            op0=ALU.mult, op1=ALU.add),
                            reads=[bSf[hh], bpF, bconsts], writes=[bSf[hh]], tag='Supd')
                pr.emit('s', lambda e, g=g: e.copy(
                    out=Sb[:, g * 512:(g + 1) * 512], in_=Sf[:, g * 512:(g + 1) * 512]),
                    reads=[bSf[hs[0]], bSf[hs[1]], bSf[hs[2]], bSf[hs[3]]],
                    writes=[bSb[hs[0]], bSb[hs[1]], bSb[hs[2]], bSb[hs[3]]], tag='Sbcopy')

            for hh in hs:
                sq = bfp[:, (hh % 4) * TILE:(hh % 4) * TILE + TILE]
                sqb = bbfp[hh % 4]
                pr.emit('s', lambda e, sq=sq, hh=hh: e.activation(
                    out=sq, in_=pb[:, hh * TILE:(hh + 1) * TILE], func=AF.Square),
                    reads=[bpb[hh]], writes=[sqb], tag='psq')

        if DEBUG_O:
            for fc in range(FC):
                pr.dma('g', 'qs', odbg[fc * 128:(fc + 1) * 128, t0:t0 + TILE],
                       pb[:, fc * TILE:(fc + 1) * TILE],
                       reads=[bpb[fc]], writes=[bout], tag=f'odbg{t}_{fc}')
            if t < 2:
                pr.dma('g', 'qs', qdbg[t, 0], qb, reads=list(bqb), writes=[bout], tag=f'qdbg{t}')
                pr.dma('g', 'qs', qdbg[t, 1], kb, reads=list(bkb), writes=[bout], tag=f'kdbg{t}')
                pr.dma('g', 'qs', qdbg[t, 2], vb, reads=list(bvb), writes=[bout], tag=f'vdbg{t}')
                pr.dma('g', 'qs', qdbg[t, 3], h, reads=list(bh), writes=[bout], tag=f'hdbg{t}')

        if t == 0:
            for j in range(4, 8):
                qthunks['p'][j]()
        for hh in range(12, 16):
            sq = bfp[:, (hh % 4) * TILE:(hh % 4) * TILE + TILE]
            sqb = bbfp[hh % 4]
            pr.emit('t', lambda e, sq=sq, hh=hh: e.matmul(
                pS1[:1, :], ocb, sq, start=(hh == 0), stop=(hh == FC - 1)),
                reads=[sqb, bconsts], writes=[bpS1], tag='osumA')
            pr.emit('t', lambda e, sq=sq, hh=hh: e.matmul(
                pS2[:1, :], nw2b[:, hh:hh + 1], sq,
                start=(hh == 0), stop=(hh == FC - 1)),
                reads=[sqb, bcols, bconsts], writes=[bpS2], tag='osumB')

        eA = 1e-6 * 128.0
        al = rows[:1, 0:TILE]
        tmp = rows[:1, TILE:2 * TILE]

        pr.emit('v', lambda e: e.tensor_scalar(out=tmp, in0=pS1[:1, :],
                                               scalar1=EPS32 / D,
                                               scalar2=EPS32 * eA, op0=ALU.mult,
                                               op1=ALU.add),
                reads=[bpS1], writes=[brows[1]], tag='alp1')
        pr.emit('v', lambda e: e.scalar_tensor_tensor(out=al, in0=pS2[:1, :],
                                                      scalar=1.0 / D, in1=tmp,
                                                      op0=ALU.mult, op1=ALU.add),
                reads=[bpS2, brows[1]], writes=[brows[0]])
        pr.emit('s', lambda e: e.activation(out=al, in_=al, func=AF.Ln),
                reads=[brows[0]], writes=[brows[0]])
        pr.emit('s', lambda e: e.activation(out=al, in_=al, func=AF.Exp,
                                            scale=cst[0:1, 2:3]),
                reads=[brows[0]], writes=[brows[0]])

        def emit_alpha_bcast():
            pr.emit('t', lambda e: e.matmul(pC, orf, al, start=True, stop=True),
                    reads=[brows[0], bconsts], writes=[bpC], tag='alb')
            pr.emit('v', lambda e: e.tensor_copy(out=brow[:, 2 * TILE:3 * TILE], in_=pC),
                    reads=[bpC], writes=[bbrow[2]], tag='albf')
        for fc in range(FC):
            pr.emit('v', lambda e, fc=fc: e.tensor_scalar_mul(
                out=h[:, fc * TILE:(fc + 1) * TILE],
                in0=pb[:, fc * TILE:(fc + 1) * TILE],
                scalar1=cols["g"][:, fc:fc + 1]),
                reads=[bpb[fc], bcols], writes=[bh[fc]], tag='zbuild')

        def emit_ln2_stat(j):
            sl_ = bfp[:, (j % 2) * TILE:(j % 2) * TILE + TILE]
            sb_ = bbfp[j % 2]
            sq_ = bfp[:, (2 + j % 2) * TILE:(2 + j % 2) * TILE + TILE]
            sqb_ = bbfp[2 + j % 2]
            pr.emit('t', lambda e, sl_=sl_, j=j: e.matmul(
                pS1[:1, :], ocb, sl_, start=(j == 0), stop=(j == FC - 1)),
                reads=[sb_, bconsts], writes=[bpS1], tag='ln2sum')
            pr.emit('t', lambda e, sq_=sq_, j=j: e.matmul(
                pS2[:1, :], ocb, sq_, start=(j == 0), stop=(j == FC - 1)),
                reads=[sqb_, bconsts], writes=[bpS2], tag='ln2sum2')

        if t + 1 < NT and t > 0:
            emit_xload(t + 1)
        for mg in range(MP):
            slot = mg % 2
            wsl = wpool[:, slot * 2048:(slot + 1) * 2048]
            psum, bps = (pA, bpA) if slot == 0 else (pB, bpB)
            pr.dma('y', 'wpA' if slot == 0 else 'wpB', wsl, wp_t[mg],
                   reads=[bwp_t[mg]], writes=[bwp[slot]], tag=f'wp{t}_{mg}')
            for kc in range(FC):
                pr.emit('t', lambda e, wsl=wsl, kc=kc, psum=psum: e.matmul(
                    psum, wsl[:, kc * 128:(kc + 1) * 128],
                    h[:, kc * TILE:(kc + 1) * TILE],
                    start=(kc == 0), stop=(kc == FC - 1)),
                    reads=[bwp[slot], bh[kc]], writes=[bps], tag=f'pmm{mg}')
            if mg == 0:
                emit_alpha_bcast()

            def proj_evac(j, psum_, bps_):
                pr.emit('v', lambda e, j=j, psum_=psum_: e.tensor_tensor(
                    out=t1s[:, (j % 2) * TILE:(j % 2) * TILE + TILE], in0=psum_,
                    in1=brow[:, 2 * TILE:3 * TILE], op=ALU.mult),
                    reads=[bps_, bbrow[2]], writes=[bt1[j % 2]], tag='res1a')
                pr.emit('v', lambda e, j=j, xs=xs: e.tensor_tensor(
                    out=xs[:, j * TILE:(j + 1) * TILE],
                    in0=t1s[:, (j % 2) * TILE:(j % 2) * TILE + TILE],
                    in1=xs[:, j * TILE:(j + 1) * TILE], op=ALU.add),
                    reads=[bt1[j % 2], bxs[j]], writes=[bxs[j]], tag='res1')
            proj_evac(mg, psum, bps)
            sl = bfp[:, (mg % 2) * TILE:(mg % 2) * TILE + TILE]
            sbuf = bbfp[mg % 2]
            sq = bfp[:, (2 + mg % 2) * TILE:(2 + mg % 2) * TILE + TILE]
            sqb = bbfp[2 + mg % 2]
            pr.emit('g', lambda e, sl=sl, xs=xs, mg=mg: e.tensor_copy(
                out=sl, in_=xs[:, mg * TILE:(mg + 1) * TILE]),
                reads=[bxs[mg]], writes=[sbuf], tag='x1b')
            pr.emit('s', lambda e, sq=sq, xs=xs, mg=mg: e.activation(
                out=sq, in_=xs[:, mg * TILE:(mg + 1) * TILE], func=AF.Square),
                reads=[bxs[mg]], writes=[sqb], tag='x1sq')
            if mg > 0:
                emit_ln2_stat(mg - 1)
        emit_ln2_stat(FC - 1)

        mu2 = rows[:1, 0:TILE]
        m22 = rows[:1, TILE:2 * TILE]
        a2 = rows[:1, 2 * TILE:3 * TILE]
        k2 = (1.0 + 1e-5) / D
        c2 = 1e-5 * 1e-5
        pr.emit('v', lambda e: e.tensor_scalar_mul(out=mu2, in0=pS1[:1, :], scalar1=1.0 / D),
                reads=[bpS1], writes=[brows[0]], tag='mu2')
        pr.emit('g', lambda e: e.tensor_tensor(out=m22, in0=mu2, in1=mu2, op=ALU.mult),
                reads=[brows[0]], writes=[brows[1]])
        pr.emit('g', lambda e: e.tensor_scalar(out=m22, in0=m22, scalar1=1.0 + 1e-5,
                                               scalar2=-c2, op0=ALU.mult, op1=ALU.add),
                reads=[brows[1]], writes=[brows[1]])
        pr.emit('v', lambda e: e.scalar_tensor_tensor(out=a2, in0=pS2[:1, :], scalar=k2,
                                                      in1=m22, op0=ALU.mult,
                                                      op1=ALU.subtract),
                reads=[bpS2, brows[1]], writes=[brows[2]])
        pr.emit('s', lambda e: e.activation(out=a2, in_=a2, func=AF.Ln),
                reads=[brows[2]], writes=[brows[2]])
        pr.emit('s', lambda e: e.activation(out=a2, in_=a2, func=AF.Exp,
                                            scale=cst[0:1, 2:3]),
                reads=[brows[2]], writes=[brows[2]])
        pr.emit('t', lambda e: e.matmul(pC, orf, mu2, start=True, stop=True),
                reads=[brows[0], bconsts], writes=[bpC], tag='mu2b')
        pr.emit('t', lambda e: e.matmul(pE, orf, a2, start=True, stop=True),
                reads=[brows[2], bconsts], writes=[bpE], tag='a2b')
        pr.emit('v', lambda e: e.tensor_copy(out=brow[:, TILE:2 * TILE], in_=pE),
                reads=[bpE], writes=[bbrow[1]], tag='a2bf')
        for fc in range(FC):
            tsl = t1s[:, (fc % 2) * TILE:(fc % 2) * TILE + TILE]
            tb = bt1[fc % 2]
            pr.emit('v', lambda e, tsl=tsl, xs=xs, fc=fc: e.tensor_tensor(
                out=tsl, in0=xs[:, fc * TILE:(fc + 1) * TILE], in1=pC, op=ALU.subtract),
                reads=[bxs[fc], bpC], writes=[tb], tag='t3')
            pr.emit('v', lambda e, tsl=tsl, fc=fc: e.tensor_scalar_mul(
                out=qb[:, fc * TILE:(fc + 1) * TILE], in0=tsl,
                scalar1=cols["lw"][:, fc:fc + 1]),
                reads=[tb, bcols], writes=[bqb[fc]], tag='xnapp')

        if t == 0:
            for j in range(4, 8):
                qthunks['f'][j]()
            emit_xload(1)
        if t + 1 < NT:
            emit_ln1_stats(t + 1)

        for mg in range(MF):
            slot = mg % 2
            wsl = wpool[:, slot * 2048:(slot + 1) * 2048]
            psum, bps = (pA, bpA) if slot == 0 else (pB, bpB)
            pr.dma('y', 'wpA' if slot == 0 else 'wpB', wsl, wf_t[mg],
                   reads=[bwf_t[mg]], writes=[bwp[slot]], tag=f'wf{t}_{mg}')
            for kc in range(FC):
                pr.emit('t', lambda e, wsl=wsl, kc=kc, psum=psum: e.matmul(
                    psum, wsl[:, kc * 128:(kc + 1) * 128],
                    qb[:, kc * TILE:(kc + 1) * TILE],
                    start=(kc == 0), stop=(kc == FC - 1)),
                    reads=[bwp[slot], bqb[kc]], writes=[bps], tag=f'fmm{mg}')
            if t == 0:
                pr.emit('t', lambda e, wsl=wsl: e.matmul(
                    pD.bitcast(F32)[:, 0:1] if False else pF[:, 0:1],
                    wsl[:, 0:128], lbbf[:, 0:1], start=True, stop=False),
                    reads=[bwp[slot], bcols], writes=[bpF], tag='lnbp')
                for kc in range(1, FC):
                    pr.emit('t', lambda e, wsl=wsl, kc=kc: e.matmul(
                        pF[:, 0:1], wsl[:, kc * 128:(kc + 1) * 128],
                        lbbf[:, kc:kc + 1], start=False, stop=(kc == FC - 1)),
                        reads=[bwp[slot], bcols], writes=[bpF], tag='lnbp')
                pr.emit('v', lambda e, mg=mg: e.tensor_scalar_mul(
                    out=cols["b2"][:, mg:mg + 1], in0=pF[:, 0:1],
                    scalar1=cols["gs"][:, mg:mg + 1]),
                    reads=[bpF, bcols], writes=[bcols], tag='b2col')
                pr.emit('v', lambda e, mg=mg: e.tensor_scalar_add(
                    out=cols["b2"][:, mg:mg + 1], in0=cols["b2"][:, mg:mg + 1],
                    scalar1=cols["be"][:, mg:mg + 1]),
                    reads=[bcols], writes=[bcols], tag='b2col2')
            pr.emit('v', lambda e, mg=mg, psum=psum: e.scalar_tensor_tensor(
                out=t1s[:, (mg % 2) * TILE:(mg % 2) * TILE + TILE], in0=psum,
                scalar=cols["gs"][:, mg:mg + 1],
                in1=brow[:, TILE:2 * TILE], op0=ALU.mult, op1=ALU.mult),
                reads=[bps, bbrow[1], bcols], writes=[bt1[mg % 2]], tag='fevA')
            pr.emit('v', lambda e, mg=mg, xs=xs: e.scalar_tensor_tensor(
                out=xs[:, mg * TILE:(mg + 1) * TILE],
                in0=t1s[:, (mg % 2) * TILE:(mg % 2) * TILE + TILE],
                scalar=cols["b2"][:, mg:mg + 1],
                in1=xs[:, mg * TILE:(mg + 1) * TILE], op0=ALU.add, op1=ALU.add),
                reads=[bt1[mg % 2], bxs[mg], bcols], writes=[bxs[mg]], tag='fevB')
            pr.dma('g', 'st', outT[mg * 128:(mg + 1) * 128, t0:t0 + TILE],
                   xs[:, mg * TILE:(mg + 1) * TILE],
                   reads=[bxs[mg]], writes=[bout], tag=f'store{t}_{mg}')
            if mg == 3 and t + 1 < NT:
                emit_ln1_apply(t + 1)

    n_s = pr.dma_cnt['st']
    pr.emit('g', lambda e: e.nop(), also=[('d', 'st', n_s)], tag='final_wait')

    with nc.Block() as block:
        pr.finalize(block, sems, dsems)
    es.close()
    return nc


def _one(bufs):
    return bufs[0]


def _ap3(qb, g, c):
    return qb.rearrange("p (hh t) -> p hh t", hh=H)[:, g * 4:(g + 1) * 4,
                                                   c * 128:c * 128 + 128]


def _ap3c(ltrep, g):
    return ltrep[:, g * 512:(g + 1) * 512].rearrange("p (r n) -> p r n", r=4)


def _ap3o(pb, g, c):
    return pb.rearrange("p (hh t) -> p hh t", hh=FC)[:, g * 4:(g + 1) * 4,
                                                     c * 128:c * 128 + 128]


_NC_CACHE = None


def kernel(L0_attn_x, L0_attn_qkv_w, L0_attn_qkv_norm_w, L0_attn_proj_w,
           L0_attn_proj_norm_w, L0_attn_norm_w, L0_attn_s, L0_ffn_weight,
           L0_ffn_gamma, L0_ffn_beta, L0_ffn_ln_weight, L0_ffn_ln_bias):
    global _NC_CACHE
    x = np.asarray(L0_attn_x, np.float32)

    def rearr(w, nmg):
        return np.ascontiguousarray(
            np.asarray(w, np.float32).reshape(nmg, 128, FC, 128).transpose(0, 3, 2, 1)
            .reshape(nmg, 128, 2048))

    wq = rearr(L0_attn_qkv_w, 48)
    wp = rearr(L0_attn_proj_w, 16)
    wf = rearr(L0_ffn_weight, 16)

    def recol(v):
        return np.ascontiguousarray(np.asarray(v, np.float32).reshape(FC, 128).T)

    base = {
        "wq_r": wq, "wp_r": wp, "wf_r": wf,
        "qn_r": recol(L0_attn_qkv_norm_w), "pn_r": recol(L0_attn_proj_norm_w),
        "nw_r": recol(L0_attn_norm_w), "ga_r": recol(L0_ffn_gamma),
        "be_r": recol(L0_ffn_beta), "lw_r": recol(L0_ffn_ln_weight),
        "lb_r": recol(L0_ffn_ln_bias),
        "s_in": np.asarray(L0_attn_s, np.float32).reshape(1, H),
        "iota_c": np.arange(128, dtype=np.float32).reshape(128, 1),
        "iota_r": np.arange(128, dtype=np.float32).reshape(1, 128),
        "diffp": np.maximum(np.arange(128)[None, :] - np.arange(128)[:, None], 0
                            ).astype(np.float32),
        "causal": (np.arange(128)[None, :] >= np.arange(128)[:, None]
                   ).astype(np.float32),
        "ones128": np.ones((128, 128), np.float32),
        "iden_in": np.eye(128).astype(ml_dtypes.bfloat16),
        "onescol_bf": np.ones((128, 1), ml_dtypes.bfloat16),
        "onescol_f": np.ones((128, 1), np.float32),
        "onesrow_f": np.ones((1, 128), np.float32),
        "cvals_in": np.tile(np.array([[RND_C, -RND_C, -0.5, 0.0]], np.float32),
                            (128, 1)),
    }
    in_maps = []
    for b in range(B):
        m = dict(base)
        m["xT"] = np.ascontiguousarray(x[b].T)
        in_maps.append(m)

    if _NC_CACHE is None:
        _NC_CACHE = build_nc()
    res = run_bass_kernel_spmd(_NC_CACHE, in_maps, core_ids=list(range(B)))
    out = np.stack([np.asarray(r["outT"]).T for r in res.results], axis=0)
    return out.astype(np.float32)


# version 17
# speedup vs baseline: 1.0115x; 1.0030x over previous
import numpy as np
import ml_dtypes
from contextlib import ExitStack

import concourse.bass as bass
import concourse.mybir as mybir
from concourse.bass_utils import run_bass_kernel_spmd

F32 = mybir.dt.float32
BF16 = mybir.dt.bfloat16
AF = mybir.ActivationFunctionType
ALU = mybir.AluOpType
AX = mybir.AxisListType

B = 8
SEQ = 4096
D = 2048
FC = 16
H = 16
HD = 128
CH = 128
TILE = 512
NT = SEQ // TILE
NCH = TILE // CH
DQ = 3 * D
MQK = 32
MP = 16
MF = 16
NEG = 4
RND_C = float(3 * 2**22)
EPS32 = float(np.finfo(np.float32).eps)

ENG_NAMES = {'t': 'tensor', 'v': 'vector', 's': 'scalar', 'g': 'gpsimd', 'y': 'sync'}
PARANOID_PE = False
STRICT_MAIN = False
DEBUG_O = False


class Buf:
    __slots__ = ('last_w', 'readers', 'name')

    def __init__(self, name=''):
        self.last_w = None
        self.readers = []
        self.name = name


DMA_SEMS = (('xA', 'xB', 'wpA', 'wpB', 'st', 'qs', 'cl', 'b1', 'b2', 'vs0',
             'qa0', 'qa1', 'qb0', 'qb1')
            + tuple(f'qi{i}' for i in range(40)))


class Prog:
    def __init__(self, nc):
        self.nc = nc
        self.recs = {k: [] for k in ENG_NAMES}
        self.dma_cnt = {k: 0 for k in DMA_SEMS}
        self.pe_strict = True

    def emit(self, ek, fn, reads=(), writes=(), also=(), tag='', tr=False):
        deps = []
        for b in reads:
            if b.last_w is not None:
                deps.append(b.last_w)
        for b in writes:
            if b.last_w is not None:
                deps.append(b.last_w)
            deps.extend(b.readers)
        deps.extend(also)
        ndeps = []
        seen = set()
        for d in deps:
            if d is None:
                continue
            kind, e2, i2 = d
            if (kind == 'c' and e2 == ek == 't' and not PARANOID_PE
                    and not self.pe_strict
                    and not tr and not self.recs['t'][i2].get('tr')):
                continue
            if kind == 'd':
                d = (kind, e2, self.dma_cnt[e2])
            if d in seen:
                continue
            seen.add(d)
            if kind == 'c':
                self.recs[e2][i2]['inc'] = True
            ndeps.append(d)
        idx = len(self.recs[ek])
        self.recs[ek].append(dict(fn=fn, waits=ndeps, inc=False, dma=None, tag=tag,
                                  tr=tr))
        tok = ('c', ek, idx)
        for b in reads:
            b.readers.append(tok)
        for b in writes:
            b.last_w = tok
            b.readers = []
        return tok

    def dma(self, qk, sem, out_ap, in_ap, reads=(), writes=(), also=(), tag='', **kw):
        deps = []
        for b in reads:
            if b.last_w is not None:
                deps.append(b.last_w)
        for b in writes:
            if b.last_w is not None:
                deps.append(b.last_w)
            deps.extend(b.readers)
        deps.extend(also)
        ndeps = []
        seen = set()
        for d in deps:
            if d is None:
                continue
            kind, e2, i2 = d
            if kind == 'd':
                d = (kind, e2, self.dma_cnt[e2])
            if d in seen:
                continue
            seen.add(d)
            if kind == 'c':
                self.recs[e2][i2]['inc'] = True
            ndeps.append(d)
        self.dma_cnt[sem] += 1
        n = self.dma_cnt[sem]
        fn = lambda eng, o=out_ap, i=in_ap, kw=kw: eng.dma_start(out=o, in_=i, **kw)
        self.recs[qk].append(dict(fn=fn, waits=ndeps, inc=False, dma=sem, tag=tag))
        tok = ('d', sem, n)
        for b in reads:
            b.readers.append(tok)
        for b in writes:
            b.last_w = tok
            b.readers = []
        return tok

    def finalize(self, block, sems, dsems):
        pref = {}
        for ek, rs in self.recs.items():
            c = 0
            p = []
            for r in rs:
                if r['inc']:
                    c += 1
                p.append(c)
            pref[ek] = p

        def mk_body(ek, rs):
            def body(eng):
                watermark = {}
                for r in rs:
                    for (kind, e2, i2) in r['waits']:
                        if kind == 'c':
                            sem = sems[e2]
                            val = pref[e2][i2]
                            if val == 0:
                                continue
                        else:
                            sem = dsems[e2]
                            val = 16 * i2
                        key = (kind, e2)
                        if watermark.get(key, -1) >= val:
                            continue
                        eng.wait_ge(sem, val)
                        watermark[key] = val
                    inst = r['fn'](eng)
                    if r['dma'] is not None:
                        inst.then_inc(dsems[r['dma']], 16)
                    elif r['inc']:
                        inst.then_inc(sems[ek], 1)
            return body

        for ek in ENG_NAMES:
            getattr(block, ENG_NAMES[ek])(mk_body(ek, self.recs[ek]))


def build_nc():
    nc = bass.Bass("TRN2", target_bir_lowering=False)

    xT = nc.dram_tensor("xT", [D, SEQ], F32, kind="ExternalInput").ap()
    wq_r = nc.dram_tensor("wq_r", [48, 128, 2048], F32, kind="ExternalInput").ap()
    wp_r = nc.dram_tensor("wp_r", [16, 128, 2048], F32, kind="ExternalInput").ap()
    wf_r = nc.dram_tensor("wf_r", [16, 128, 2048], F32, kind="ExternalInput").ap()
    vin = {}
    for nm in ("qn", "pn", "nw", "ga", "be", "lw", "lb"):
        vin[nm] = nc.dram_tensor(nm + "_r", [128, FC], F32, kind="ExternalInput").ap()
    s_in = nc.dram_tensor("s_in", [1, H], F32, kind="ExternalInput").ap()
    iota_c = nc.dram_tensor("iota_c", [128, 1], F32, kind="ExternalInput").ap()
    iota_r = nc.dram_tensor("iota_r", [1, 128], F32, kind="ExternalInput").ap()
    diffp = nc.dram_tensor("diffp", [128, 128], F32, kind="ExternalInput").ap()
    causal = nc.dram_tensor("causal", [128, 128], F32, kind="ExternalInput").ap()
    ones128 = nc.dram_tensor("ones128", [128, 128], F32, kind="ExternalInput").ap()
    iden_in = nc.dram_tensor("iden_in", [128, 128], BF16, kind="ExternalInput").ap()
    onescol_bf = nc.dram_tensor("onescol_bf", [128, 1], BF16, kind="ExternalInput").ap()
    onescol_f = nc.dram_tensor("onescol_f", [128, 1], F32, kind="ExternalInput").ap()
    onesrow_f = nc.dram_tensor("onesrow_f", [1, 128], F32, kind="ExternalInput").ap()
    cvals_in = nc.dram_tensor("cvals_in", [128, 4], F32, kind="ExternalInput").ap()
    outT = nc.dram_tensor("outT", [D, SEQ], F32, kind="ExternalOutput").ap()
    odbg = (nc.dram_tensor("odbg", [D, SEQ], BF16, kind="ExternalOutput").ap()
            if DEBUG_O else None)
    qdbg = (nc.dram_tensor("qdbg", [2, 4, 128, FC * TILE], BF16,
                           kind="ExternalOutput").ap() if DEBUG_O else None)

    wq_t = nc.dram_tensor("wq_t", [MQK, 128, FC * 128], BF16, kind="Internal").ap()
    wv_t = nc.dram_tensor("wv_t", [8, 2, 128, 2048], BF16, kind="Internal").ap()
    wp_t = nc.dram_tensor("wp_t", [MP, 128, FC * 128], BF16, kind="Internal").ap()
    wf_t = nc.dram_tensor("wf_t", [MF, 128, FC * 128], BF16, kind="Internal").ap()
    bounce = nc.dram_tensor("bounce", [128], F32, kind="Internal").ap()

    A = nc.alloc_sbuf_tensor
    xA = A("xA", [128, FC * TILE], F32).ap()
    xB = A("xB", [128, FC * TILE], F32).ap()
    h = A("h", [128, FC * TILE], BF16).ap()
    qb = A("qb", [128, H * TILE], BF16).ap()
    kb = A("kb", [128, H * TILE], BF16).ap()
    vb = A("vb", [128, NCH * D], BF16).ap()
    pb = A("pb", [128, FC * TILE], BF16).ap()
    Sf = A("Sf", [128, H * HD], F32).ap()
    Sb = A("Sb", [128, H * HD], BF16).ap()
    wpool = A("wpool", [128, 2 * FC * 128], BF16).ap()
    vsl = A("vsl", [128, 8 * 256], BF16).ap()
    maskT = A("maskT", [128, H * 128], BF16).ap()
    ltrep = A("ltrep", [128, H * 128], BF16).ap()
    lrT = A("lrT", [128, H * 128], BF16).ap()
    o128 = A("o128", [128, 128], F32).ap()
    iden = A("iden", [128, 128], BF16).ap()
    dfp = A("dfp", [128, 128], F32).ap()
    cau = A("cau", [128, 128], F32).ap()
    ioc = A("ioc", [128, 1], F32).ap()
    ior = A("ior", [1, 128], F32).ap()
    ocb = A("ocb", [128, 1], BF16).ap()
    ocf = A("ocf", [128, 1], F32).ap()
    orf = A("orf", [1, 128], F32).ap()
    cst = A("cst", [128, 4], F32).ap()
    cols = {nm: A("c_" + nm, [128, FC], F32).ap()
            for nm in ("qn", "pn", "nw", "ga", "be", "lw", "lb",
                       "qnm", "g", "nw2", "gs", "b2")}
    srow = A("srow", [1, H], F32).ap()
    nw2b = A("nw2b", [128, FC], BF16).ap()
    lbbf = A("lbbf", [128, FC], BF16).ap()
    sbc = A("sbc", [128, H], F32).ap()
    snb = A("snb", [128, H], F32).ap()
    sm128 = A("sm128", [128, H], F32).ap()
    lamT_c = A("lamT_c", [128, H], F32).ap()
    lrev_c = A("lrev_c", [128, H], F32).ap()
    mcols = A("mcols", [128, 4], F32).ap()
    sfcol = A("sfcol", [128, 1], F32).ap()
    stat = A("stat", [128, 3 * FC], F32).ap()
    statm = A("statm", [128, FC], F32).ap()
    rows = A("rows", [1, 3 * TILE], F32).ap()
    tot = A("tot", [1, 8], F32).ap()
    bfp = A("bfp", [128, 4 * TILE], BF16).ap()
    t1s = A("t1s", [128, 2 * TILE], BF16).ap()
    qts = A("qts", [128, 2 * TILE], BF16).ap()
    kts = A("kts", [128, 2 * TILE], BF16).ap()
    scm = A("scm", [128, 2 * TILE], BF16).ap()
    brow = A("brow", [128, 3 * TILE], BF16).ap()
    rbf = A("rbf", [1, TILE], BF16).ap()

    P = nc.alloc_psum_tensor
    pA = P("pA", [128, 512], F32).ap()
    pB = P("pB", [128, 512], F32).ap()
    pC = P("pC", [128, 512], F32).ap()
    pD = P("pD", [128, 1024], BF16).ap()
    pE = P("pE", [128, 512], F32).ap()
    pF = P("pF", [128, 512], F32).ap()
    pS1 = P("pS1", [1, 512], F32).ap()
    pS2 = P("pS2", [1, 512], F32).ap()

    es = ExitStack()
    sems = {k: es.enter_context(nc.semaphore("sem_" + k)) for k in ENG_NAMES}
    dsems = {k: es.enter_context(nc.semaphore("dma_" + k)) for k in DMA_SEMS}

    pr = Prog(nc)

    bxA = [Buf(f"xA{i}") for i in range(FC)]
    bxB = [Buf(f"xB{i}") for i in range(FC)]
    bh = [Buf(f"h{i}") for i in range(FC)]
    bqb = [Buf(f"qb{i}") for i in range(H)]
    bkb = [Buf(f"kb{i}") for i in range(H)]
    bvb = [Buf(f"vb{i}") for i in range(NCH * 4)]
    bpb = [Buf(f"pb{i}") for i in range(FC)]
    bSf = [Buf(f"Sf{i}") for i in range(H)]
    bSb = [Buf(f"Sb{i}") for i in range(H)]
    bwp = [Buf("wp0"), Buf("wp1")]
    bvsl = [Buf(f"vsl{i}") for i in range(8)]
    bconsts = Buf("consts")
    bcols = Buf("cols")
    brows = [Buf(f"row{i}") for i in range(4)]
    btot = Buf("tot")
    bbfp = [Buf(f"bfp{i}") for i in range(4)]
    bt1 = [Buf("t1a"), Buf("t1b")]
    bqt = [Buf("qta"), Buf("qtb")]
    bkt = [Buf("kta"), Buf("ktb")]
    bscm = [Buf("scma"), Buf("scmb")]
    bbrow = [Buf("brw0"), Buf("brw1"), Buf("brw2")]
    brbf = Buf("rbf")
    bstat = Buf("stat")
    bpA = Buf("pA"); bpB = Buf("pB"); bpC = Buf("pC"); bpD = [Buf("pD0"), Buf("pD1")]
    bpE = Buf("pE"); bpF = Buf("pF"); bpS1 = Buf("pS1"); bpS2 = Buf("pS2")
    bwq_t = [Buf(f"wqt{i}") for i in range(MQK)]
    bwv_t = [Buf(f"wvt{i}") for i in range(16)]
    bwp_t = [Buf(f"wpt{i}") for i in range(MP)]
    bwf_t = [Buf(f"wft{i}") for i in range(MF)]
    bbounce = Buf("bounce")
    bout = Buf("outT")

    tload = []
    for (dst, src) in ((dfp, diffp), (cau, causal), (o128, ones128), (iden, iden_in),
                       (ioc, iota_c), (ior, iota_r), (ocb, onescol_bf), (ocf, onescol_f),
                       (orf, onesrow_f), (cst, cvals_in), (srow[:1, :], s_in)):
        tload.append(pr.dma('g', 'cl', dst, src, writes=[bconsts], tag='cload'))
    for nm in ("qn", "pn", "nw", "ga", "be", "lw", "lb"):
        tload.append(pr.dma('g', 'cl', cols[nm], vin[nm], writes=[bcols], tag='vload'))

    tok = pr.emit('t', lambda e: e.matmul(pA[:, 0:H], orf, srow[:1, :], start=True, stop=True),
                  reads=[bconsts], writes=[bpA], tag='sbcast')
    pr.emit('v', lambda e: e.tensor_copy(out=sbc, in_=pA[:, 0:H]), reads=[bpA], writes=[bconsts])
    pr.emit('v', lambda e: e.tensor_scalar_mul(out=snb, in0=sbc, scalar1=-1.0),
            reads=[bconsts], writes=[bconsts])
    pr.emit('v', lambda e: e.tensor_scalar_mul(out=sm128, in0=sbc, scalar1=-128.0),
            reads=[bconsts], writes=[bconsts])
    pr.emit('s', lambda e: e.activation(out=lamT_c, in_=sm128, func=AF.Exp),
            reads=[bconsts], writes=[bconsts])
    for hh in range(H):
        pr.emit('s', lambda e, hh=hh: e.activation(
            out=lrev_c[:, hh:hh + 1], in_=ioc, func=AF.Exp,
            scale=sbc[:, hh:hh + 1], bias=sm128[:, hh:hh + 1]),
            reads=[bconsts], writes=[bconsts])
    for hh in range(H):
        pr.emit('v', lambda e, hh=hh: e.tensor_scalar_mul(
            out=lrT[:, hh * 128:(hh + 1) * 128], in0=o128, scalar1=lrev_c[:, hh:hh + 1]),
            reads=[bconsts], writes=[bconsts])
    for hh in range(H):
        pr.emit('s', lambda e, hh=hh: e.activation(
            out=rows[:1, (hh % 3) * 512:(hh % 3) * 512 + 128],
            in_=ior, func=AF.Exp, scale=snb[0:1, hh:hh + 1]),
            reads=[bconsts], writes=[brows[hh % 3]], tag='ltrow')
        pr.emit('t', lambda e, hh=hh: e.matmul(
            pB[:, 0:128], orf, rows[:1, (hh % 3) * 512:(hh % 3) * 512 + 128], start=True, stop=True),
            reads=[brows[hh % 3]], writes=[bpB])
        pr.emit('v', lambda e, hh=hh: e.tensor_copy(
            out=ltrep[:, hh * 128:(hh + 1) * 128], in_=pB[:, 0:128]),
            reads=[bpB], writes=[bconsts])
    for hh in range(H):
        pr.emit('s', lambda e, hh=hh: e.activation(
            out=pE[:, 0:128],
            in_=dfp, func=AF.Exp, scale=snb[:, hh:hh + 1]),
            reads=[bconsts], writes=[bpE], tag='maskexp')
        pr.emit('v', lambda e, hh=hh: e.tensor_tensor(
            out=maskT[:, hh * 128:(hh + 1) * 128], in0=pE[:, 0:128], in1=cau, op=ALU.mult),
            reads=[bpE, bconsts], writes=[bconsts])

    stat_it = [0]

    def stat_pass(wr, nmg, col_off, do_max):
        for i in range(nmg // 2):
            hl = stat_it[0] % 2
            sl = xB[:, hl * 4096:(hl + 1) * 4096]
            buf = bxB[hl * 8:(hl + 1) * 8]
            qk = 'y' if i % 2 == 0 else 's'
            dt = pr.dma(qk, 'qb0' if hl == 0 else 'qb1',
                        sl.rearrange("p (a c) -> p a c", a=2),
                        wr[i * 2:(i + 1) * 2].rearrange("a p c -> p a c"),
                        writes=buf, tag='statload')
            pr.emit('v', lambda e, sl=sl, i=i, col_off=col_off: e.tensor_reduce(
                out=stat[:, col_off + i:col_off + i + 1], in_=sl,
                axis=AX.X, op=ALU.add, apply_absolute_value=True),
                reads=buf, writes=[bstat], also=[dt], tag='statred')
            if do_max:
                pr.emit('v', lambda e, sl=sl, i=i: e.tensor_reduce(
                    out=statm[:, i:i + 1], in_=sl,
                    axis=AX.X, op=ALU.max, apply_absolute_value=True),
                    reads=buf, writes=[bstat], also=[dt], tag='statmax')
            stat_it[0] += 1

    stat_pass(wq_r, 48, 0, False)
    stat_pass(wp_r, 16, 24, False)
    stat_pass(wf_r, 16, 32, True)
    stat_rng = {0: (0, 24), 1: (24, 32), 2: (32, 40)}
    for j, name in enumerate(("q", "p", "f")):
        lo, hi = stat_rng[j]
        pr.emit('v', lambda e, j=j, lo=lo, hi=hi: e.tensor_reduce(
            out=mcols[:, j:j + 1],
            in_=stat[:, lo:hi], axis=AX.X, op=ALU.add),
            reads=[bstat], writes=[bcols], tag='stot')
        pr.emit('t', lambda e, j=j: e.matmul(
            pA[0:1, j:j + 1],
            mcols[:, j:j + 1], ocf, start=True, stop=True),
            reads=[bcols, bconsts], writes=[bpA], tag='stotmm')
    pr.emit('v', lambda e: e.tensor_copy(out=tot[:1, 0:3], in_=pA[0:1, 0:3]),
            reads=[bpA], writes=[btot])
    pr.emit('v', lambda e: e.tensor_scalar(out=tot[:1, 3:4], in0=tot[:1, 0:1],
                                           scalar1=1.0 / (DQ * D), scalar2=1e-5,
                                           op0=ALU.mult, op1=ALU.max),
            reads=[btot], writes=[btot])
    pr.emit('v', lambda e: e.tensor_scalar(out=tot[:1, 4:5], in0=tot[:1, 1:2],
                                           scalar1=1.0 / (D * D), scalar2=1e-5,
                                           op0=ALU.mult, op1=ALU.max),
            reads=[btot], writes=[btot])
    pr.emit('v', lambda e: e.reciprocal(out=tot[:1, 5:6], in_=tot[:1, 3:4]),
            reads=[btot], writes=[btot])
    pr.emit('v', lambda e: e.reciprocal(out=tot[:1, 6:7], in_=tot[:1, 4:5]),
            reads=[btot], writes=[btot])
    for j, src in enumerate((5, 3, 6, 4)):
        pr.emit('t', lambda e, j=j, src=src: e.matmul(
            pA[:, 4 + j:5 + j], orf, tot[:1, src:src + 1], start=True, stop=True),
            reads=[btot, bconsts], writes=[bpA])
    pr.emit('v', lambda e: e.tensor_copy(out=mcols[:, 0:4], in_=pA[:, 4:8]),
            reads=[bpA], writes=[bcols])
    pr.emit('v', lambda e: e.tensor_reduce(out=statm[:, 0:1], in_=statm[:, 0:8],
                                           axis=AX.X, op=ALU.max),
            reads=[bstat], writes=[bstat])
    pr.dma('g', 'b1', bounce, statm[:, 0:1], reads=[bstat], writes=[bbounce], tag='bounce1')
    pr.dma('g', 'b2', rows[:1, 0:128],
           bounce.rearrange("(a b) -> a b", a=1), reads=[bbounce], writes=[brows[0]], tag='bounce2')
    pr.emit('v', lambda e: e.tensor_reduce(out=tot[:1, 7:8], in_=rows[:1, 0:128],
                                           axis=AX.X, op=ALU.max),
            reads=[brows[0]], writes=[btot])
    pr.emit('t', lambda e: e.matmul(pA[:, 8:9], orf, tot[:1, 7:8], start=True, stop=True),
            reads=[btot, bconsts], writes=[bpA])
    pr.emit('v', lambda e: e.tensor_copy(out=sfcol, in_=pA[:, 8:9]),
            reads=[bpA], writes=[bcols])

    pr.emit('v', lambda e: e.tensor_scalar_mul(out=cols["qnm"], in0=cols["qn"],
                                               scalar1=mcols[:, 1:2]),
            reads=[bcols], writes=[bcols])
    pr.emit('v', lambda e: e.tensor_tensor(out=cols["g"], in0=cols["nw"], in1=cols["pn"],
                                           op=ALU.mult),
            reads=[bcols], writes=[bcols])
    pr.emit('v', lambda e: e.tensor_scalar_mul(out=cols["g"], in0=cols["g"],
                                               scalar1=mcols[:, 3:4]),
            reads=[bcols], writes=[bcols])
    pr.emit('s', lambda e: e.activation(out=cols["nw2"], in_=cols["nw"], func=AF.Square),
            reads=[bcols], writes=[bcols])
    pr.emit('v', lambda e: e.tensor_copy(out=nw2b, in_=cols["nw2"]),
            reads=[bcols], writes=[bcols])
    pr.emit('v', lambda e: e.tensor_copy(out=lbbf, in_=cols["lb"]),
            reads=[bcols], writes=[bcols])
    pr.emit('v', lambda e: e.tensor_scalar_mul(out=cols["gs"], in0=cols["ga"],
                                               scalar1=sfcol[:, 0:1]),
            reads=[bcols], writes=[bcols])

    def quant_tile(src_ap, src_bufs, dst_ap, dst_bufs, scale_col, extra=()):
        src_bufs = list(src_bufs)
        dst_bufs = list(dst_bufs)
        pr.emit('s', lambda e: e.activation(out=src_ap, in_=src_ap, func=AF.Identity,
                                            scale=scale_col, bias=cst[:, 0:1]),
                reads=src_bufs + [bcols, bconsts], writes=src_bufs, also=list(extra),
                tag='q1')
        pr.emit('s', lambda e: e.activation(out=src_ap, in_=src_ap, func=AF.Identity,
                                            bias=cst[:, 1:2]),
                reads=src_bufs + [bconsts], writes=src_bufs, tag='q2')
        pr.emit('v', lambda e: e.tensor_scalar_max(out=src_ap, in0=src_ap, scalar1=-1.0),
                reads=src_bufs, writes=src_bufs, tag='q3')
        pr.emit('v', lambda e: e.tensor_scalar_min(out=dst_ap, in0=src_ap, scalar1=1.0),
                reads=src_bufs, writes=dst_bufs, tag='q4')

    def quant2(i, wr, mg0, scale_col, kind):
        hl = i % 2
        sap = xB[:, hl * 4096:(hl + 1) * 4096]
        sbufs = bxB[hl * 8:(hl + 1) * 8]
        sqsem = 'qb0' if hl == 0 else 'qb1'
        dap = (kb if hl == 0 else vb)[:, 0:4096]
        dbufs = (bkb if hl == 0 else bvb)[0:8]
        wsem = f'qi{i}'
        wq_queue = 'y' if i % 2 == 0 else 's'
        dt = pr.dma(wq_queue, sqsem,
                    sap.rearrange("p (a c) -> p a c", a=2),
                    wr[mg0:mg0 + 2].rearrange("a p c -> p a c"),
                    writes=sbufs, tag=f'qin{kind}{mg0}')
        if kind == 'f':
            pr.emit('s', lambda e: e.sign(out=dap, in_=sap),
                    reads=list(sbufs), writes=list(dbufs), also=[dt], tag='fsign')
        else:
            pr.emit('s', lambda e: e.activation(out=sap, in_=sap, func=AF.Identity,
                                                scale=scale_col, bias=cst[:, 0:1]),
                    reads=list(sbufs) + [bcols, bconsts], writes=list(sbufs),
                    also=[dt], tag='q1')
            pr.emit('s', lambda e: e.activation(out=sap, in_=sap, func=AF.Identity,
                                                bias=cst[:, 1:2]),
                    reads=list(sbufs) + [bconsts], writes=list(sbufs), tag='q2')
            pr.emit('v', lambda e: e.tensor_scalar_max(out=sap, in0=sap, scalar1=-1.0),
                    reads=list(sbufs), writes=list(sbufs), tag='q3')
            pr.emit('v', lambda e: e.tensor_scalar_min(out=dap, in0=sap, scalar1=1.0),
                    reads=list(sbufs), writes=list(dbufs), tag='q4')
        for j in range(2):
            mg = mg0 + j
            dsl = dap[:, j * 2048:(j + 1) * 2048]
            if kind == 'q':
                pr.dma(wq_queue, wsem, wq_t[mg], dsl, reads=list(dbufs),
                       writes=[bwq_t[mg]], tag=f'qout{mg}')
            elif kind == 'v':
                vm = mg - MQK
                heg = vm // 2
                eo = (vm % 2) * 128
                for kc in range(FC):
                    pr.dma(wq_queue, wsem,
                           wv_t[heg, kc // 8, :, (kc % 8) * 256 + eo:(kc % 8) * 256 + eo + 128],
                           dsl[:, kc * 128:(kc + 1) * 128],
                           reads=list(dbufs), writes=[bwv_t[heg * 2 + kc // 8]],
                           tag=f'vout{mg}_{kc}')
            elif kind == 'p':
                pr.dma(wq_queue, wsem, wp_t[mg], dsl, reads=list(dbufs),
                       writes=[bwp_t[mg]], tag=f'pout{mg}')
            else:
                pr.dma(wq_queue, wsem, wf_t[mg], dsl, reads=list(dbufs),
                       writes=[bwf_t[mg]], tag=f'fout{mg}')

    quant_sched = []
    qit = [0]

    def qthunk(wr, mg0, sc, kind):
        def run():
            quant2(qit[0], wr, mg0, sc, kind)
            qit[0] += 1
        return run

    qthunks = {'q': [], 'k': [], 'v': [], 'p': [], 'f': []}
    for mg0 in range(0, 16, 2):
        qthunks['q'].append(qthunk(wq_r, mg0, mcols[:, 0:1], 'q'))
    for mg0 in range(16, MQK, 2):
        qthunks['k'].append(qthunk(wq_r, mg0, mcols[:, 0:1], 'q'))
    for mg0 in range(0, 16, 2):
        qthunks['v'].append(qthunk(wq_r, MQK + mg0, mcols[:, 0:1], 'v'))
    for mg0 in range(0, MP, 2):
        qthunks['p'].append(qthunk(wp_r, mg0, mcols[:, 2:3], 'p'))
    for mg0 in range(0, MF, 2):
        qthunks['f'].append(qthunk(wf_r, mg0, None, 'f'))

    pr.pe_strict = STRICT_MAIN
    x_of = lambda t: (xA, bxA) if t % 2 == 0 else (xB, bxB)

    def emit_xload(t):
        xs, bxs = x_of(t)
        for fc in range(FC):
            pr.dma('g', 'xA' if t % 2 == 0 else 'xB',
                   xs[:, fc * TILE:(fc + 1) * TILE],
                   xT[fc * 128:(fc + 1) * 128, t * TILE:(t + 1) * TILE],
                   writes=[bxs[fc]], tag=f'xload{t}_{fc}')

    def emit_ln1_stats(t):
        xs, bxs = x_of(t)
        def ln1_stat(j):
            sl_ = bfp[:, (j % 2) * TILE:(j % 2) * TILE + TILE]
            sb_ = bbfp[j % 2]
            sq_ = bfp[:, (2 + j % 2) * TILE:(2 + j % 2) * TILE + TILE]
            sqb_ = bbfp[2 + j % 2]
            pr.emit('t', lambda e, sl_=sl_, j=j: e.matmul(
                pS1[:1, :], ocb, sl_, start=(j == 0), stop=(j == FC - 1)),
                reads=[sb_, bconsts], writes=[bpS1], tag='ln1sum')
            pr.emit('t', lambda e, sq_=sq_, j=j: e.matmul(
                pS2[:1, :], ocb, sq_, start=(j == 0), stop=(j == FC - 1)),
                reads=[sqb_, bconsts], writes=[bpS2], tag='ln1sum2')

        for fc in range(FC):
            sl = bfp[:, (fc % 2) * TILE:(fc % 2) * TILE + TILE]
            sbuf = bbfp[fc % 2]
            sq = bfp[:, (2 + fc % 2) * TILE:(2 + fc % 2) * TILE + TILE]
            sqb = bbfp[2 + fc % 2]
            pr.emit('g', lambda e, sl=sl, xs=xs, fc=fc: e.tensor_copy(
                out=sl, in_=xs[:, fc * TILE:(fc + 1) * TILE]),
                reads=[bxs[fc]], writes=[sbuf], tag='xbcast')
            pr.emit('s', lambda e, sq=sq, xs=xs, fc=fc: e.activation(
                out=sq, in_=xs[:, fc * TILE:(fc + 1) * TILE], func=AF.Square),
                reads=[bxs[fc]], writes=[sqb], tag='xsq')
            if fc > 0:
                ln1_stat(fc - 1)
        ln1_stat(FC - 1)
        mu = rows[:1, 0:TILE]
        m2 = rows[:1, TILE:2 * TILE]
        beta = rows[:1, 2 * TILE:3 * TILE]
        k1 = (1.0 + EPS32) / D
        c0 = 1e-5 * EPS32
        pr.emit('v', lambda e: e.tensor_scalar_mul(out=mu, in0=pS1[:1, :], scalar1=1.0 / D),
                reads=[bpS1], writes=[brows[0]], tag='mu')
        pr.emit('g', lambda e: e.tensor_tensor(out=m2, in0=mu, in1=mu, op=ALU.mult),
                reads=[brows[0]], writes=[brows[1]])
        pr.emit('g', lambda e: e.tensor_scalar(out=m2, in0=m2, scalar1=1.0 + EPS32,
                                               scalar2=-c0, op0=ALU.mult, op1=ALU.add),
                reads=[brows[1]], writes=[brows[1]])
        pr.emit('v', lambda e: e.scalar_tensor_tensor(out=beta, in0=pS2[:1, :], scalar=k1,
                                                      in1=m2, op0=ALU.mult, op1=ALU.subtract),
                reads=[bpS2, brows[1]], writes=[brows[2]])
        pr.emit('s', lambda e: e.activation(out=beta, in_=beta, func=AF.Ln),
                reads=[brows[2]], writes=[brows[2]])
        pr.emit('s', lambda e: e.activation(out=beta, in_=beta, func=AF.Exp,
                                            scale=cst[0:1, 2:3]),
                reads=[brows[2]], writes=[brows[2]])

    def emit_ln1_apply(t):
        xs, bxs = x_of(t)
        mu = rows[:1, 0:TILE]
        beta = rows[:1, 2 * TILE:3 * TILE]
        pr.emit('t', lambda e: e.matmul(pC, orf, mu, start=True, stop=True),
                reads=[brows[0], bconsts], writes=[bpC], tag='mub')
        pr.emit('t', lambda e: e.matmul(pE, orf, beta, start=True, stop=True),
                reads=[brows[2], bconsts], writes=[bpE], tag='betab')
        pr.emit('v', lambda e: e.tensor_copy(out=brow[:, 0:TILE], in_=pE),
                reads=[bpE], writes=[bbrow[0]], tag='betabf')
        for fc in range(FC):
            tsl = t1s[:, (fc % 2) * TILE:(fc % 2) * TILE + TILE]
            tb = bt1[fc % 2]
            pr.emit('v', lambda e, tsl=tsl, xs=xs, fc=fc: e.tensor_tensor(
                out=tsl, in0=xs[:, fc * TILE:(fc + 1) * TILE], in1=pC, op=ALU.subtract),
                reads=[bxs[fc], bpC], writes=[tb], tag='t1')
            pr.emit('v', lambda e, tsl=tsl, fc=fc: e.scalar_tensor_tensor(
                out=h[:, fc * TILE:(fc + 1) * TILE], in0=tsl,
                scalar=cols["qnm"][:, fc:fc + 1],
                in1=brow[:, 0:TILE], op0=ALU.mult, op1=ALU.mult),
                reads=[tb, bbrow[0], bcols], writes=[bh[fc]], tag='happ')

    emit_xload(0)
    emit_ln1_stats(0)
    emit_ln1_apply(0)

    for t in range(NT):
        xs, bxs = x_of(t)
        t0 = t * TILE

        for g in range(NEG):
            if t == 0:
                qthunks['q'][2 * g](); qthunks['q'][2 * g + 1]()
                qthunks['k'][2 * g](); qthunks['k'][2 * g + 1]()
                qthunks['v'][2 * g](); qthunks['v'][2 * g + 1]()
                qthunks['p'][g](); qthunks['f'][g]()
            for j in range(8):
                mg = (4 * g + j) if j < 4 else (16 + 4 * g + j - 4)
                slot = j % 2
                wsl = wpool[:, slot * 2048:(slot + 1) * 2048]
                psum, bps = (pA, bpA) if slot == 0 else (pB, bpB)
                pr.dma('y', 'wpA' if slot == 0 else 'wpB', wsl, wq_t[mg],
                       reads=[bwq_t[mg]], writes=[bwp[slot]], tag=f'wq{t}_{mg}')
                for kc in range(FC):
                    pr.emit('t', lambda e, wsl=wsl, kc=kc, psum=psum: e.matmul(
                        psum, wsl[:, kc * 128:(kc + 1) * 128],
                        h[:, kc * TILE:(kc + 1) * TILE],
                        start=(kc == 0), stop=(kc == FC - 1)),
                        reads=[bwp[slot], bh[kc]], writes=[bps], tag=f'qkmm{mg}')
                if mg < 16:
                    pr.emit('s', lambda e, mg=mg, psum=psum: e.copy(
                        out=qb[:, mg * TILE:(mg + 1) * TILE], in_=psum),
                        reads=[bps], writes=[bqb[mg]], tag='qevac')
                else:
                    hh = mg - 16
                    pr.emit('s', lambda e, hh=hh, psum=psum: e.copy(
                        out=kb[:, hh * TILE:(hh + 1) * TILE], in_=psum),
                        reads=[bps], writes=[bkb[hh]], tag='kevac')
            if g > 0:
                for hh in range((g - 1) * 4, g * 4):
                    sq = bfp[:, (hh % 4) * TILE:(hh % 4) * TILE + TILE]
                    sqb = bbfp[hh % 4]
                    pr.emit('t', lambda e, sq=sq, hh=hh: e.matmul(
                        pS1[:1, :], ocb, sq, start=(hh == 0), stop=(hh == FC - 1)),
                        reads=[sqb, bconsts], writes=[bpS1], tag='osumA')
                    pr.emit('t', lambda e, sq=sq, hh=hh: e.matmul(
                        pS2[:1, :], nw2b[:, hh:hh + 1], sq,
                        start=(hh == 0), stop=(hh == FC - 1)),
                        reads=[sqb, bcols, bconsts], writes=[bpS2], tag='osumB')

            for heg in (2 * g, 2 * g + 1):
                wslot = heg % 2
                wsl = wpool[:, wslot * 2048:(wslot + 1) * 2048]
                pr.dma('y', 'vs0', vsl, wv_t[heg, 0],
                       reads=[bwv_t[heg * 2]], writes=[bvsl[0]], tag=f'wvlo{t}_{heg}')
                pr.dma('y', 'wpA' if wslot == 0 else 'wpB', wsl, wv_t[heg, 1],
                       reads=[bwv_t[heg * 2 + 1]], writes=[bwp[wslot]],
                       tag=f'wvhi{t}_{heg}')
                for tc in range(NCH):
                    psum, bps = ((pA, bpA), (pB, bpB))[tc % 2]
                    po = (tc // 2) * 256
                    for kc in range(FC):
                        rsl = (vsl[:, (kc % 8) * 256:(kc % 8) * 256 + 256] if kc < 8
                               else wsl[:, (kc - 8) * 256:(kc - 8) * 256 + 256])
                        rb = bvsl[0] if kc < 8 else bwp[wslot]
                        pr.emit('t', lambda e, kc=kc, tc=tc, psum=psum, po=po,
                                rsl=rsl: e.matmul(
                            psum[:, po:po + 256],
                            h[:, kc * TILE + tc * 128: kc * TILE + tc * 128 + 128],
                            rsl, start=(kc == 0), stop=(kc == FC - 1)),
                            reads=[rb, bh[kc]], writes=[bps], tag=f'vmm{heg}_{tc}')
                    pr.emit('s', lambda e, tc=tc, heg=heg, psum=psum, po=po: e.copy(
                        out=vb[:, tc * D + heg * 256: tc * D + heg * 256 + 256],
                        in_=psum[:, po:po + 256]),
                        reads=[bps], writes=[bvb[tc * 4 + heg // 2]], tag='vevac')

            hs = range(g * 4, (g + 1) * 4)
            for c in range(NCH):
                cg = t * NCH + c
                for i, hh in enumerate(hs):
                    pr.emit('t', lambda e, hh=hh, i=i, c=c: e.matmul(
                        pC[:, i * 128:(i + 1) * 128],
                        kb[:, hh * TILE + c * 128: hh * TILE + c * 128 + 128],
                        qb[:, hh * TILE + c * 128: hh * TILE + c * 128 + 128],
                        start=True, stop=True),
                        reads=[bkb[hh], bqb[hh]], writes=[bpC], tag=f'score{c}_{hh}')
                sc = scm[:, (c % 2) * TILE:(c % 2) * TILE + TILE]
                scb = bscm[c % 2]
                pr.emit('v', lambda e, sc=sc, g=g: e.tensor_tensor(
                    out=sc, in0=pC, in1=maskT[:, g * 512:(g + 1) * 512], op=ALU.mult),
                    reads=[bpC, bconsts], writes=[scb], tag='scevac')
                qt = qts[:, (c % 2) * TILE:(c % 2) * TILE + TILE]
                qtb = bqt[c % 2]
                pr.emit('v', lambda e, qt=qt, g=g, c=c: e.tensor_tensor(
                    out=qt.rearrange("p (r n) -> p r n", r=4),
                    in0=_ap3(qb, g, c),
                    in1=_ap3c(ltrep, g), op=ALU.mult),
                    reads=[bqb[hs[0]], bqb[hs[1]], bqb[hs[2]], bqb[hs[3]], bconsts],
                    writes=[qtb], tag='qtilde')
                pdo = (c % 2) * 512
                for i, hh in enumerate(hs):
                    pr.emit('t', lambda e, hh=hh, i=i, c=c, pdo=pdo: e.transpose(
                        pD[:, pdo + i * 128: pdo + (i + 1) * 128],
                        kb[:, hh * TILE + c * 128: hh * TILE + c * 128 + 128], iden),
                        reads=[bkb[hh], bconsts], writes=[bpD[c % 2]], tag='ktr', tr=True)
                kt = kts[:, (c % 2) * TILE:(c % 2) * TILE + TILE]
                ktb = bkt[c % 2]
                pr.emit('v', lambda e, kt=kt, g=g, pdo=pdo, c=c: e.tensor_tensor(
                    out=kt, in0=pD[:, pdo:pdo + 512], in1=lrT[:, g * 512:(g + 1) * 512],
                    op=ALU.mult),
                    reads=[bpD[c % 2], bconsts], writes=[ktb], tag='ktevac')
                for i, hh in enumerate(hs):
                    pr.emit('t', lambda e, hh=hh, i=i, c=c, sc=sc: e.matmul(
                        pE[:, i * 128:(i + 1) * 128],
                        vb[:, c * D + hh * 128: c * D + hh * 128 + 128],
                        sc[:, i * 128:(i + 1) * 128], start=True, stop=(cg == 0),
                        skip_group_check=True),
                        reads=[bvb[c * 4 + hh // 4], scb], writes=[bpE],
                        tag=f'intra{c}_{hh}')
                    if cg > 0:
                        pr.emit('t', lambda e, hh=hh, i=i, qt=qt: e.matmul(
                            pE[:, i * 128:(i + 1) * 128],
                            Sb[:, hh * 128:(hh + 1) * 128],
                            qt[:, i * 128:(i + 1) * 128], start=False, stop=True,
                            skip_group_check=True),
                            reads=[bSb[hh], qtb], writes=[bpE], tag=f'cross{c}_{hh}')
                pr.emit('v', lambda e, g=g, c=c: e.tensor_copy(
                    out=_ap3o(pb, g, c), in_=pE.rearrange("p (r n) -> p r n", r=4)),
                    reads=[bpE], writes=[bpb[g * 4], bpb[g * 4 + 1], bpb[g * 4 + 2],
                                         bpb[g * 4 + 3]], tag='oevac')
                for i, hh in enumerate(hs):
                    pr.emit('t', lambda e, hh=hh, i=i, c=c, kt=kt: e.matmul(
                        pF[:, i * 128:(i + 1) * 128],
                        kt[:, i * 128:(i + 1) * 128],
                        vb[:, c * D + hh * 128: c * D + hh * 128 + 128],
                        start=True, stop=True),
                        reads=[ktb, bvb[c * 4 + hh // 4]], writes=[bpF], tag=f'kv{c}_{hh}')
                for i, hh in enumerate(hs):
                    if cg == 0:
                        pr.emit('v', lambda e, hh=hh, i=i: e.tensor_copy(
                            out=Sf[:, hh * 128:(hh + 1) * 128],
                            in_=pF[:, i * 128:(i + 1) * 128]),
                            reads=[bpF], writes=[bSf[hh]], tag='Sinit')
                    else:
                        pr.emit('v', lambda e, hh=hh, i=i: e.scalar_tensor_tensor(
                            out=Sf[:, hh * 128:(hh + 1) * 128],
                            in0=Sf[:, hh * 128:(hh + 1) * 128],
                            scalar=lamT_c[:, hh:hh + 1],
                            in1=pF[:, i * 128:(i + 1) * 128],
                            op0=ALU.mult, op1=ALU.add),
                            reads=[bSf[hh], bpF, bconsts], writes=[bSf[hh]], tag='Supd')
                pr.emit('s', lambda e, g=g: e.copy(
                    out=Sb[:, g * 512:(g + 1) * 512], in_=Sf[:, g * 512:(g + 1) * 512]),
                    reads=[bSf[hs[0]], bSf[hs[1]], bSf[hs[2]], bSf[hs[3]]],
                    writes=[bSb[hs[0]], bSb[hs[1]], bSb[hs[2]], bSb[hs[3]]], tag='Sbcopy')

            for hh in hs:
                sq = bfp[:, (hh % 4) * TILE:(hh % 4) * TILE + TILE]
                sqb = bbfp[hh % 4]
                pr.emit('s', lambda e, sq=sq, hh=hh: e.activation(
                    out=sq, in_=pb[:, hh * TILE:(hh + 1) * TILE], func=AF.Square),
                    reads=[bpb[hh]], writes=[sqb], tag='psq')

        if DEBUG_O:
            for fc in range(FC):
                pr.dma('g', 'qs', odbg[fc * 128:(fc + 1) * 128, t0:t0 + TILE],
                       pb[:, fc * TILE:(fc + 1) * TILE],
                       reads=[bpb[fc]], writes=[bout], tag=f'odbg{t}_{fc}')
            if t < 2:
                pr.dma('g', 'qs', qdbg[t, 0], qb, reads=list(bqb), writes=[bout], tag=f'qdbg{t}')
                pr.dma('g', 'qs', qdbg[t, 1], kb, reads=list(bkb), writes=[bout], tag=f'kdbg{t}')
                pr.dma('g', 'qs', qdbg[t, 2], vb, reads=list(bvb), writes=[bout], tag=f'vdbg{t}')
                pr.dma('g', 'qs', qdbg[t, 3], h, reads=list(bh), writes=[bout], tag=f'hdbg{t}')

        if t == 0:
            for j in range(4, 8):
                qthunks['p'][j]()
        for hh in range(12, 16):
            sq = bfp[:, (hh % 4) * TILE:(hh % 4) * TILE + TILE]
            sqb = bbfp[hh % 4]
            pr.emit('t', lambda e, sq=sq, hh=hh: e.matmul(
                pS1[:1, :], ocb, sq, start=(hh == 0), stop=(hh == FC - 1)),
                reads=[sqb, bconsts], writes=[bpS1], tag='osumA')
            pr.emit('t', lambda e, sq=sq, hh=hh: e.matmul(
                pS2[:1, :], nw2b[:, hh:hh + 1], sq,
                start=(hh == 0), stop=(hh == FC - 1)),
                reads=[sqb, bcols, bconsts], writes=[bpS2], tag='osumB')

        eA = 1e-6 * 128.0
        al = rows[:1, 0:TILE]
        tmp = rows[:1, TILE:2 * TILE]

        pr.emit('v', lambda e: e.tensor_scalar(out=tmp, in0=pS1[:1, :],
                                               scalar1=EPS32 / D,
                                               scalar2=EPS32 * eA, op0=ALU.mult,
                                               op1=ALU.add),
                reads=[bpS1], writes=[brows[1]], tag='alp1')
        pr.emit('v', lambda e: e.scalar_tensor_tensor(out=al, in0=pS2[:1, :],
                                                      scalar=1.0 / D, in1=tmp,
                                                      op0=ALU.mult, op1=ALU.add),
                reads=[bpS2, brows[1]], writes=[brows[0]])
        pr.emit('s', lambda e: e.activation(out=al, in_=al, func=AF.Ln),
                reads=[brows[0]], writes=[brows[0]])
        pr.emit('s', lambda e: e.activation(out=al, in_=al, func=AF.Exp,
                                            scale=cst[0:1, 2:3]),
                reads=[brows[0]], writes=[brows[0]])

        def emit_alpha_bcast():
            pr.emit('t', lambda e: e.matmul(pC, orf, al, start=True, stop=True),
                    reads=[brows[0], bconsts], writes=[bpC], tag='alb')
            pr.emit('v', lambda e: e.tensor_copy(out=brow[:, 2 * TILE:3 * TILE], in_=pC),
                    reads=[bpC], writes=[bbrow[2]], tag='albf')
        for fc in range(FC):
            pr.emit('v', lambda e, fc=fc: e.tensor_scalar_mul(
                out=h[:, fc * TILE:(fc + 1) * TILE],
                in0=pb[:, fc * TILE:(fc + 1) * TILE],
                scalar1=cols["g"][:, fc:fc + 1]),
                reads=[bpb[fc], bcols], writes=[bh[fc]], tag='zbuild')

        def emit_ln2_stat(j):
            sl_ = bfp[:, (j % 2) * TILE:(j % 2) * TILE + TILE]
            sb_ = bbfp[j % 2]
            sq_ = bfp[:, (2 + j % 2) * TILE:(2 + j % 2) * TILE + TILE]
            sqb_ = bbfp[2 + j % 2]
            pr.emit('t', lambda e, sl_=sl_, j=j: e.matmul(
                pS1[:1, :], ocb, sl_, start=(j == 0), stop=(j == FC - 1)),
                reads=[sb_, bconsts], writes=[bpS1], tag='ln2sum')
            pr.emit('t', lambda e, sq_=sq_, j=j: e.matmul(
                pS2[:1, :], ocb, sq_, start=(j == 0), stop=(j == FC - 1)),
                reads=[sqb_, bconsts], writes=[bpS2], tag='ln2sum2')

        if t + 1 < NT and t > 0:
            emit_xload(t + 1)
        for mg in range(MP):
            slot = mg % 2
            wsl = wpool[:, slot * 2048:(slot + 1) * 2048]
            psum, bps = (pA, bpA) if slot == 0 else (pB, bpB)
            pr.dma('y', 'wpA' if slot == 0 else 'wpB', wsl, wp_t[mg],
                   reads=[bwp_t[mg]], writes=[bwp[slot]], tag=f'wp{t}_{mg}')
            for kc in range(FC):
                pr.emit('t', lambda e, wsl=wsl, kc=kc, psum=psum: e.matmul(
                    psum, wsl[:, kc * 128:(kc + 1) * 128],
                    h[:, kc * TILE:(kc + 1) * TILE],
                    start=(kc == 0), stop=(kc == FC - 1)),
                    reads=[bwp[slot], bh[kc]], writes=[bps], tag=f'pmm{mg}')
            if mg == 0:
                emit_alpha_bcast()

            def proj_evac(j, psum_, bps_):
                pr.emit('v', lambda e, j=j, psum_=psum_: e.tensor_tensor(
                    out=t1s[:, (j % 2) * TILE:(j % 2) * TILE + TILE], in0=psum_,
                    in1=brow[:, 2 * TILE:3 * TILE], op=ALU.mult),
                    reads=[bps_, bbrow[2]], writes=[bt1[j % 2]], tag='res1a')
                pr.emit('v', lambda e, j=j, xs=xs: e.tensor_tensor(
                    out=xs[:, j * TILE:(j + 1) * TILE],
                    in0=t1s[:, (j % 2) * TILE:(j % 2) * TILE + TILE],
                    in1=xs[:, j * TILE:(j + 1) * TILE], op=ALU.add),
                    reads=[bt1[j % 2], bxs[j]], writes=[bxs[j]], tag='res1')
            proj_evac(mg, psum, bps)
            sl = bfp[:, (mg % 2) * TILE:(mg % 2) * TILE + TILE]
            sbuf = bbfp[mg % 2]
            sq = bfp[:, (2 + mg % 2) * TILE:(2 + mg % 2) * TILE + TILE]
            sqb = bbfp[2 + mg % 2]
            pr.emit('g', lambda e, sl=sl, xs=xs, mg=mg: e.tensor_copy(
                out=sl, in_=xs[:, mg * TILE:(mg + 1) * TILE]),
                reads=[bxs[mg]], writes=[sbuf], tag='x1b')
            pr.emit('s', lambda e, sq=sq, xs=xs, mg=mg: e.activation(
                out=sq, in_=xs[:, mg * TILE:(mg + 1) * TILE], func=AF.Square),
                reads=[bxs[mg]], writes=[sqb], tag='x1sq')
            if mg > 0:
                emit_ln2_stat(mg - 1)
        emit_ln2_stat(FC - 1)

        mu2 = rows[:1, 0:TILE]
        m22 = rows[:1, TILE:2 * TILE]
        a2 = rows[:1, 2 * TILE:3 * TILE]
        k2 = (1.0 + 1e-5) / D
        c2 = 1e-5 * 1e-5
        pr.emit('v', lambda e: e.tensor_scalar_mul(out=mu2, in0=pS1[:1, :], scalar1=1.0 / D),
                reads=[bpS1], writes=[brows[0]], tag='mu2')
        pr.emit('g', lambda e: e.tensor_tensor(out=m22, in0=mu2, in1=mu2, op=ALU.mult),
                reads=[brows[0]], writes=[brows[1]])
        pr.emit('g', lambda e: e.tensor_scalar(out=m22, in0=m22, scalar1=1.0 + 1e-5,
                                               scalar2=-c2, op0=ALU.mult, op1=ALU.add),
                reads=[brows[1]], writes=[brows[1]])
        pr.emit('v', lambda e: e.scalar_tensor_tensor(out=a2, in0=pS2[:1, :], scalar=k2,
                                                      in1=m22, op0=ALU.mult,
                                                      op1=ALU.subtract),
                reads=[bpS2, brows[1]], writes=[brows[2]])
        pr.emit('s', lambda e: e.activation(out=a2, in_=a2, func=AF.Ln),
                reads=[brows[2]], writes=[brows[2]])
        pr.emit('s', lambda e: e.activation(out=a2, in_=a2, func=AF.Exp,
                                            scale=cst[0:1, 2:3]),
                reads=[brows[2]], writes=[brows[2]])
        pr.emit('t', lambda e: e.matmul(pC, orf, mu2, start=True, stop=True),
                reads=[brows[0], bconsts], writes=[bpC], tag='mu2b')
        pr.emit('t', lambda e: e.matmul(pE, orf, a2, start=True, stop=True),
                reads=[brows[2], bconsts], writes=[bpE], tag='a2b')
        pr.emit('v', lambda e: e.tensor_copy(out=brow[:, TILE:2 * TILE], in_=pE),
                reads=[bpE], writes=[bbrow[1]], tag='a2bf')
        for fc in range(FC):
            tsl = t1s[:, (fc % 2) * TILE:(fc % 2) * TILE + TILE]
            tb = bt1[fc % 2]
            pr.emit('v', lambda e, tsl=tsl, xs=xs, fc=fc: e.tensor_tensor(
                out=tsl, in0=xs[:, fc * TILE:(fc + 1) * TILE], in1=pC, op=ALU.subtract),
                reads=[bxs[fc], bpC], writes=[tb], tag='t3')
            pr.emit('v', lambda e, tsl=tsl, fc=fc: e.tensor_scalar_mul(
                out=qb[:, fc * TILE:(fc + 1) * TILE], in0=tsl,
                scalar1=cols["lw"][:, fc:fc + 1]),
                reads=[tb, bcols], writes=[bqb[fc]], tag='xnapp')

        if t == 0:
            for j in range(4, 8):
                qthunks['f'][j]()
            emit_xload(1)
        if t + 1 < NT:
            emit_ln1_stats(t + 1)

        for mg in range(MF):
            slot = mg % 2
            wsl = wpool[:, slot * 2048:(slot + 1) * 2048]
            psum, bps = (pA, bpA) if slot == 0 else (pB, bpB)
            pr.dma('y', 'wpA' if slot == 0 else 'wpB', wsl, wf_t[mg],
                   reads=[bwf_t[mg]], writes=[bwp[slot]], tag=f'wf{t}_{mg}')
            for kc in range(FC):
                pr.emit('t', lambda e, wsl=wsl, kc=kc, psum=psum: e.matmul(
                    psum, wsl[:, kc * 128:(kc + 1) * 128],
                    qb[:, kc * TILE:(kc + 1) * TILE],
                    start=(kc == 0), stop=(kc == FC - 1)),
                    reads=[bwp[slot], bqb[kc]], writes=[bps], tag=f'fmm{mg}')
            if t == 0:
                pr.emit('t', lambda e, wsl=wsl: e.matmul(
                    pD.bitcast(F32)[:, 0:1] if False else pF[:, 0:1],
                    wsl[:, 0:128], lbbf[:, 0:1], start=True, stop=False),
                    reads=[bwp[slot], bcols], writes=[bpF], tag='lnbp')
                for kc in range(1, FC):
                    pr.emit('t', lambda e, wsl=wsl, kc=kc: e.matmul(
                        pF[:, 0:1], wsl[:, kc * 128:(kc + 1) * 128],
                        lbbf[:, kc:kc + 1], start=False, stop=(kc == FC - 1)),
                        reads=[bwp[slot], bcols], writes=[bpF], tag='lnbp')
                pr.emit('v', lambda e, mg=mg: e.tensor_scalar_mul(
                    out=cols["b2"][:, mg:mg + 1], in0=pF[:, 0:1],
                    scalar1=cols["gs"][:, mg:mg + 1]),
                    reads=[bpF, bcols], writes=[bcols], tag='b2col')
                pr.emit('v', lambda e, mg=mg: e.tensor_scalar_add(
                    out=cols["b2"][:, mg:mg + 1], in0=cols["b2"][:, mg:mg + 1],
                    scalar1=cols["be"][:, mg:mg + 1]),
                    reads=[bcols], writes=[bcols], tag='b2col2')
            pr.emit('v', lambda e, mg=mg, psum=psum: e.scalar_tensor_tensor(
                out=t1s[:, (mg % 2) * TILE:(mg % 2) * TILE + TILE], in0=psum,
                scalar=cols["gs"][:, mg:mg + 1],
                in1=brow[:, TILE:2 * TILE], op0=ALU.mult, op1=ALU.mult),
                reads=[bps, bbrow[1], bcols], writes=[bt1[mg % 2]], tag='fevA')
            pr.emit('v', lambda e, mg=mg, xs=xs: e.scalar_tensor_tensor(
                out=xs[:, mg * TILE:(mg + 1) * TILE],
                in0=t1s[:, (mg % 2) * TILE:(mg % 2) * TILE + TILE],
                scalar=cols["b2"][:, mg:mg + 1],
                in1=xs[:, mg * TILE:(mg + 1) * TILE], op0=ALU.add, op1=ALU.add),
                reads=[bt1[mg % 2], bxs[mg], bcols], writes=[bxs[mg]], tag='fevB')
            pr.dma('g', 'st', outT[mg * 128:(mg + 1) * 128, t0:t0 + TILE],
                   xs[:, mg * TILE:(mg + 1) * TILE],
                   reads=[bxs[mg]], writes=[bout], tag=f'store{t}_{mg}')
            if mg == 3 and t + 1 < NT:
                emit_ln1_apply(t + 1)

    n_s = pr.dma_cnt['st']
    pr.emit('g', lambda e: e.nop(), also=[('d', 'st', n_s)], tag='final_wait')

    with nc.Block() as block:
        pr.finalize(block, sems, dsems)
    es.close()
    return nc


def _one(bufs):
    return bufs[0]


def _ap3(qb, g, c):
    return qb.rearrange("p (hh t) -> p hh t", hh=H)[:, g * 4:(g + 1) * 4,
                                                   c * 128:c * 128 + 128]


def _ap3c(ltrep, g):
    return ltrep[:, g * 512:(g + 1) * 512].rearrange("p (r n) -> p r n", r=4)


def _ap3o(pb, g, c):
    return pb.rearrange("p (hh t) -> p hh t", hh=FC)[:, g * 4:(g + 1) * 4,
                                                     c * 128:c * 128 + 128]


_NC_CACHE = None


def kernel(L0_attn_x, L0_attn_qkv_w, L0_attn_qkv_norm_w, L0_attn_proj_w,
           L0_attn_proj_norm_w, L0_attn_norm_w, L0_attn_s, L0_ffn_weight,
           L0_ffn_gamma, L0_ffn_beta, L0_ffn_ln_weight, L0_ffn_ln_bias):
    global _NC_CACHE
    x = np.asarray(L0_attn_x, np.float32)

    def rearr(w, nmg):
        return np.ascontiguousarray(
            np.asarray(w, np.float32).reshape(nmg, 128, FC, 128).transpose(0, 3, 2, 1)
            .reshape(nmg, 128, 2048))

    wq = rearr(L0_attn_qkv_w, 48)
    wp = rearr(L0_attn_proj_w, 16)
    wf = rearr(L0_ffn_weight, 16)

    def recol(v):
        return np.ascontiguousarray(np.asarray(v, np.float32).reshape(FC, 128).T)

    base = {
        "wq_r": wq, "wp_r": wp, "wf_r": wf,
        "qn_r": recol(L0_attn_qkv_norm_w), "pn_r": recol(L0_attn_proj_norm_w),
        "nw_r": recol(L0_attn_norm_w), "ga_r": recol(L0_ffn_gamma),
        "be_r": recol(L0_ffn_beta), "lw_r": recol(L0_ffn_ln_weight),
        "lb_r": recol(L0_ffn_ln_bias),
        "s_in": np.asarray(L0_attn_s, np.float32).reshape(1, H),
        "iota_c": np.arange(128, dtype=np.float32).reshape(128, 1),
        "iota_r": np.arange(128, dtype=np.float32).reshape(1, 128),
        "diffp": np.maximum(np.arange(128)[None, :] - np.arange(128)[:, None], 0
                            ).astype(np.float32),
        "causal": (np.arange(128)[None, :] >= np.arange(128)[:, None]
                   ).astype(np.float32),
        "ones128": np.ones((128, 128), np.float32),
        "iden_in": np.eye(128).astype(ml_dtypes.bfloat16),
        "onescol_bf": np.ones((128, 1), ml_dtypes.bfloat16),
        "onescol_f": np.ones((128, 1), np.float32),
        "onesrow_f": np.ones((1, 128), np.float32),
        "cvals_in": np.tile(np.array([[RND_C, -RND_C, -0.5, 0.0]], np.float32),
                            (128, 1)),
    }
    in_maps = []
    for b in range(B):
        m = dict(base)
        m["xT"] = np.ascontiguousarray(x[b].T)
        in_maps.append(m)

    if _NC_CACHE is None:
        _NC_CACHE = build_nc()
    res = run_bass_kernel_spmd(_NC_CACHE, in_maps, core_ids=list(range(B)))
    out = np.stack([np.asarray(r["outT"]).T for r in res.results], axis=0)
    return out.astype(np.float32)


# version 18
# speedup vs baseline: 1.0214x; 1.0098x over previous
import numpy as np
import ml_dtypes
from contextlib import ExitStack

import concourse.bass as bass
import concourse.mybir as mybir
from concourse.bass_utils import run_bass_kernel_spmd

F32 = mybir.dt.float32
BF16 = mybir.dt.bfloat16
AF = mybir.ActivationFunctionType
ALU = mybir.AluOpType
AX = mybir.AxisListType

B = 8
SEQ = 4096
D = 2048
FC = 16
H = 16
HD = 128
CH = 128
TILE = 512
NT = SEQ // TILE
NCH = TILE // CH
DQ = 3 * D
MQK = 32
MP = 16
MF = 16
NEG = 4
RND_C = float(3 * 2**22)
EPS32 = float(np.finfo(np.float32).eps)

ENG_NAMES = {'t': 'tensor', 'v': 'vector', 's': 'scalar', 'g': 'gpsimd', 'y': 'sync'}
PARANOID_PE = False
STRICT_MAIN = False
DEBUG_O = False


class Buf:
    __slots__ = ('last_w', 'readers', 'name')

    def __init__(self, name=''):
        self.last_w = None
        self.readers = []
        self.name = name


DMA_SEMS = (('xA', 'xB', 'wpA', 'wpB', 'st', 'qs', 'cl', 'b1', 'b2', 'vs0',
             'qa0', 'qa1', 'qb0', 'qb1')
            + tuple(f'qi{i}' for i in range(40)))


class Prog:
    def __init__(self, nc):
        self.nc = nc
        self.recs = {k: [] for k in ENG_NAMES}
        self.dma_cnt = {k: 0 for k in DMA_SEMS}
        self.pe_strict = True

    def emit(self, ek, fn, reads=(), writes=(), also=(), tag='', tr=False):
        deps = []
        for b in reads:
            if b.last_w is not None:
                deps.append(b.last_w)
        for b in writes:
            if b.last_w is not None:
                deps.append(b.last_w)
            deps.extend(b.readers)
        deps.extend(also)
        ndeps = []
        seen = set()
        for d in deps:
            if d is None:
                continue
            kind, e2, i2 = d
            if (kind == 'c' and e2 == ek == 't' and not PARANOID_PE
                    and not self.pe_strict
                    and not tr and not self.recs['t'][i2].get('tr')):
                continue
            if kind == 'd':
                d = (kind, e2, self.dma_cnt[e2])
            if d in seen:
                continue
            seen.add(d)
            if kind == 'c':
                self.recs[e2][i2]['inc'] = True
            ndeps.append(d)
        idx = len(self.recs[ek])
        self.recs[ek].append(dict(fn=fn, waits=ndeps, inc=False, dma=None, tag=tag,
                                  tr=tr))
        tok = ('c', ek, idx)
        for b in reads:
            b.readers.append(tok)
        for b in writes:
            b.last_w = tok
            b.readers = []
        return tok

    def dma(self, qk, sem, out_ap, in_ap, reads=(), writes=(), also=(), tag='', **kw):
        deps = []
        for b in reads:
            if b.last_w is not None:
                deps.append(b.last_w)
        for b in writes:
            if b.last_w is not None:
                deps.append(b.last_w)
            deps.extend(b.readers)
        deps.extend(also)
        ndeps = []
        seen = set()
        for d in deps:
            if d is None:
                continue
            kind, e2, i2 = d
            if kind == 'd':
                d = (kind, e2, self.dma_cnt[e2])
            if d in seen:
                continue
            seen.add(d)
            if kind == 'c':
                self.recs[e2][i2]['inc'] = True
            ndeps.append(d)
        self.dma_cnt[sem] += 1
        n = self.dma_cnt[sem]
        fn = lambda eng, o=out_ap, i=in_ap, kw=kw: eng.dma_start(out=o, in_=i, **kw)
        self.recs[qk].append(dict(fn=fn, waits=ndeps, inc=False, dma=sem, tag=tag))
        tok = ('d', sem, n)
        for b in reads:
            b.readers.append(tok)
        for b in writes:
            b.last_w = tok
            b.readers = []
        return tok

    def finalize(self, block, sems, dsems):
        pref = {}
        for ek, rs in self.recs.items():
            c = 0
            p = []
            for r in rs:
                if r['inc']:
                    c += 1
                p.append(c)
            pref[ek] = p

        def mk_body(ek, rs):
            def body(eng):
                watermark = {}
                for r in rs:
                    for (kind, e2, i2) in r['waits']:
                        if kind == 'c':
                            sem = sems[e2]
                            val = pref[e2][i2]
                            if val == 0:
                                continue
                        else:
                            sem = dsems[e2]
                            val = 16 * i2
                        key = (kind, e2)
                        if watermark.get(key, -1) >= val:
                            continue
                        eng.wait_ge(sem, val)
                        watermark[key] = val
                    inst = r['fn'](eng)
                    if r['dma'] is not None:
                        inst.then_inc(dsems[r['dma']], 16)
                    elif r['inc']:
                        inst.then_inc(sems[ek], 1)
            return body

        for ek in ENG_NAMES:
            getattr(block, ENG_NAMES[ek])(mk_body(ek, self.recs[ek]))


def build_nc():
    nc = bass.Bass("TRN2", target_bir_lowering=False)

    xT = nc.dram_tensor("xT", [D, SEQ], F32, kind="ExternalInput").ap()
    wq_r = nc.dram_tensor("wq_r", [48, 128, 2048], F32, kind="ExternalInput").ap()
    wp_r = nc.dram_tensor("wp_r", [16, 128, 2048], F32, kind="ExternalInput").ap()
    wf_r = nc.dram_tensor("wf_r", [16, 128, 2048], F32, kind="ExternalInput").ap()
    vin = {}
    for nm in ("qn", "pn", "nw", "ga", "be", "lw", "lb"):
        vin[nm] = nc.dram_tensor(nm + "_r", [128, FC], F32, kind="ExternalInput").ap()
    s_in = nc.dram_tensor("s_in", [1, H], F32, kind="ExternalInput").ap()
    iota_c = nc.dram_tensor("iota_c", [128, 1], F32, kind="ExternalInput").ap()
    iota_r = nc.dram_tensor("iota_r", [1, 128], F32, kind="ExternalInput").ap()
    diffp = nc.dram_tensor("diffp", [128, 128], F32, kind="ExternalInput").ap()
    causal = nc.dram_tensor("causal", [128, 128], F32, kind="ExternalInput").ap()
    ones128 = nc.dram_tensor("ones128", [128, 128], F32, kind="ExternalInput").ap()
    iden_in = nc.dram_tensor("iden_in", [128, 128], BF16, kind="ExternalInput").ap()
    onescol_bf = nc.dram_tensor("onescol_bf", [128, 1], BF16, kind="ExternalInput").ap()
    onescol_f = nc.dram_tensor("onescol_f", [128, 1], F32, kind="ExternalInput").ap()
    onesrow_f = nc.dram_tensor("onesrow_f", [1, 128], F32, kind="ExternalInput").ap()
    cvals_in = nc.dram_tensor("cvals_in", [128, 4], F32, kind="ExternalInput").ap()
    outT = nc.dram_tensor("outT", [D, SEQ], F32, kind="ExternalOutput").ap()
    odbg = (nc.dram_tensor("odbg", [D, SEQ], BF16, kind="ExternalOutput").ap()
            if DEBUG_O else None)
    qdbg = (nc.dram_tensor("qdbg", [2, 4, 128, FC * TILE], BF16,
                           kind="ExternalOutput").ap() if DEBUG_O else None)

    wq_t = nc.dram_tensor("wq_t", [MQK, 128, FC * 128], BF16, kind="Internal").ap()
    wv_t = nc.dram_tensor("wv_t", [8, 2, 128, 2048], BF16, kind="Internal").ap()
    wp_t = nc.dram_tensor("wp_t", [MP, 128, FC * 128], BF16, kind="Internal").ap()
    wf_t = nc.dram_tensor("wf_t", [MF, 128, FC * 128], BF16, kind="Internal").ap()
    bounce = nc.dram_tensor("bounce", [128], F32, kind="Internal").ap()

    A = nc.alloc_sbuf_tensor
    xA = A("xA", [128, FC * TILE], F32).ap()
    xB = A("xB", [128, FC * TILE], F32).ap()
    h = A("h", [128, FC * TILE], BF16).ap()
    qb = A("qb", [128, H * TILE], BF16).ap()
    kb = A("kb", [128, H * TILE], BF16).ap()
    vb = A("vb", [128, NCH * D], BF16).ap()
    pb = A("pb", [128, FC * TILE], BF16).ap()
    Sf = A("Sf", [128, H * HD], F32).ap()
    Sb = A("Sb", [128, H * HD], BF16).ap()
    wpool = A("wpool", [128, 2 * FC * 128], BF16).ap()
    vsl = A("vsl", [128, 8 * 256], BF16).ap()
    maskT = A("maskT", [128, H * 128], BF16).ap()
    ltrep = A("ltrep", [128, H * 128], BF16).ap()
    lrT = A("lrT", [128, H * 128], BF16).ap()
    o128 = A("o128", [128, 128], F32).ap()
    iden = A("iden", [128, 128], BF16).ap()
    dfp = A("dfp", [128, 128], F32).ap()
    cau = A("cau", [128, 128], F32).ap()
    ioc = A("ioc", [128, 1], F32).ap()
    ior = A("ior", [1, 128], F32).ap()
    ocb = A("ocb", [128, 1], BF16).ap()
    ocf = A("ocf", [128, 1], F32).ap()
    orf = A("orf", [1, 128], F32).ap()
    cst = A("cst", [128, 4], F32).ap()
    cols = {nm: A("c_" + nm, [128, FC], F32).ap()
            for nm in ("qn", "pn", "nw", "ga", "be", "lw", "lb",
                       "qnm", "g", "nw2", "gs", "b2")}
    srow = A("srow", [1, H], F32).ap()
    nw2b = A("nw2b", [128, FC], BF16).ap()
    lbbf = A("lbbf", [128, FC], BF16).ap()
    sbc = A("sbc", [128, H], F32).ap()
    snb = A("snb", [128, H], F32).ap()
    sm128 = A("sm128", [128, H], F32).ap()
    lamT_c = A("lamT_c", [128, H], F32).ap()
    lrev_c = A("lrev_c", [128, H], F32).ap()
    mcols = A("mcols", [128, 4], F32).ap()
    sfcol = A("sfcol", [128, 1], F32).ap()
    stat = A("stat", [128, 3 * FC], F32).ap()
    statm = A("statm", [128, FC], F32).ap()
    rows = A("rows", [1, 3 * TILE], F32).ap()
    tot = A("tot", [1, 8], F32).ap()
    bfp = A("bfp", [128, 4 * TILE], BF16).ap()
    t1s = A("t1s", [128, 2 * TILE], BF16).ap()
    qts = A("qts", [128, 2 * TILE], BF16).ap()
    kts = A("kts", [128, 2 * TILE], BF16).ap()
    scm = A("scm", [128, 2 * TILE], BF16).ap()
    brow = A("brow", [128, 3 * TILE], BF16).ap()
    rbf = A("rbf", [1, TILE], BF16).ap()

    P = nc.alloc_psum_tensor
    pA = P("pA", [128, 512], F32).ap()
    pB = P("pB", [128, 512], F32).ap()
    pC = P("pC", [128, 512], F32).ap()
    pD = P("pD", [128, 1024], BF16).ap()
    pE = P("pE", [128, 512], F32).ap()
    pF = P("pF", [128, 512], F32).ap()
    pS1 = P("pS1", [1, 512], F32).ap()
    pS2 = P("pS2", [1, 512], F32).ap()

    es = ExitStack()
    sems = {k: es.enter_context(nc.semaphore("sem_" + k)) for k in ENG_NAMES}
    dsems = {k: es.enter_context(nc.semaphore("dma_" + k)) for k in DMA_SEMS}

    pr = Prog(nc)

    bxA = [Buf(f"xA{i}") for i in range(FC)]
    bxB = [Buf(f"xB{i}") for i in range(FC)]
    bh = [Buf(f"h{i}") for i in range(FC)]
    bqb = [Buf(f"qb{i}") for i in range(H)]
    bkb = [Buf(f"kb{i}") for i in range(H)]
    bvb = [Buf(f"vb{i}") for i in range(NCH * 4)]
    bpb = [Buf(f"pb{i}") for i in range(FC)]
    bSf = [Buf(f"Sf{i}") for i in range(H)]
    bSb = [Buf(f"Sb{i}") for i in range(H)]
    bwp = [Buf("wp0"), Buf("wp1")]
    bvsl = [Buf(f"vsl{i}") for i in range(8)]
    bconsts = Buf("consts")
    bcols = Buf("cols")
    brows = [Buf(f"row{i}") for i in range(4)]
    btot = Buf("tot")
    bbfp = [Buf(f"bfp{i}") for i in range(4)]
    bt1 = [Buf("t1a"), Buf("t1b")]
    bqt = [Buf("qta"), Buf("qtb")]
    bkt = [Buf("kta"), Buf("ktb")]
    bscm = [Buf("scma"), Buf("scmb")]
    bbrow = [Buf("brw0"), Buf("brw1"), Buf("brw2")]
    brbf = Buf("rbf")
    bstat = Buf("stat")
    bpA = Buf("pA"); bpB = Buf("pB"); bpC = Buf("pC"); bpD = [Buf("pD0"), Buf("pD1")]
    bpE = Buf("pE"); bpF = Buf("pF"); bpS1 = Buf("pS1"); bpS2 = Buf("pS2")
    bwq_t = [Buf(f"wqt{i}") for i in range(MQK)]
    bwv_t = [Buf(f"wvt{i}") for i in range(16)]
    bwp_t = [Buf(f"wpt{i}") for i in range(MP)]
    bwf_t = [Buf(f"wft{i}") for i in range(MF)]
    bbounce = Buf("bounce")
    bout = Buf("outT")

    tload = []
    for (dst, src) in ((dfp, diffp), (cau, causal), (o128, ones128), (iden, iden_in),
                       (ioc, iota_c), (ior, iota_r), (ocb, onescol_bf), (ocf, onescol_f),
                       (orf, onesrow_f), (cst, cvals_in), (srow[:1, :], s_in)):
        tload.append(pr.dma('g', 'cl', dst, src, writes=[bconsts], tag='cload'))
    for nm in ("qn", "pn", "nw", "ga", "be", "lw", "lb"):
        tload.append(pr.dma('g', 'cl', cols[nm], vin[nm], writes=[bcols], tag='vload'))

    tok = pr.emit('t', lambda e: e.matmul(pA[:, 0:H], orf, srow[:1, :], start=True, stop=True),
                  reads=[bconsts], writes=[bpA], tag='sbcast')
    pr.emit('v', lambda e: e.tensor_copy(out=sbc, in_=pA[:, 0:H]), reads=[bpA], writes=[bconsts])
    pr.emit('v', lambda e: e.tensor_scalar_mul(out=snb, in0=sbc, scalar1=-1.0),
            reads=[bconsts], writes=[bconsts])
    pr.emit('v', lambda e: e.tensor_scalar_mul(out=sm128, in0=sbc, scalar1=-128.0),
            reads=[bconsts], writes=[bconsts])
    pr.emit('s', lambda e: e.activation(out=lamT_c, in_=sm128, func=AF.Exp),
            reads=[bconsts], writes=[bconsts])
    for hh in range(H):
        pr.emit('s', lambda e, hh=hh: e.activation(
            out=lrev_c[:, hh:hh + 1], in_=ioc, func=AF.Exp,
            scale=sbc[:, hh:hh + 1], bias=sm128[:, hh:hh + 1]),
            reads=[bconsts], writes=[bconsts])
    for hh in range(H):
        pr.emit('v', lambda e, hh=hh: e.tensor_scalar_mul(
            out=lrT[:, hh * 128:(hh + 1) * 128], in0=o128, scalar1=lrev_c[:, hh:hh + 1]),
            reads=[bconsts], writes=[bconsts])
    for hh in range(H):
        pr.emit('s', lambda e, hh=hh: e.activation(
            out=rows[:1, (hh % 3) * 512:(hh % 3) * 512 + 128],
            in_=ior, func=AF.Exp, scale=snb[0:1, hh:hh + 1]),
            reads=[bconsts], writes=[brows[hh % 3]], tag='ltrow')
        pr.emit('t', lambda e, hh=hh: e.matmul(
            pB[:, 0:128], orf, rows[:1, (hh % 3) * 512:(hh % 3) * 512 + 128], start=True, stop=True),
            reads=[brows[hh % 3]], writes=[bpB])
        pr.emit('v', lambda e, hh=hh: e.tensor_copy(
            out=ltrep[:, hh * 128:(hh + 1) * 128], in_=pB[:, 0:128]),
            reads=[bpB], writes=[bconsts])
    for hh in range(H):
        pr.emit('s', lambda e, hh=hh: e.activation(
            out=pE[:, 0:128],
            in_=dfp, func=AF.Exp, scale=snb[:, hh:hh + 1]),
            reads=[bconsts], writes=[bpE], tag='maskexp')
        pr.emit('v', lambda e, hh=hh: e.tensor_tensor(
            out=maskT[:, hh * 128:(hh + 1) * 128], in0=pE[:, 0:128], in1=cau, op=ALU.mult),
            reads=[bpE, bconsts], writes=[bconsts])

    stat_it = [0]

    def stat_pass(wr, nmg, col_off, do_max):
        for i in range(nmg // 2):
            hl = stat_it[0] % 2
            sl = xB[:, hl * 4096:(hl + 1) * 4096]
            buf = bxB[hl * 8:(hl + 1) * 8]
            qk = 'y' if i % 2 == 0 else 's'
            dt = pr.dma(qk, 'qb0' if hl == 0 else 'qb1',
                        sl.rearrange("p (a c) -> p a c", a=2),
                        wr[i * 2:(i + 1) * 2].rearrange("a p c -> p a c"),
                        writes=buf, tag='statload')
            pr.emit('v', lambda e, sl=sl, i=i, col_off=col_off: e.tensor_reduce(
                out=stat[:, col_off + i:col_off + i + 1], in_=sl,
                axis=AX.X, op=ALU.add, apply_absolute_value=True),
                reads=buf, writes=[bstat], also=[dt], tag='statred')
            if do_max:
                pr.emit('v', lambda e, sl=sl, i=i: e.tensor_reduce(
                    out=statm[:, i:i + 1], in_=sl,
                    axis=AX.X, op=ALU.max, apply_absolute_value=True),
                    reads=buf, writes=[bstat], also=[dt], tag='statmax')
            stat_it[0] += 1

    stat_pass(wq_r, 48, 0, False)
    stat_pass(wp_r, 16, 24, False)
    stat_pass(wf_r, 16, 32, True)
    stat_rng = {0: (0, 24), 1: (24, 32), 2: (32, 40)}
    for j, name in enumerate(("q", "p", "f")):
        lo, hi = stat_rng[j]
        pr.emit('v', lambda e, j=j, lo=lo, hi=hi: e.tensor_reduce(
            out=mcols[:, j:j + 1],
            in_=stat[:, lo:hi], axis=AX.X, op=ALU.add),
            reads=[bstat], writes=[bcols], tag='stot')
        pr.emit('t', lambda e, j=j: e.matmul(
            pA[0:1, j:j + 1],
            mcols[:, j:j + 1], ocf, start=True, stop=True),
            reads=[bcols, bconsts], writes=[bpA], tag='stotmm')
    pr.emit('v', lambda e: e.tensor_copy(out=tot[:1, 0:3], in_=pA[0:1, 0:3]),
            reads=[bpA], writes=[btot])
    pr.emit('v', lambda e: e.tensor_scalar(out=tot[:1, 3:4], in0=tot[:1, 0:1],
                                           scalar1=1.0 / (DQ * D), scalar2=1e-5,
                                           op0=ALU.mult, op1=ALU.max),
            reads=[btot], writes=[btot])
    pr.emit('v', lambda e: e.tensor_scalar(out=tot[:1, 4:5], in0=tot[:1, 1:2],
                                           scalar1=1.0 / (D * D), scalar2=1e-5,
                                           op0=ALU.mult, op1=ALU.max),
            reads=[btot], writes=[btot])
    pr.emit('v', lambda e: e.reciprocal(out=tot[:1, 5:6], in_=tot[:1, 3:4]),
            reads=[btot], writes=[btot])
    pr.emit('v', lambda e: e.reciprocal(out=tot[:1, 6:7], in_=tot[:1, 4:5]),
            reads=[btot], writes=[btot])
    for j, src in enumerate((5, 3, 6, 4)):
        pr.emit('t', lambda e, j=j, src=src: e.matmul(
            pA[:, 4 + j:5 + j], orf, tot[:1, src:src + 1], start=True, stop=True),
            reads=[btot, bconsts], writes=[bpA])
    pr.emit('v', lambda e: e.tensor_copy(out=mcols[:, 0:4], in_=pA[:, 4:8]),
            reads=[bpA], writes=[bcols])
    pr.emit('v', lambda e: e.tensor_reduce(out=statm[:, 0:1], in_=statm[:, 0:8],
                                           axis=AX.X, op=ALU.max),
            reads=[bstat], writes=[bstat])
    pr.dma('g', 'b1', bounce, statm[:, 0:1], reads=[bstat], writes=[bbounce], tag='bounce1')
    pr.dma('g', 'b2', rows[:1, 0:128],
           bounce.rearrange("(a b) -> a b", a=1), reads=[bbounce], writes=[brows[0]], tag='bounce2')
    pr.emit('v', lambda e: e.tensor_reduce(out=tot[:1, 7:8], in_=rows[:1, 0:128],
                                           axis=AX.X, op=ALU.max),
            reads=[brows[0]], writes=[btot])
    pr.emit('t', lambda e: e.matmul(pA[:, 8:9], orf, tot[:1, 7:8], start=True, stop=True),
            reads=[btot, bconsts], writes=[bpA])
    pr.emit('v', lambda e: e.tensor_copy(out=sfcol, in_=pA[:, 8:9]),
            reads=[bpA], writes=[bcols])

    pr.emit('v', lambda e: e.tensor_scalar_mul(out=cols["qnm"], in0=cols["qn"],
                                               scalar1=mcols[:, 1:2]),
            reads=[bcols], writes=[bcols])
    pr.emit('v', lambda e: e.tensor_tensor(out=cols["g"], in0=cols["nw"], in1=cols["pn"],
                                           op=ALU.mult),
            reads=[bcols], writes=[bcols])
    pr.emit('v', lambda e: e.tensor_scalar_mul(out=cols["g"], in0=cols["g"],
                                               scalar1=mcols[:, 3:4]),
            reads=[bcols], writes=[bcols])
    pr.emit('s', lambda e: e.activation(out=cols["nw2"], in_=cols["nw"], func=AF.Square),
            reads=[bcols], writes=[bcols])
    pr.emit('v', lambda e: e.tensor_copy(out=nw2b, in_=cols["nw2"]),
            reads=[bcols], writes=[bcols])
    pr.emit('v', lambda e: e.tensor_copy(out=lbbf, in_=cols["lb"]),
            reads=[bcols], writes=[bcols])
    pr.emit('v', lambda e: e.tensor_scalar_mul(out=cols["gs"], in0=cols["ga"],
                                               scalar1=sfcol[:, 0:1]),
            reads=[bcols], writes=[bcols])

    def quant_tile(src_ap, src_bufs, dst_ap, dst_bufs, scale_col, extra=()):
        src_bufs = list(src_bufs)
        dst_bufs = list(dst_bufs)
        pr.emit('s', lambda e: e.activation(out=src_ap, in_=src_ap, func=AF.Identity,
                                            scale=scale_col, bias=cst[:, 0:1]),
                reads=src_bufs + [bcols, bconsts], writes=src_bufs, also=list(extra),
                tag='q1')
        pr.emit('s', lambda e: e.activation(out=src_ap, in_=src_ap, func=AF.Identity,
                                            bias=cst[:, 1:2]),
                reads=src_bufs + [bconsts], writes=src_bufs, tag='q2')
        pr.emit('v', lambda e: e.tensor_scalar_max(out=src_ap, in0=src_ap, scalar1=-1.0),
                reads=src_bufs, writes=src_bufs, tag='q3')
        pr.emit('v', lambda e: e.tensor_scalar_min(out=dst_ap, in0=src_ap, scalar1=1.0),
                reads=src_bufs, writes=dst_bufs, tag='q4')

    def quant2(i, wr, mg0, scale_col, kind):
        hl = i % 2
        sap = xB[:, hl * 4096:(hl + 1) * 4096]
        sbufs = bxB[hl * 8:(hl + 1) * 8]
        sqsem = 'qb0' if hl == 0 else 'qb1'
        dap = (kb if hl == 0 else vb)[:, 0:4096]
        dbufs = (bkb if hl == 0 else bvb)[0:8]
        wsem = f'qi{i}'
        wq_queue = 'y' if i % 2 == 0 else 's'
        dt = pr.dma(wq_queue, sqsem,
                    sap.rearrange("p (a c) -> p a c", a=2),
                    wr[mg0:mg0 + 2].rearrange("a p c -> p a c"),
                    writes=sbufs, tag=f'qin{kind}{mg0}')
        if kind == 'f':
            pr.emit('s', lambda e: e.sign(out=dap, in_=sap),
                    reads=list(sbufs), writes=list(dbufs), also=[dt], tag='fsign')
        else:
            pr.emit('s', lambda e: e.activation(out=sap, in_=sap, func=AF.Identity,
                                                scale=scale_col, bias=cst[:, 0:1]),
                    reads=list(sbufs) + [bcols, bconsts], writes=list(sbufs),
                    also=[dt], tag='q1')
            pr.emit('s', lambda e: e.activation(out=sap, in_=sap, func=AF.Identity,
                                                bias=cst[:, 1:2]),
                    reads=list(sbufs) + [bconsts], writes=list(sbufs), tag='q2')
            pr.emit('v', lambda e: e.tensor_scalar_max(out=sap, in0=sap, scalar1=-1.0),
                    reads=list(sbufs), writes=list(sbufs), tag='q3')
            pr.emit('v', lambda e: e.tensor_scalar_min(out=dap, in0=sap, scalar1=1.0),
                    reads=list(sbufs), writes=list(dbufs), tag='q4')
        for j in range(2):
            mg = mg0 + j
            dsl = dap[:, j * 2048:(j + 1) * 2048]
            if kind == 'q':
                pr.dma(wq_queue, wsem, wq_t[mg], dsl, reads=list(dbufs),
                       writes=[bwq_t[mg]], tag=f'qout{mg}')
            elif kind == 'v':
                vm = mg - MQK
                heg = vm // 2
                eo = (vm % 2) * 128
                for kc in range(FC):
                    pr.dma(wq_queue, wsem,
                           wv_t[heg, kc // 8, :, (kc % 8) * 256 + eo:(kc % 8) * 256 + eo + 128],
                           dsl[:, kc * 128:(kc + 1) * 128],
                           reads=list(dbufs), writes=[bwv_t[heg * 2 + kc // 8]],
                           tag=f'vout{mg}_{kc}')
            elif kind == 'p':
                pr.dma(wq_queue, wsem, wp_t[mg], dsl, reads=list(dbufs),
                       writes=[bwp_t[mg]], tag=f'pout{mg}')
            else:
                pr.dma(wq_queue, wsem, wf_t[mg], dsl, reads=list(dbufs),
                       writes=[bwf_t[mg]], tag=f'fout{mg}')

    quant_sched = []
    qit = [0]

    def qthunk(wr, mg0, sc, kind):
        def run():
            quant2(qit[0], wr, mg0, sc, kind)
            qit[0] += 1
        return run

    qthunks = {'q': [], 'k': [], 'v': [], 'p': [], 'f': []}
    for mg0 in range(0, 16, 2):
        qthunks['q'].append(qthunk(wq_r, mg0, mcols[:, 0:1], 'q'))
    for mg0 in range(16, MQK, 2):
        qthunks['k'].append(qthunk(wq_r, mg0, mcols[:, 0:1], 'q'))
    for mg0 in range(0, 16, 2):
        qthunks['v'].append(qthunk(wq_r, MQK + mg0, mcols[:, 0:1], 'v'))
    for mg0 in range(0, MP, 2):
        qthunks['p'].append(qthunk(wp_r, mg0, mcols[:, 2:3], 'p'))
    for mg0 in range(0, MF, 2):
        qthunks['f'].append(qthunk(wf_r, mg0, None, 'f'))

    pr.pe_strict = STRICT_MAIN
    x_of = lambda t: (xA, bxA) if t % 2 == 0 else (xB, bxB)

    def emit_xload(t):
        xs, bxs = x_of(t)
        for fc in range(FC):
            pr.dma('g', 'xA' if t % 2 == 0 else 'xB',
                   xs[:, fc * TILE:(fc + 1) * TILE],
                   xT[fc * 128:(fc + 1) * 128, t * TILE:(t + 1) * TILE],
                   writes=[bxs[fc]], tag=f'xload{t}_{fc}')

    def emit_ln1_stats(t):
        xs, bxs = x_of(t)
        def ln1_stat(j):
            sl_ = bfp[:, (j % 2) * TILE:(j % 2) * TILE + TILE]
            sb_ = bbfp[j % 2]
            sq_ = bfp[:, (2 + j % 2) * TILE:(2 + j % 2) * TILE + TILE]
            sqb_ = bbfp[2 + j % 2]
            pr.emit('t', lambda e, sl_=sl_, j=j: e.matmul(
                pS1[:1, :], ocb, sl_, start=(j == 0), stop=(j == FC - 1)),
                reads=[sb_, bconsts], writes=[bpS1], tag='ln1sum')
            pr.emit('t', lambda e, sq_=sq_, j=j: e.matmul(
                pS2[:1, :], ocb, sq_, start=(j == 0), stop=(j == FC - 1)),
                reads=[sqb_, bconsts], writes=[bpS2], tag='ln1sum2')

        for fc in range(FC):
            sl = bfp[:, (fc % 2) * TILE:(fc % 2) * TILE + TILE]
            sbuf = bbfp[fc % 2]
            sq = bfp[:, (2 + fc % 2) * TILE:(2 + fc % 2) * TILE + TILE]
            sqb = bbfp[2 + fc % 2]
            pr.emit('v', lambda e, sl=sl, xs=xs, fc=fc: e.tensor_copy(
                out=sl, in_=xs[:, fc * TILE:(fc + 1) * TILE]),
                reads=[bxs[fc]], writes=[sbuf], tag='xbcast')
            pr.emit('s', lambda e, sq=sq, xs=xs, fc=fc: e.activation(
                out=sq, in_=xs[:, fc * TILE:(fc + 1) * TILE], func=AF.Square),
                reads=[bxs[fc]], writes=[sqb], tag='xsq')
            if fc > 0:
                ln1_stat(fc - 1)
        ln1_stat(FC - 1)
        mu = rows[:1, 0:TILE]
        m2 = rows[:1, TILE:2 * TILE]
        beta = rows[:1, 2 * TILE:3 * TILE]
        k1 = (1.0 + EPS32) / D
        c0 = 1e-5 * EPS32
        pr.emit('v', lambda e: e.tensor_scalar_mul(out=mu, in0=pS1[:1, :], scalar1=1.0 / D),
                reads=[bpS1], writes=[brows[0]], tag='mu')
        pr.emit('g', lambda e: e.tensor_tensor(out=m2, in0=mu, in1=mu, op=ALU.mult),
                reads=[brows[0]], writes=[brows[1]])
        pr.emit('g', lambda e: e.tensor_scalar(out=m2, in0=m2, scalar1=1.0 + EPS32,
                                               scalar2=-c0, op0=ALU.mult, op1=ALU.add),
                reads=[brows[1]], writes=[brows[1]])
        pr.emit('v', lambda e: e.scalar_tensor_tensor(out=beta, in0=pS2[:1, :], scalar=k1,
                                                      in1=m2, op0=ALU.mult, op1=ALU.subtract),
                reads=[bpS2, brows[1]], writes=[brows[2]])
        pr.emit('s', lambda e: e.activation(out=beta, in_=beta, func=AF.Ln),
                reads=[brows[2]], writes=[brows[2]])
        pr.emit('s', lambda e: e.activation(out=beta, in_=beta, func=AF.Exp,
                                            scale=cst[0:1, 2:3]),
                reads=[brows[2]], writes=[brows[2]])

    def emit_ln1_apply(t):
        xs, bxs = x_of(t)
        mu = rows[:1, 0:TILE]
        beta = rows[:1, 2 * TILE:3 * TILE]
        pr.emit('t', lambda e: e.matmul(pC, orf, mu, start=True, stop=True),
                reads=[brows[0], bconsts], writes=[bpC], tag='mub')
        pr.emit('t', lambda e: e.matmul(pE, orf, beta, start=True, stop=True),
                reads=[brows[2], bconsts], writes=[bpE], tag='betab')
        pr.emit('v', lambda e: e.tensor_copy(out=brow[:, 0:TILE], in_=pE),
                reads=[bpE], writes=[bbrow[0]], tag='betabf')
        for fc in range(FC):
            tsl = t1s[:, (fc % 2) * TILE:(fc % 2) * TILE + TILE]
            tb = bt1[fc % 2]
            pr.emit('v', lambda e, tsl=tsl, xs=xs, fc=fc: e.tensor_tensor(
                out=tsl, in0=xs[:, fc * TILE:(fc + 1) * TILE], in1=pC, op=ALU.subtract),
                reads=[bxs[fc], bpC], writes=[tb], tag='t1')
            pr.emit('v', lambda e, tsl=tsl, fc=fc: e.scalar_tensor_tensor(
                out=h[:, fc * TILE:(fc + 1) * TILE], in0=tsl,
                scalar=cols["qnm"][:, fc:fc + 1],
                in1=brow[:, 0:TILE], op0=ALU.mult, op1=ALU.mult),
                reads=[tb, bbrow[0], bcols], writes=[bh[fc]], tag='happ')

    emit_xload(0)
    emit_ln1_stats(0)
    emit_ln1_apply(0)

    for t in range(NT):
        xs, bxs = x_of(t)
        t0 = t * TILE

        for g in range(NEG):
            if t == 0:
                qthunks['q'][2 * g](); qthunks['q'][2 * g + 1]()
                qthunks['k'][2 * g](); qthunks['k'][2 * g + 1]()
                qthunks['v'][2 * g](); qthunks['v'][2 * g + 1]()
                qthunks['p'][g](); qthunks['f'][g]()
            for j in range(8):
                mg = (4 * g + j) if j < 4 else (16 + 4 * g + j - 4)
                slot = j % 2
                wsl = wpool[:, slot * 2048:(slot + 1) * 2048]
                psum, bps = (pA, bpA) if slot == 0 else (pB, bpB)
                pr.dma('y', 'wpA' if slot == 0 else 'wpB', wsl, wq_t[mg],
                       reads=[bwq_t[mg]], writes=[bwp[slot]], tag=f'wq{t}_{mg}')
                for kc in range(FC):
                    pr.emit('t', lambda e, wsl=wsl, kc=kc, psum=psum: e.matmul(
                        psum, wsl[:, kc * 128:(kc + 1) * 128],
                        h[:, kc * TILE:(kc + 1) * TILE],
                        start=(kc == 0), stop=(kc == FC - 1)),
                        reads=[bwp[slot], bh[kc]], writes=[bps], tag=f'qkmm{mg}')
                if mg < 16:
                    pr.emit('s', lambda e, mg=mg, psum=psum: e.copy(
                        out=qb[:, mg * TILE:(mg + 1) * TILE], in_=psum),
                        reads=[bps], writes=[bqb[mg]], tag='qevac')
                else:
                    hh = mg - 16
                    pr.emit('s', lambda e, hh=hh, psum=psum: e.copy(
                        out=kb[:, hh * TILE:(hh + 1) * TILE], in_=psum),
                        reads=[bps], writes=[bkb[hh]], tag='kevac')
            if g > 0:
                for hh in range((g - 1) * 4, g * 4):
                    sq = bfp[:, (hh % 4) * TILE:(hh % 4) * TILE + TILE]
                    sqb = bbfp[hh % 4]
                    pr.emit('t', lambda e, sq=sq, hh=hh: e.matmul(
                        pS1[:1, :], ocb, sq, start=(hh == 0), stop=(hh == FC - 1)),
                        reads=[sqb, bconsts], writes=[bpS1], tag='osumA')
                    pr.emit('t', lambda e, sq=sq, hh=hh: e.matmul(
                        pS2[:1, :], nw2b[:, hh:hh + 1], sq,
                        start=(hh == 0), stop=(hh == FC - 1)),
                        reads=[sqb, bcols, bconsts], writes=[bpS2], tag='osumB')

            for heg in (2 * g, 2 * g + 1):
                wslot = heg % 2
                wsl = wpool[:, wslot * 2048:(wslot + 1) * 2048]
                pr.dma('y', 'vs0', vsl, wv_t[heg, 0],
                       reads=[bwv_t[heg * 2]], writes=[bvsl[0]], tag=f'wvlo{t}_{heg}')
                pr.dma('y', 'wpA' if wslot == 0 else 'wpB', wsl, wv_t[heg, 1],
                       reads=[bwv_t[heg * 2 + 1]], writes=[bwp[wslot]],
                       tag=f'wvhi{t}_{heg}')
                for tc in range(NCH):
                    psum, bps = ((pA, bpA), (pB, bpB))[tc % 2]
                    po = (tc // 2) * 256
                    for kc in range(FC):
                        rsl = (vsl[:, (kc % 8) * 256:(kc % 8) * 256 + 256] if kc < 8
                               else wsl[:, (kc - 8) * 256:(kc - 8) * 256 + 256])
                        rb = bvsl[0] if kc < 8 else bwp[wslot]
                        pr.emit('t', lambda e, kc=kc, tc=tc, psum=psum, po=po,
                                rsl=rsl: e.matmul(
                            psum[:, po:po + 256],
                            h[:, kc * TILE + tc * 128: kc * TILE + tc * 128 + 128],
                            rsl, start=(kc == 0), stop=(kc == FC - 1)),
                            reads=[rb, bh[kc]], writes=[bps], tag=f'vmm{heg}_{tc}')
                    pr.emit('s', lambda e, tc=tc, heg=heg, psum=psum, po=po: e.copy(
                        out=vb[:, tc * D + heg * 256: tc * D + heg * 256 + 256],
                        in_=psum[:, po:po + 256]),
                        reads=[bps], writes=[bvb[tc * 4 + heg // 2]], tag='vevac')

            hs = range(g * 4, (g + 1) * 4)
            for c in range(NCH):
                cg = t * NCH + c
                for i, hh in enumerate(hs):
                    pr.emit('t', lambda e, hh=hh, i=i, c=c: e.matmul(
                        pC[:, i * 128:(i + 1) * 128],
                        kb[:, hh * TILE + c * 128: hh * TILE + c * 128 + 128],
                        qb[:, hh * TILE + c * 128: hh * TILE + c * 128 + 128],
                        start=True, stop=True),
                        reads=[bkb[hh], bqb[hh]], writes=[bpC], tag=f'score{c}_{hh}')
                sc = scm[:, (c % 2) * TILE:(c % 2) * TILE + TILE]
                scb = bscm[c % 2]
                pr.emit('v', lambda e, sc=sc, g=g: e.tensor_tensor(
                    out=sc, in0=pC, in1=maskT[:, g * 512:(g + 1) * 512], op=ALU.mult),
                    reads=[bpC, bconsts], writes=[scb], tag='scevac')
                qt = qts[:, (c % 2) * TILE:(c % 2) * TILE + TILE]
                qtb = bqt[c % 2]
                pr.emit('v', lambda e, qt=qt, g=g, c=c: e.tensor_tensor(
                    out=qt.rearrange("p (r n) -> p r n", r=4),
                    in0=_ap3(qb, g, c),
                    in1=_ap3c(ltrep, g), op=ALU.mult),
                    reads=[bqb[hs[0]], bqb[hs[1]], bqb[hs[2]], bqb[hs[3]], bconsts],
                    writes=[qtb], tag='qtilde')
                pdo = (c % 2) * 512
                for i, hh in enumerate(hs):
                    pr.emit('t', lambda e, hh=hh, i=i, c=c, pdo=pdo: e.transpose(
                        pD[:, pdo + i * 128: pdo + (i + 1) * 128],
                        kb[:, hh * TILE + c * 128: hh * TILE + c * 128 + 128], iden),
                        reads=[bkb[hh], bconsts], writes=[bpD[c % 2]], tag='ktr', tr=True)
                kt = kts[:, (c % 2) * TILE:(c % 2) * TILE + TILE]
                ktb = bkt[c % 2]
                pr.emit('v', lambda e, kt=kt, g=g, pdo=pdo, c=c: e.tensor_tensor(
                    out=kt, in0=pD[:, pdo:pdo + 512], in1=lrT[:, g * 512:(g + 1) * 512],
                    op=ALU.mult),
                    reads=[bpD[c % 2], bconsts], writes=[ktb], tag='ktevac')
                for i, hh in enumerate(hs):
                    pr.emit('t', lambda e, hh=hh, i=i, c=c, sc=sc: e.matmul(
                        pE[:, i * 128:(i + 1) * 128],
                        vb[:, c * D + hh * 128: c * D + hh * 128 + 128],
                        sc[:, i * 128:(i + 1) * 128], start=True, stop=(cg == 0)),
                        reads=[bvb[c * 4 + hh // 4], scb], writes=[bpE],
                        tag=f'intra{c}_{hh}')
                    if cg > 0:
                        pr.emit('t', lambda e, hh=hh, i=i, qt=qt: e.matmul(
                            pE[:, i * 128:(i + 1) * 128],
                            Sb[:, hh * 128:(hh + 1) * 128],
                            qt[:, i * 128:(i + 1) * 128], start=False, stop=True),
                            reads=[bSb[hh], qtb], writes=[bpE], tag=f'cross{c}_{hh}')
                pr.emit('v', lambda e, g=g, c=c: e.tensor_copy(
                    out=_ap3o(pb, g, c), in_=pE.rearrange("p (r n) -> p r n", r=4)),
                    reads=[bpE], writes=[bpb[g * 4], bpb[g * 4 + 1], bpb[g * 4 + 2],
                                         bpb[g * 4 + 3]], tag='oevac')
                for i, hh in enumerate(hs):
                    pr.emit('t', lambda e, hh=hh, i=i, c=c, kt=kt: e.matmul(
                        pF[:, i * 128:(i + 1) * 128],
                        kt[:, i * 128:(i + 1) * 128],
                        vb[:, c * D + hh * 128: c * D + hh * 128 + 128],
                        start=True, stop=True),
                        reads=[ktb, bvb[c * 4 + hh // 4]], writes=[bpF], tag=f'kv{c}_{hh}')
                for i, hh in enumerate(hs):
                    if cg == 0:
                        pr.emit('v', lambda e, hh=hh, i=i: e.tensor_copy(
                            out=Sf[:, hh * 128:(hh + 1) * 128],
                            in_=pF[:, i * 128:(i + 1) * 128]),
                            reads=[bpF], writes=[bSf[hh]], tag='Sinit')
                    else:
                        pr.emit('v', lambda e, hh=hh, i=i: e.scalar_tensor_tensor(
                            out=Sf[:, hh * 128:(hh + 1) * 128],
                            in0=Sf[:, hh * 128:(hh + 1) * 128],
                            scalar=lamT_c[:, hh:hh + 1],
                            in1=pF[:, i * 128:(i + 1) * 128],
                            op0=ALU.mult, op1=ALU.add),
                            reads=[bSf[hh], bpF, bconsts], writes=[bSf[hh]], tag='Supd')
                pr.emit('s', lambda e, g=g: e.copy(
                    out=Sb[:, g * 512:(g + 1) * 512], in_=Sf[:, g * 512:(g + 1) * 512]),
                    reads=[bSf[hs[0]], bSf[hs[1]], bSf[hs[2]], bSf[hs[3]]],
                    writes=[bSb[hs[0]], bSb[hs[1]], bSb[hs[2]], bSb[hs[3]]], tag='Sbcopy')

            for hh in hs:
                sq = bfp[:, (hh % 4) * TILE:(hh % 4) * TILE + TILE]
                sqb = bbfp[hh % 4]
                pr.emit('s', lambda e, sq=sq, hh=hh: e.activation(
                    out=sq, in_=pb[:, hh * TILE:(hh + 1) * TILE], func=AF.Square),
                    reads=[bpb[hh]], writes=[sqb], tag='psq')

        if DEBUG_O:
            for fc in range(FC):
                pr.dma('g', 'qs', odbg[fc * 128:(fc + 1) * 128, t0:t0 + TILE],
                       pb[:, fc * TILE:(fc + 1) * TILE],
                       reads=[bpb[fc]], writes=[bout], tag=f'odbg{t}_{fc}')
            if t < 2:
                pr.dma('g', 'qs', qdbg[t, 0], qb, reads=list(bqb), writes=[bout], tag=f'qdbg{t}')
                pr.dma('g', 'qs', qdbg[t, 1], kb, reads=list(bkb), writes=[bout], tag=f'kdbg{t}')
                pr.dma('g', 'qs', qdbg[t, 2], vb, reads=list(bvb), writes=[bout], tag=f'vdbg{t}')
                pr.dma('g', 'qs', qdbg[t, 3], h, reads=list(bh), writes=[bout], tag=f'hdbg{t}')

        if t == 0:
            for j in range(4, 8):
                qthunks['p'][j]()
        for hh in range(12, 16):
            sq = bfp[:, (hh % 4) * TILE:(hh % 4) * TILE + TILE]
            sqb = bbfp[hh % 4]
            pr.emit('t', lambda e, sq=sq, hh=hh: e.matmul(
                pS1[:1, :], ocb, sq, start=(hh == 0), stop=(hh == FC - 1)),
                reads=[sqb, bconsts], writes=[bpS1], tag='osumA')
            pr.emit('t', lambda e, sq=sq, hh=hh: e.matmul(
                pS2[:1, :], nw2b[:, hh:hh + 1], sq,
                start=(hh == 0), stop=(hh == FC - 1)),
                reads=[sqb, bcols, bconsts], writes=[bpS2], tag='osumB')

        eA = 1e-6 * 128.0
        al = rows[:1, 0:TILE]
        tmp = rows[:1, TILE:2 * TILE]

        pr.emit('v', lambda e: e.tensor_scalar(out=tmp, in0=pS1[:1, :],
                                               scalar1=EPS32 / D,
                                               scalar2=EPS32 * eA, op0=ALU.mult,
                                               op1=ALU.add),
                reads=[bpS1], writes=[brows[1]], tag='alp1')
        pr.emit('v', lambda e: e.scalar_tensor_tensor(out=al, in0=pS2[:1, :],
                                                      scalar=1.0 / D, in1=tmp,
                                                      op0=ALU.mult, op1=ALU.add),
                reads=[bpS2, brows[1]], writes=[brows[0]])
        pr.emit('s', lambda e: e.activation(out=al, in_=al, func=AF.Ln),
                reads=[brows[0]], writes=[brows[0]])
        pr.emit('s', lambda e: e.activation(out=al, in_=al, func=AF.Exp,
                                            scale=cst[0:1, 2:3]),
                reads=[brows[0]], writes=[brows[0]])

        def emit_alpha_bcast():
            pr.emit('t', lambda e: e.matmul(pC, orf, al, start=True, stop=True),
                    reads=[brows[0], bconsts], writes=[bpC], tag='alb')
            pr.emit('v', lambda e: e.tensor_copy(out=brow[:, 2 * TILE:3 * TILE], in_=pC),
                    reads=[bpC], writes=[bbrow[2]], tag='albf')
        for fc in range(FC):
            pr.emit('v', lambda e, fc=fc: e.tensor_scalar_mul(
                out=h[:, fc * TILE:(fc + 1) * TILE],
                in0=pb[:, fc * TILE:(fc + 1) * TILE],
                scalar1=cols["g"][:, fc:fc + 1]),
                reads=[bpb[fc], bcols], writes=[bh[fc]], tag='zbuild')

        def emit_ln2_stat(j):
            sl_ = bfp[:, (j % 2) * TILE:(j % 2) * TILE + TILE]
            sb_ = bbfp[j % 2]
            sq_ = bfp[:, (2 + j % 2) * TILE:(2 + j % 2) * TILE + TILE]
            sqb_ = bbfp[2 + j % 2]
            pr.emit('t', lambda e, sl_=sl_, j=j: e.matmul(
                pS1[:1, :], ocb, sl_, start=(j == 0), stop=(j == FC - 1)),
                reads=[sb_, bconsts], writes=[bpS1], tag='ln2sum')
            pr.emit('t', lambda e, sq_=sq_, j=j: e.matmul(
                pS2[:1, :], ocb, sq_, start=(j == 0), stop=(j == FC - 1)),
                reads=[sqb_, bconsts], writes=[bpS2], tag='ln2sum2')

        if t + 1 < NT and t > 0:
            emit_xload(t + 1)
        for mg in range(MP):
            slot = mg % 2
            wsl = wpool[:, slot * 2048:(slot + 1) * 2048]
            psum, bps = (pA, bpA) if slot == 0 else (pB, bpB)
            pr.dma('y', 'wpA' if slot == 0 else 'wpB', wsl, wp_t[mg],
                   reads=[bwp_t[mg]], writes=[bwp[slot]], tag=f'wp{t}_{mg}')
            for kc in range(FC):
                pr.emit('t', lambda e, wsl=wsl, kc=kc, psum=psum: e.matmul(
                    psum, wsl[:, kc * 128:(kc + 1) * 128],
                    h[:, kc * TILE:(kc + 1) * TILE],
                    start=(kc == 0), stop=(kc == FC - 1)),
                    reads=[bwp[slot], bh[kc]], writes=[bps], tag=f'pmm{mg}')
            if mg == 0:
                emit_alpha_bcast()

            def proj_evac(j, psum_, bps_):
                pr.emit('v', lambda e, j=j, psum_=psum_: e.tensor_tensor(
                    out=t1s[:, (j % 2) * TILE:(j % 2) * TILE + TILE], in0=psum_,
                    in1=brow[:, 2 * TILE:3 * TILE], op=ALU.mult),
                    reads=[bps_, bbrow[2]], writes=[bt1[j % 2]], tag='res1a')
                pr.emit('v', lambda e, j=j, xs=xs: e.tensor_tensor(
                    out=xs[:, j * TILE:(j + 1) * TILE],
                    in0=t1s[:, (j % 2) * TILE:(j % 2) * TILE + TILE],
                    in1=xs[:, j * TILE:(j + 1) * TILE], op=ALU.add),
                    reads=[bt1[j % 2], bxs[j]], writes=[bxs[j]], tag='res1')
            proj_evac(mg, psum, bps)
            sl = bfp[:, (mg % 2) * TILE:(mg % 2) * TILE + TILE]
            sbuf = bbfp[mg % 2]
            sq = bfp[:, (2 + mg % 2) * TILE:(2 + mg % 2) * TILE + TILE]
            sqb = bbfp[2 + mg % 2]
            pr.emit('v', lambda e, sl=sl, xs=xs, mg=mg: e.tensor_copy(
                out=sl, in_=xs[:, mg * TILE:(mg + 1) * TILE]),
                reads=[bxs[mg]], writes=[sbuf], tag='x1b')
            pr.emit('s', lambda e, sq=sq, xs=xs, mg=mg: e.activation(
                out=sq, in_=xs[:, mg * TILE:(mg + 1) * TILE], func=AF.Square),
                reads=[bxs[mg]], writes=[sqb], tag='x1sq')
            if mg > 0:
                emit_ln2_stat(mg - 1)
        emit_ln2_stat(FC - 1)

        mu2 = rows[:1, 0:TILE]
        m22 = rows[:1, TILE:2 * TILE]
        a2 = rows[:1, 2 * TILE:3 * TILE]
        k2 = (1.0 + 1e-5) / D
        c2 = 1e-5 * 1e-5
        pr.emit('v', lambda e: e.tensor_scalar_mul(out=mu2, in0=pS1[:1, :], scalar1=1.0 / D),
                reads=[bpS1], writes=[brows[0]], tag='mu2')
        pr.emit('g', lambda e: e.tensor_tensor(out=m22, in0=mu2, in1=mu2, op=ALU.mult),
                reads=[brows[0]], writes=[brows[1]])
        pr.emit('g', lambda e: e.tensor_scalar(out=m22, in0=m22, scalar1=1.0 + 1e-5,
                                               scalar2=-c2, op0=ALU.mult, op1=ALU.add),
                reads=[brows[1]], writes=[brows[1]])
        pr.emit('v', lambda e: e.scalar_tensor_tensor(out=a2, in0=pS2[:1, :], scalar=k2,
                                                      in1=m22, op0=ALU.mult,
                                                      op1=ALU.subtract),
                reads=[bpS2, brows[1]], writes=[brows[2]])
        pr.emit('s', lambda e: e.activation(out=a2, in_=a2, func=AF.Ln),
                reads=[brows[2]], writes=[brows[2]])
        pr.emit('s', lambda e: e.activation(out=a2, in_=a2, func=AF.Exp,
                                            scale=cst[0:1, 2:3]),
                reads=[brows[2]], writes=[brows[2]])
        pr.emit('t', lambda e: e.matmul(pC, orf, mu2, start=True, stop=True),
                reads=[brows[0], bconsts], writes=[bpC], tag='mu2b')
        pr.emit('t', lambda e: e.matmul(pE, orf, a2, start=True, stop=True),
                reads=[brows[2], bconsts], writes=[bpE], tag='a2b')
        pr.emit('v', lambda e: e.tensor_copy(out=brow[:, TILE:2 * TILE], in_=pE),
                reads=[bpE], writes=[bbrow[1]], tag='a2bf')
        for fc in range(FC):
            tsl = t1s[:, (fc % 2) * TILE:(fc % 2) * TILE + TILE]
            tb = bt1[fc % 2]
            pr.emit('v', lambda e, tsl=tsl, xs=xs, fc=fc: e.tensor_tensor(
                out=tsl, in0=xs[:, fc * TILE:(fc + 1) * TILE], in1=pC, op=ALU.subtract),
                reads=[bxs[fc], bpC], writes=[tb], tag='t3')
            pr.emit('v', lambda e, tsl=tsl, fc=fc: e.tensor_scalar_mul(
                out=qb[:, fc * TILE:(fc + 1) * TILE], in0=tsl,
                scalar1=cols["lw"][:, fc:fc + 1]),
                reads=[tb, bcols], writes=[bqb[fc]], tag='xnapp')

        if t == 0:
            for j in range(4, 8):
                qthunks['f'][j]()
            emit_xload(1)
        if t + 1 < NT:
            emit_ln1_stats(t + 1)

        for mg in range(MF):
            slot = mg % 2
            wsl = wpool[:, slot * 2048:(slot + 1) * 2048]
            psum, bps = (pA, bpA) if slot == 0 else (pB, bpB)
            pr.dma('y', 'wpA' if slot == 0 else 'wpB', wsl, wf_t[mg],
                   reads=[bwf_t[mg]], writes=[bwp[slot]], tag=f'wf{t}_{mg}')
            for kc in range(FC):
                pr.emit('t', lambda e, wsl=wsl, kc=kc, psum=psum: e.matmul(
                    psum, wsl[:, kc * 128:(kc + 1) * 128],
                    qb[:, kc * TILE:(kc + 1) * TILE],
                    start=(kc == 0), stop=(kc == FC - 1)),
                    reads=[bwp[slot], bqb[kc]], writes=[bps], tag=f'fmm{mg}')
            if t == 0:
                pr.emit('t', lambda e, wsl=wsl: e.matmul(
                    pD.bitcast(F32)[:, 0:1] if False else pF[:, 0:1],
                    wsl[:, 0:128], lbbf[:, 0:1], start=True, stop=False),
                    reads=[bwp[slot], bcols], writes=[bpF], tag='lnbp')
                for kc in range(1, FC):
                    pr.emit('t', lambda e, wsl=wsl, kc=kc: e.matmul(
                        pF[:, 0:1], wsl[:, kc * 128:(kc + 1) * 128],
                        lbbf[:, kc:kc + 1], start=False, stop=(kc == FC - 1)),
                        reads=[bwp[slot], bcols], writes=[bpF], tag='lnbp')
                pr.emit('v', lambda e, mg=mg: e.tensor_scalar_mul(
                    out=cols["b2"][:, mg:mg + 1], in0=pF[:, 0:1],
                    scalar1=cols["gs"][:, mg:mg + 1]),
                    reads=[bpF, bcols], writes=[bcols], tag='b2col')
                pr.emit('v', lambda e, mg=mg: e.tensor_scalar_add(
                    out=cols["b2"][:, mg:mg + 1], in0=cols["b2"][:, mg:mg + 1],
                    scalar1=cols["be"][:, mg:mg + 1]),
                    reads=[bcols], writes=[bcols], tag='b2col2')
            pr.emit('v', lambda e, mg=mg, psum=psum: e.scalar_tensor_tensor(
                out=t1s[:, (mg % 2) * TILE:(mg % 2) * TILE + TILE], in0=psum,
                scalar=cols["gs"][:, mg:mg + 1],
                in1=brow[:, TILE:2 * TILE], op0=ALU.mult, op1=ALU.mult),
                reads=[bps, bbrow[1], bcols], writes=[bt1[mg % 2]], tag='fevA')
            pr.emit('v', lambda e, mg=mg, xs=xs: e.scalar_tensor_tensor(
                out=xs[:, mg * TILE:(mg + 1) * TILE],
                in0=t1s[:, (mg % 2) * TILE:(mg % 2) * TILE + TILE],
                scalar=cols["b2"][:, mg:mg + 1],
                in1=xs[:, mg * TILE:(mg + 1) * TILE], op0=ALU.add, op1=ALU.add),
                reads=[bt1[mg % 2], bxs[mg], bcols], writes=[bxs[mg]], tag='fevB')
            pr.dma('g', 'st', outT[mg * 128:(mg + 1) * 128, t0:t0 + TILE],
                   xs[:, mg * TILE:(mg + 1) * TILE],
                   reads=[bxs[mg]], writes=[bout], tag=f'store{t}_{mg}')
            if mg == 3 and t + 1 < NT:
                emit_ln1_apply(t + 1)

    n_s = pr.dma_cnt['st']
    pr.emit('g', lambda e: e.nop(), also=[('d', 'st', n_s)], tag='final_wait')

    with nc.Block() as block:
        pr.finalize(block, sems, dsems)
    es.close()
    return nc


def _one(bufs):
    return bufs[0]


def _ap3(qb, g, c):
    return qb.rearrange("p (hh t) -> p hh t", hh=H)[:, g * 4:(g + 1) * 4,
                                                   c * 128:c * 128 + 128]


def _ap3c(ltrep, g):
    return ltrep[:, g * 512:(g + 1) * 512].rearrange("p (r n) -> p r n", r=4)


def _ap3o(pb, g, c):
    return pb.rearrange("p (hh t) -> p hh t", hh=FC)[:, g * 4:(g + 1) * 4,
                                                     c * 128:c * 128 + 128]


_NC_CACHE = None


def kernel(L0_attn_x, L0_attn_qkv_w, L0_attn_qkv_norm_w, L0_attn_proj_w,
           L0_attn_proj_norm_w, L0_attn_norm_w, L0_attn_s, L0_ffn_weight,
           L0_ffn_gamma, L0_ffn_beta, L0_ffn_ln_weight, L0_ffn_ln_bias):
    global _NC_CACHE
    x = np.asarray(L0_attn_x, np.float32)

    def rearr(w, nmg):
        return np.ascontiguousarray(
            np.asarray(w, np.float32).reshape(nmg, 128, FC, 128).transpose(0, 3, 2, 1)
            .reshape(nmg, 128, 2048))

    wq = rearr(L0_attn_qkv_w, 48)
    wp = rearr(L0_attn_proj_w, 16)
    wf = rearr(L0_ffn_weight, 16)

    def recol(v):
        return np.ascontiguousarray(np.asarray(v, np.float32).reshape(FC, 128).T)

    base = {
        "wq_r": wq, "wp_r": wp, "wf_r": wf,
        "qn_r": recol(L0_attn_qkv_norm_w), "pn_r": recol(L0_attn_proj_norm_w),
        "nw_r": recol(L0_attn_norm_w), "ga_r": recol(L0_ffn_gamma),
        "be_r": recol(L0_ffn_beta), "lw_r": recol(L0_ffn_ln_weight),
        "lb_r": recol(L0_ffn_ln_bias),
        "s_in": np.asarray(L0_attn_s, np.float32).reshape(1, H),
        "iota_c": np.arange(128, dtype=np.float32).reshape(128, 1),
        "iota_r": np.arange(128, dtype=np.float32).reshape(1, 128),
        "diffp": np.maximum(np.arange(128)[None, :] - np.arange(128)[:, None], 0
                            ).astype(np.float32),
        "causal": (np.arange(128)[None, :] >= np.arange(128)[:, None]
                   ).astype(np.float32),
        "ones128": np.ones((128, 128), np.float32),
        "iden_in": np.eye(128).astype(ml_dtypes.bfloat16),
        "onescol_bf": np.ones((128, 1), ml_dtypes.bfloat16),
        "onescol_f": np.ones((128, 1), np.float32),
        "onesrow_f": np.ones((1, 128), np.float32),
        "cvals_in": np.tile(np.array([[RND_C, -RND_C, -0.5, 0.0]], np.float32),
                            (128, 1)),
    }
    in_maps = []
    for b in range(B):
        m = dict(base)
        m["xT"] = np.ascontiguousarray(x[b].T)
        in_maps.append(m)

    if _NC_CACHE is None:
        _NC_CACHE = build_nc()
    res = run_bass_kernel_spmd(_NC_CACHE, in_maps, core_ids=list(range(B)))
    out = np.stack([np.asarray(r["outT"]).T for r in res.results], axis=0)
    return out.astype(np.float32)
